# version 13 (final)
# baseline (speedup 1.0000x reference)
import numpy as np
import concourse.bass as bass
import concourse.mybir as mybir
from concourse.bass_utils import run_bass_kernel_spmd
from concourse.ap import AP
from contextlib import ExitStack

F32 = mybir.dt.float32
BF16 = mybir.dt.bfloat16
I32 = mybir.dt.int32
AF = mybir.ActivationFunctionType
ALU = mybir.AluOpType
AX = mybir.AxisListType

T = 2048
DM = 1024
NEG = -1.0e18
TS3 = 256
_DBG = []
_MAX_STAGE = 99

LN1E4 = np.float64(np.log(np.float64(10000.0)))
LNHI = float(np.float32(LN1E4))
LNLO = float(np.float32(LN1E4 - np.float64(np.float32(LN1E4))))
PIHI = float(np.float32(np.pi))
PILO = float(np.float32(np.float64(np.pi) - np.float64(np.float32(np.pi))))
INVPI = float(np.float32(1.0 / np.pi))


_TRIG = {}


def _trig_tables():
    if _TRIG:
        return _TRIG
    try:
        import jax
        jax.config.update("jax_enable_x64", True)
        import jax.numpy as jnp
        with jax.default_device(jax.devices("cpu")[0]):
            inv0 = 1.0 / 10000.0 ** (jnp.arange(0, 64, 2, dtype=jnp.float32) / 64)
            t = jnp.arange(T, dtype=jnp.float32)
            ang0 = jnp.outer(t, inv0)
            c0, s0 = np.asarray(jnp.cos(ang0)), np.asarray(jnp.sin(ang0))
            inv1 = 1.0 / 10000.0 ** (jnp.arange(64, dtype=jnp.float32) / 64)
            ang1 = jnp.arange(T, dtype=jnp.float32)[:, None] * inv1
            c1, s1 = np.asarray(jnp.cos(ang1)), np.asarray(jnp.sin(ang1))
    except Exception:
        inv0 = (1.0 / 10000.0 ** (np.arange(0, 64, 2, dtype=np.float32) / np.float32(64))).astype(np.float32)
        ang0 = np.outer(np.arange(T, dtype=np.float32), inv0).astype(np.float32)
        c0, s0 = np.cos(ang0), np.sin(ang0)
        inv1 = (1.0 / 10000.0 ** (np.arange(64, dtype=np.float32) / np.float32(64))).astype(np.float32)
        ang1 = (np.arange(T, dtype=np.float32)[:, None] * inv1).astype(np.float32)
        c1, s1 = np.cos(ang1), np.sin(ang1)
    _TRIG.update(c0=c0.astype(np.float32), s0=s0.astype(np.float32),
                 c1=c1.astype(np.float32), s1=s1.astype(np.float32))
    return _TRIG


def _host_consts():
    c = {}
    p = np.arange(128)
    pm = np.zeros((128, 128), dtype=np.float32); pm[p, p ^ 1] = 1.0
    c["c_permpair"] = pm
    ph = np.zeros((128, 128), dtype=np.float32); ph[p, (p + 64) % 128] = 1.0
    c["c_permhalf"] = ph
    c["c_ones_r"] = np.ones((1, 128), dtype=np.float32)
    c["c_ones_c"] = np.ones((128, 1), dtype=np.float32)
    c["c_iota_c"] = p.astype(np.float32)[:, None]
    c["c_rev_c"] = (127 - p).astype(np.float32)[:, None]
    c["c_i_l0"] = ((p % 64) >> 1).astype(np.float32)[:, None]
    c["c_sign_l0"] = np.where(p % 2 == 0, -1.0, 1.0).astype(np.float32)[:, None]
    c["c_j_l1"] = (p % 64).astype(np.float32)[:, None]
    c["c_sign_l1"] = np.where(p < 64, -1.0, 1.0).astype(np.float32)[:, None]
    tg = _trig_tables()
    i0 = ((p % 64) >> 1)
    sg0 = np.where(p % 2 == 0, -1.0, 1.0).astype(np.float32)[:, None]
    c["c_cosl0"] = np.ascontiguousarray(tg["c0"][:, i0].T)
    c["c_ssinl0"] = np.ascontiguousarray(tg["s0"][:, i0].T * sg0)
    j1 = (p % 64)
    sg1 = np.where(p < 64, -1.0, 1.0).astype(np.float32)[:, None]
    c["c_cosl1"] = np.ascontiguousarray(tg["c1"][:, j1].T)
    c["c_ssinl1"] = np.ascontiguousarray(tg["s1"][:, j1].T * sg1)
    return c


def _prep_inputs(inp):
    g = {}
    g["x0"] = np.ascontiguousarray(np.transpose(np.asarray(inp["L0_attn_x"]), (0, 2, 1)))
    for nm, key in (("wqT", "L0_attn_w_q"), ("wkT", "L0_attn_w_k"),
                    ("wvT", "L0_attn_w_v"), ("woT", "L0_attn_w_o")):
        g[nm] = np.ascontiguousarray(np.asarray(inp[key]).T)
    g["rel"] = np.ascontiguousarray(np.asarray(inp["L0_attn_rel_emb"]))
    g["w1T"] = np.ascontiguousarray(np.asarray(inp["L0_ffn_w1"]).T)
    g["cw"] = np.ascontiguousarray(np.asarray(inp["L0_ffn_conv_w"])[:, 0, :])
    g["cb"] = np.ascontiguousarray(np.asarray(inp["L0_ffn_conv_b"])[:, None])
    g["w2T"] = np.ascontiguousarray(np.asarray(inp["L0_ffn_w2"]).T)
    g["dec"] = np.asarray(inp["L1_attn_decay"], dtype=np.float32).reshape(1, 1)
    g["Wt"] = np.ascontiguousarray(np.asarray(inp["L1_attn_W"]).T)
    g["b1r"] = np.ascontiguousarray(np.asarray(inp["L1_attn_b"])[None, :].astype(np.float32))
    g["gamr"] = np.ascontiguousarray(np.asarray(inp["L1_attn_gamma"])[None, :].astype(np.float32))
    g["betr"] = np.ascontiguousarray(np.asarray(inp["L1_attn_beta"])[None, :].astype(np.float32))
    g["winT"] = np.ascontiguousarray(np.asarray(inp["L1_ffn_w_in"]).T)
    g["woutT"] = np.ascontiguousarray(np.asarray(inp["L1_ffn_w_out"]).T)
    g["kfT"] = np.ascontiguousarray(np.transpose(np.asarray(inp["L1_ffn_k_ffwd"]), (0, 1, 3, 2)))
    g["vf"] = np.ascontiguousarray(np.asarray(inp["L1_ffn_v_ffwd"]))
    return g


class _StopEmission(Exception):
    pass


class Sem:
    def __init__(self, h, owner=None):
        self.h = h
        self.n = 0
        self.owner = owner
        self.orders = []


class Prog:
    ENGS = ("sync", "tensor", "vector", "scalar", "gpsimd")

    def __init__(self):
        self.streams = {k: [] for k in self.ENGS}

    def __init2__(self):
        pass

    def set_progress(self, mapping):
        self.progress = mapping

    def emit(self, eng, fn, wait=(), inc=None, inck=1):
        wait = [(s, s.n if s.owner is not None else v) for (s, v) in wait]
        wait = [(s, v) for (s, v) in wait if v > 0]
        if inc is not None and inc.owner is not None:
            wait = wait + inc.orders
            inc.orders = []
        wait = list({id(s): (s, v) for s, v in sorted(wait, key=lambda t: t[1])}.values())
        for s, v in wait:
            assert v <= s.n, f"wait on future sem value {v} > {s.n} on {eng}"
        inc_pre = None
        if inc is not None:
            inc_pre = inc.n
            inc.n += inck
        foreign = [s for s, v in wait if s.owner is not None and s.owner != eng]
        extra_inc = None
        if foreign:
            ps = self.progress[eng]
            if inc is ps:
                proofv = ps.n
            else:
                ps.n += 1
                proofv = ps.n
                extra_inc = ps
            for s in foreign:
                s.orders.append((ps, proofv))

        extra_pre = None if extra_inc is None else proofv - 1

        def closure(e, fn=fn, wait=tuple(wait), inc=inc, inck=inck,
                    extra_inc=extra_inc, inc_pre=inc_pre, extra_pre=extra_pre):
            for s, v in wait:
                e.wait_ge(s.h, v)
            r = fn(e)
            if inc is not None:
                if r is None:
                    e.wait_ge(inc.h, inc_pre)
                    e.sem_inc(inc.h, inck)
                else:
                    r.then_inc(inc.h, inck)
            if extra_inc is not None:
                e.wait_ge(extra_inc.h, extra_pre)
                e.sem_inc(extra_inc.h, 1)

        self.streams[eng].append(closure)
        if not hasattr(self, "log"):
            self.log = []
        import traceback
        tb = traceback.extract_stack(limit=5)
        loc = ";".join(f"{f.lineno}" for f in tb[:-1])
        self.log.append((eng, getattr(inc, "h", None) and id(inc), inck,
                         [(id(s), v) for s, v in wait],
                         extra_inc is not None and id(extra_inc), loc))
        return None if inc is None else inc.n


def build_nc(dbg=()):
    nc = bass.Bass("TRN2", target_bir_lowering=False)
    consts = _host_consts()

    din = {}
    def dt_in(name, shape, dtype=F32):
        din[name] = nc.dram_tensor(name, list(shape), dtype, kind="ExternalInput").ap()

    dt_in("x0", (DM, T))
    for nm in ("wqT", "wkT", "wvT", "woT"):
        dt_in(nm, (DM, DM))
    dt_in("rel", (65, 64))
    dt_in("w1T", (DM, 4096)); dt_in("cw", (4096, 3)); dt_in("cb", (4096, 1)); dt_in("w2T", (4096, DM))
    dt_in("dec", (1, 1)); dt_in("Wt", (DM, DM)); dt_in("b1r", (1, DM)); dt_in("gamr", (1, DM)); dt_in("betr", (1, DM))
    dt_in("winT", (DM, 4112)); dt_in("woutT", (2 * DM, DM))
    dt_in("kfT", (16, 8, 128, 256)); dt_in("vf", (16, 8, 256, 128))
    for k, v in consts.items():
        dt_in(k, v.shape)

    out_h = nc.dram_tensor("out", [T, DM], F32, kind="ExternalOutput").ap()
    dbg_h = {}
    def dbgt(name, rows, cols):
        if name in dbg and name not in dbg_h:
            dbg_h[name] = nc.dram_tensor("dbg_" + name, [rows, cols], F32,
                                         kind="ExternalOutput").ap()
        return dbg_h.get(name)

    hb_band = nc.dram_tensor("hb_band", [128 * 160 + 64], F32)
    EXPP = 128 * 160
    hb_exp = nc.dram_tensor("hb_exp", [16 * EXPP], F32)
    hb_LT = nc.dram_tensor("hb_LT", [128 * 130 + 64], F32)
    hb_w1q = nc.dram_tensor("hb_w1q", [DM * 4096], BF16)
    hb_w2q = nc.dram_tensor("hb_w2q", [4096 * DM], BF16)
    hb_xq = nc.dram_tensor("hb_xq", [DM * T], BF16)

    prog = Prog()
    EM = prog.emit
    stack = ExitStack()
    en = stack.enter_context

    A = en(nc.sbuf_tensor("arA", [128, 16448], F32))
    B = en(nc.sbuf_tensor("arB", [128, 16384], F32))
    Dq = en(nc.sbuf_tensor("arD", [128, 6160], BF16))
    EW = en(nc.sbuf_tensor("arE", [128, 6144], F32))
    TBL = en(nc.sbuf_tensor("arT", [128, 4096], F32))
    WW = en(nc.sbuf_tensor("arW", [128, 3072], F32))
    WX = en(nc.sbuf_tensor("arX", [128, 1536], F32))
    MM = en(nc.sbuf_tensor("arM", [128, 64], F32))
    MI = en(nc.sbuf_tensor("arMi", [128, 512], I32))
    MB = en(nc.sbuf_tensor("arMb", [128, 128], BF16))
    CXID = en(nc.sbuf_tensor("c_idm", [128, 128], F32))
    CXM160 = en(nc.sbuf_tensor("c_m160", [128, 160], F32))
    CXTRI = en(nc.sbuf_tensor("c_trim", [128, 128], F32))
    RELS = en(nc.sbuf_tensor("rels", [128, 164], F32))
    BIGC = ("c_cosl0", "c_ssinl0", "c_cosl1", "c_ssinl1")
    CX = {k: en(nc.sbuf_tensor("s" + k, list(v.shape), F32))
          for k, v in consts.items() if k not in BIGC}

    PSA = en(nc.psum_tensor("psA", [128, 2048], F32))
    PSB = en(nc.psum_tensor("psB", [128, 1024], F32))
    PSC = en(nc.psum_tensor("psC", [128, 512], F32))
    PSD = en(nc.psum_tensor("psD", [128, 512], F32))

    sd = Sem(en(nc.semaphore("sd")), owner="sync")
    sg = Sem(en(nc.semaphore("sg")), owner="gpsimd")
    st_ = Sem(en(nc.semaphore("st")), owner="sync")
    sv = Sem(en(nc.semaphore("sv")))
    sc = Sem(en(nc.semaphore("sc")))
    sp = Sem(en(nc.semaphore("sp")))
    sq = Sem(en(nc.semaphore("sq")))
    ssy = Sem(en(nc.semaphore("ssy")))
    sW1 = Sem(en(nc.semaphore("sW1")), owner="sync")
    sX0a = Sem(en(nc.semaphore("sX0a")), owner="sync")
    sX0b = Sem(en(nc.semaphore("sX0b")), owner="sync")
    sKVa = Sem(en(nc.semaphore("sKVa")), owner="sync")
    sKVb = Sem(en(nc.semaphore("sKVb")), owner="sync")
    sWQa = Sem(en(nc.semaphore("sWQa")), owner="sync")
    sWQb = Sem(en(nc.semaphore("sWQb")), owner="sync")
    sWTa = Sem(en(nc.semaphore("sWTa")), owner="sync")
    sWTb = Sem(en(nc.semaphore("sWTb")), owner="sync")
    sWGa = Sem(en(nc.semaphore("sWGa")), owner="gpsimd")
    sWGb = Sem(en(nc.semaphore("sWGb")), owner="gpsimd")
    swq = Sem(en(nc.semaphore("swq")), owner="sync")
    swqw = Sem(en(nc.semaphore("swqw")), owner="sync")
    prog.set_progress({"sync": ssy, "tensor": sp, "vector": sv,
                       "scalar": sc, "gpsimd": sq})

    blk = en(nc.Block())

    def ckpt(n):
        if _MAX_STAGE <= n:
            raise _StopEmission()

    def barrier():
        snap = []
        for eng, s in (("tensor", sp), ("vector", sv), ("scalar", sc), ("gpsimd", sq)):
            EM(eng, lambda e: None, inc=s)
            snap.append((s, s.n))
        dsnap = [(sd, sd.n), (sg, sg.n)]
        for eng in Prog.ENGS:
            EM(eng, lambda e: None, wait=list(snap) + dsnap)

    def V(fn, wait=()):
        return EM("vector", fn, wait=list(wait) + [(sv, sv.n)], inc=sv)

    def SCE(fn, wait=()):
        return EM("scalar", fn, wait=list(wait) + [(sc, sc.n)], inc=sc)

    def PE(fn, wait=()):
        return EM("tensor", fn, wait=wait, inc=sp)

    def SY(fn, wait=()):
        return EM("sync", fn, wait=wait, inc=sd, inck=16)

    def GD(fn, wait=()):
        return EM("gpsimd", fn, wait=list(wait) + [(sq, sq.n)], inc=sg, inck=16)

    def GP(fn, wait=()):
        return EM("gpsimd", fn, wait=list(wait) + [(sq, sq.n)], inc=sq)

    def _emit_all():
        for k in consts:
            if k in BIGC:
                continue
            SY(lambda e, d=CX[k], s=din[k]: e.dma_start(out=d[:], in_=s))
        W_CONST = (sd, sd.n)
        ONES_R = CX["c_ones_r"][:]
        ONES_C = CX["c_ones_c"][:]

        GP(lambda e: e.memset(CXID[:], 1.0))
        GP(lambda e: e.affine_select(out=CXID[:], in_=CXID[:], pattern=[[-1, 128]],
                                     compare_op=ALU.is_ge, fill=0.0, base=0, channel_multiplier=1))
        GP(lambda e: e.affine_select(out=CXID[:], in_=CXID[:], pattern=[[1, 128]],
                                     compare_op=ALU.is_ge, fill=0.0, base=0, channel_multiplier=-1))
        GP(lambda e: e.memset(CXM160[:], 0.0))
        GP(lambda e: e.affine_select(out=CXM160[:], in_=CXM160[:], pattern=[[-1, 160]],
                                     compare_op=ALU.is_ge, fill=NEG, base=32, channel_multiplier=1))
        GP(lambda e: e.memset(CXTRI[:], 1.0))
        GP(lambda e: e.affine_select(out=CXTRI[:], in_=CXTRI[:], pattern=[[-1, 128]],
                                     compare_op=ALU.is_ge, fill=0.0, base=127, channel_multiplier=-1))
        TROW = B[:, 14336:16384]
        GP(lambda e: e.iota(TROW, pattern=[[1, 2048]], base=0, channel_multiplier=0,
                            allow_small_or_imprecise_dtypes=True))
        DEV_CONST = (sq, sq.n)
        IDBF = MB[:, 0:128]
        V(lambda e: e.tensor_copy(out=IDBF, in_=CXID[:]), wait=[DEV_CONST])

        ZT = B[:, 0:160]
        V(lambda e: e.memset(ZT, 0.0))
        ZT_RDY = (sv, sv.n)
        SY(lambda e: e.dma_start(out=AP(hb_band, 0, [[160, 128], [1, 160]]), in_=ZT), wait=[ZT_RDY])
        SY(lambda e: e.dma_start(out=AP(hb_LT, 0, [[130, 128], [1, 130]]), in_=B[:, 0:130]), wait=[ZT_RDY])
        SY(lambda e: e.dma_start(out=AP(hb_exp, 0, [[160, 128], [1, 32]]), in_=B[:, 0:32]), wait=[ZT_RDY])
        HBZERO = (sd, sd.n)

        def round_cols(y, a, b, pre=()):
            n = y.shape[1]
            ii = MI[:, 0:n]
            GD(lambda e, y=y, ii=ii: e.dma_start(out=ii, in_=y),
               wait=[(sv, sv.n), (sg, sg.n)] + list(pre))
            GD(lambda e, a=a, ii=ii: e.dma_start(out=a, in_=ii), wait=[(sg, sg.n)])
            dw = (sg, sg.n)
            V(lambda e, y=y, a=a, b=b: e.tensor_sub(b, y, a), wait=[dw])
            V(lambda e, y=y, b=b: e.tensor_scalar(out=y, in0=b, scalar1=0.5, scalar2=None, op0=ALU.is_ge))
            V(lambda e, y=y, a=a: e.tensor_add(y, y, a))
            V(lambda e, a=a, b=b: e.tensor_scalar(out=a, in0=b, scalar1=-0.5, scalar2=None, op0=ALU.is_le))
            V(lambda e, y=y, a=a: e.tensor_sub(y, y, a))
            return (sv, sv.n)

        def round_tile(y, a, b):
            n = y.shape[1]
            done = None
            for o in range(0, n, 512):
                w = min(512, n - o)
                done = round_cols(y[:, o:o + w], a[:, o:o + w], b[:, o:o + w])
            return done

        def round_to(dst, src, a, b):
            n = src.shape[1]
            for o in range(0, n, 512):
                w = min(512, n - o)
                sl = slice(o, o + w)
                ii = MI[:, 0:w]
                GD(lambda e, y=src[:, sl], ii=ii: e.dma_start(out=ii, in_=y),
                   wait=[(sv, sv.n), (sg, sg.n)])
                GD(lambda e, a2=a[:, sl], ii=ii: e.dma_start(out=a2, in_=ii), wait=[(sg, sg.n)])
                dw = (sg, sg.n)
                V(lambda e, sl=sl: e.tensor_sub(b[:, sl], src[:, sl], a[:, sl]), wait=[dw])
                V(lambda e, sl=sl: e.tensor_scalar(out=dst[:, sl], in0=b[:, sl], scalar1=0.5,
                                                   scalar2=None, op0=ALU.is_ge))
                V(lambda e, sl=sl: e.tensor_add(dst[:, sl], dst[:, sl], a[:, sl]))
                V(lambda e, sl=sl: e.tensor_scalar(out=a[:, sl], in0=b[:, sl], scalar1=-0.5,
                                                   scalar2=None, op0=ALU.is_le))
                V(lambda e, sl=sl: e.tensor_sub(dst[:, sl], dst[:, sl], a[:, sl]))
            return (sv, sv.n)

        W0 = B[:, 0:2048]; W1 = B[:, 2048:4096]; W2 = B[:, 4096:6144]
        W3 = B[:, 6144:8192]; W4 = B[:, 8192:10240]
        C_L0 = TBL[:, 0:2048]; S_L0 = TBL[:, 2048:4096]
        FCOL = MM[:, 0:1]

        def emit_freq_col(idx_col, denom):
            t1 = MM[:, 1:2]; t2 = MM[:, 2:3]
            V(lambda e: e.tensor_scalar(out=t1, in0=idx_col, scalar1=1.0 / denom,
                                        scalar2=-LNHI, op0=ALU.mult, op1=ALU.mult), wait=[W_CONST])
            V(lambda e: e.tensor_scalar(out=t2, in0=idx_col, scalar1=1.0 / denom,
                                        scalar2=-LNLO, op0=ALU.mult, op1=ALU.mult))
            V(lambda e: e.tensor_add(t1, t1, t2))
            vw = (sv, sv.n)
            SCE(lambda e: e.activation(out=FCOL, in_=t1, func=AF.Exp), wait=[vw])
            return (sc, sc.n)

        def emit_sin(ang, dst, kb, tmp, w4):
            V(lambda e: e.tensor_scalar(out=kb, in0=ang, scalar1=INVPI, scalar2=None, op0=ALU.mult))
            round_tile(kb, dst, tmp)
            V(lambda e: e.scalar_tensor_tensor(out=tmp, in0=kb, scalar=-PIHI, in1=ang,
                                               op0=ALU.mult, op1=ALU.add))
            V(lambda e: e.scalar_tensor_tensor(out=tmp, in0=kb, scalar=-PILO, in1=tmp,
                                               op0=ALU.mult, op1=ALU.add))
            V(lambda e: e.tensor_scalar(out=ang, in0=kb, scalar1=0.5, scalar2=None, op0=ALU.mult))
            round_to(dst, ang, kb, w4)
            V(lambda e: e.tensor_sub(ang, ang, dst))
            V(lambda e: e.tensor_mul(ang, ang, ang))
            V(lambda e: e.tensor_scalar(out=ang, in0=ang, scalar1=-8.0, scalar2=1.0,
                                        op0=ALU.mult, op1=ALU.add))
            vw = (sv, sv.n)
            SCE(lambda e: e.activation(out=dst, in_=tmp, func=AF.Sin), wait=[vw])
            cw_ = (sc, sc.n)
            V(lambda e: e.tensor_mul(dst, dst, ang), wait=[cw_])
            return (sv, sv.n)

        def emit_tables(idx_col, denom, c_out, s_out, sign_col):
            emit_freq_col(idx_col, denom)
            fr = (sc, sc.n)
            V(lambda e: e.tensor_scalar(out=W0, in0=TROW, scalar1=FCOL, scalar2=None,
                                        op0=ALU.mult), wait=[fr, (sq, sq.n), HBZERO])
            V(lambda e: e.tensor_scalar(out=W1, in0=W0, scalar1=PIHI / 2, scalar2=PILO / 2,
                                        op0=ALU.add, op1=ALU.add))
            emit_sin(W1, c_out, W2, W3, W4)
            emit_sin(W0, s_out, W1, W2, W3)
            V(lambda e: e.tensor_scalar(out=s_out, in0=s_out, scalar1=sign_col,
                                        scalar2=None, op0=ALU.mult))
            return (sv, sv.n)

        SY(lambda e: e.dma_start(out=C_L0, in_=din["c_cosl0"]))
        SY(lambda e: e.dma_start(out=S_L0, in_=din["c_ssinl0"]))

        RELDT2 = RELS[:, 0:32]
        REL0C = RELS[:, 32:33]
        RELDIF = RELS[0:32, 33:97]
        SY(lambda e: e.dma_start(out=RELS[0:32, 33:97], in_=din["rel"][1:33, :]))
        SY(lambda e: e.dma_start(out=RELS[0:64, 32:33],
                                 in_=AP(din["rel"].tensor, 0, [[1, 64], [1, 1]])))
        SY(lambda e: e.dma_start(out=RELS[64:128, 32:33],
                                 in_=AP(din["rel"].tensor, 0, [[1, 64], [1, 1]])))
        SY(lambda e: e.dma_start(out=RELS[0:1, 97:161],
                                 in_=AP(din["rel"].tensor, 0, [[64, 1], [1, 64]])))
        RELL = (sd, sd.n)
        PE(lambda e: e.matmul(PSD[0:32, 0:64], ONES_R[0:1, 0:32], RELS[0:1, 97:161],
                              start=True, stop=True), wait=[RELL, W_CONST])
        pw = (sp, sp.n)
        V(lambda e: e.tensor_sub(RELDIF, RELS[0:32, 33:97], PSD[0:32, 0:64]), wait=[pw])
        V(lambda e: e.tensor_scalar(out=RELS[0:32, 97:161], in0=RELDIF, scalar1=0.125,
                                    scalar2=None, op0=ALU.mult))
        vw2 = (sv, sv.n)
        PE(lambda e: e.transpose(PSD[0:64, 64:96], RELS[0:32, 97:161], CXID[0:32, 0:32]),
           wait=[vw2, DEV_CONST])
        pw2 = (sp, sp.n)
        V(lambda e: e.tensor_copy(out=RELDT2[0:64, :], in_=PSD[0:64, 64:96]), wait=[pw2])
        vw3 = (sv, sv.n)
        GD(lambda e: e.dma_start(out=RELDT2[64:128, :], in_=RELDT2[0:64, :]), wait=[vw3])

        barrier()
        ckpt(0)

        W2MCC = MM[:, 16:17]; W1MCC = MM[:, 17:18]; SW1C = MM[:, 18:19]; SW2C = MM[:, 19:20]
        WQSS = (EW[:, 3072:4096], EW[:, 4096:5120])
        WQT = EW[:, 5120:6144]
        ACC1 = MM[:, 20:21]; ACC2 = MM[:, 24:25]; WQTMP = MM[:, 25:26]
        wq_specs = []
        for cc in range(8):
            for h in range(0, 4096, 1024):
                wq_specs.append(("w1T", hb_w1q, cc * 128 * 4096 + h, ACC1, SW1C))
        for cc in range(32):
            wq_specs.append(("w2T", hb_w2q, cc * 128 * 1024, ACC2, SW2C))
        wq = {"mi": 0, "mr": 0, "qi": 0, "qp": 0, "mid": False,
              "mrdy": None, "dn": {}, "wn": {}}

        def wq_mean_dma(i):
            nm, _, off, _, _ = wq_specs[i]
            ap = AP(din[nm].tensor, off, [[4096 if nm == "w1T" else 1024, 128], [1, 1024]])
            EM("sync", lambda e, s=ap, d=WQSS[i % 2]: e.dma_start(out=d, in_=s),
               wait=[(sv, sv.n)], inc=swq, inck=16)
            wq["dn"][i] = (swq, swq.n)

        def wq_mean_red(i):
            acc = wq_specs[i][3]
            if i in (0, 32):
                V(lambda e, acc=acc, i=i: e.reduce_sum(out=acc, in_=WQSS[i % 2], axis=AX.X,
                                                       apply_absolute_value=True),
                  wait=[wq["dn"][i]])
            else:
                V(lambda e, i=i: e.reduce_sum(out=WQTMP, in_=WQSS[i % 2], axis=AX.X,
                                              apply_absolute_value=True), wait=[wq["dn"][i]])
                V(lambda e, acc=acc: e.tensor_add(acc, acc, WQTMP))

        def wq_mid():
            for k, (acc, mc, sw, n) in enumerate((
                    (ACC1, W1MCC, SW1C, 8 * 128 * 4096), (ACC2, W2MCC, SW2C, 32 * 128 * 1024))):
                PE(lambda e, acc=acc: e.matmul(PSD[0:1, 0:1], ONES_C, acc,
                                               start=True, stop=True),
                   wait=[(sv, sv.n)] if k == 0 else [(sv, sv.n)])
                pw = (sp, sp.n)
                V(lambda e, n=n: e.tensor_scalar(out=MM[0:1, 26:27], in0=PSD[0:1, 0:1],
                                                 scalar1=1.0 / n, scalar2=1e-5,
                                                 op0=ALU.mult, op1=ALU.max), wait=[pw])
                V(lambda e: e.reciprocal(out=MM[0:1, 27:28], in_=MM[0:1, 26:27]))
                vw = (sv, sv.n)
                PE(lambda e: e.matmul(PSD[:, 0:1], ONES_R, MM[0:1, 26:27],
                                      start=True, stop=True), wait=[vw])
                PE(lambda e: e.matmul(PSD[:, 1:2], ONES_R, MM[0:1, 27:28],
                                      start=True, stop=True))
                pw = (sp, sp.n)
                V(lambda e, mc=mc: e.tensor_copy(out=mc, in_=PSD[:, 0:1]), wait=[pw])
                V(lambda e, sw=sw: e.tensor_copy(out=sw, in_=PSD[:, 1:2]))
            wq["mrdy"] = (sv, sv.n)
            wq["mid"] = True

        def wq_quant_dma(i):
            nm, _, off, _, _ = wq_specs[i]
            ap = AP(din[nm].tensor, off, [[4096 if nm == "w1T" else 1024, 128], [1, 1024]])
            EM("sync", lambda e, s=ap, d=WQSS[i % 2]: e.dma_start(out=d, in_=s),
               wait=[(sv, sv.n)], inc=swq, inck=16)
            wq["dn"][64 + i] = (swq, swq.n)

        def wq_quant_proc(i):
            nm, hbq, off, _, sw = wq_specs[i]
            bu = WQSS[i % 2]
            w0 = [wq["dn"][64 + i]] + ([wq["mrdy"]] if i == 0 else [])
            V(lambda e, bu=bu, sw=sw: e.tensor_scalar(out=WQT, in0=bu, scalar1=sw,
                                                      scalar2=0.5, op0=ALU.mult,
                                                      op1=ALU.is_ge), wait=w0)
            V(lambda e, bu=bu, sw=sw: e.tensor_scalar(out=bu, in0=bu, scalar1=sw,
                                                      scalar2=-0.5, op0=ALU.mult,
                                                      op1=ALU.is_le))
            V(lambda e, bu=bu: e.tensor_sub(Dq[:, 0:1024], WQT, bu),
              wait=[wq["wn"][i - 1]] if i - 1 in wq["wn"] else [])
            vw = (sv, sv.n)
            oap = AP(hbq, off, [[4096 if nm == "w1T" else 1024, 128], [1, 1024]])
            EM("sync", lambda e, o=oap: e.dma_start(out=o, in_=Dq[:, 0:1024]),
               wait=[vw], inc=swqw, inck=16)
            wq["wn"][i] = (swqw, swqw.n)

        def wq_slot():
            if wq["mi"] < 64:
                if wq["mi"] >= 2:
                    wq_mean_red(wq["mr"]); wq["mr"] += 1
                wq_mean_dma(wq["mi"]); wq["mi"] += 1
            elif wq["mid"] and wq["qi"] < 64:
                if wq["qi"] >= 2:
                    wq_quant_proc(wq["qp"]); wq["qp"] += 1
                wq_quant_dma(wq["qi"]); wq["qi"] += 1

        def wq_flush():
            while wq["mr"] < wq["mi"]:
                wq_mean_red(wq["mr"]); wq["mr"] += 1
            while wq["qp"] < wq["qi"]:
                wq_quant_proc(wq["qp"]); wq["qp"] += 1

        KF = B[:, 0:2048]
        SCR = B[:, 2048:4096]
        VP = B[:, 4096:6144]
        CTX = B[:, 6144:8192]
        QRAW = EW[:, 0:2048]
        MS1 = EW[:, 2048:3072]
        X0BS = (B[:, 8192:12288], B[:, 12288:16384])
        X0SEM = (sX0a, sX0b)
        RELOUT = MM[:, 8:16]

        def load_pair_weights(hp, pe_gate):
            for wi, nm in enumerate(("wkT", "wvT", "wqT")):
                ap = AP(din[nm].tensor, hp * 128,
                        [[DM, 128], [128 * DM, 8], [1, 128]])
                EM("sync", lambda e, d=WW[:, wi * 1024:(wi + 1) * 1024], s=ap:
                   e.dma_start(out=d, in_=s), wait=[pe_gate], inc=sW1, inck=16)

        def load_wo(hp, pe_gate):
            ap = AP(din["woT"].tensor, hp * 128 * DM, [[DM, 128], [1, 1024]])
            EM("sync", lambda e, s=ap: e.dma_start(out=WX[:, 0:1024], in_=s),
               wait=[pe_gate], inc=sW1, inck=16)

        def load_x0_block(tbb, pe_gate):
            buf = X0BS[tbb % 2]
            sem = X0SEM[tbb % 2]
            ap = AP(din["x0"].tensor, tbb * 512, [[T, 128], [128 * T, 8], [1, 512]])
            EM("sync", lambda e, s=ap, buf=buf: e.dma_start(out=buf, in_=s),
               wait=[pe_gate], inc=sem, inck=16)

        def s1_pair(hp):
            load_wo(hp, (sp, sp.n))
            WRD = (sW1, sW1.n)
            evK = (sc, sc.n)
            evVQ = (sv, sv.n)
            for tbb in range(4):
                tsl = slice(tbb * 512, (tbb + 1) * 512)
                X0B = X0BS[tbb % 2]
                xw = (X0SEM[tbb % 2], X0SEM[tbb % 2].n)
                if tbb < 3:
                    load_x0_block(tbb + 1, (sp, sp.n))
                for cc in range(8):
                    PE(lambda e, cc=cc, tbb=tbb, X0B=X0B: e.matmul(
                        PSA[:, tbb * 512:(tbb + 1) * 512],
                        WW[:, cc * 128:cc * 128 + 128],
                        X0B[:, cc * 512:(cc + 1) * 512],
                        start=(cc == 0), stop=(cc == 7)),
                       wait=([xw, WRD, evK] if cc == 0 else []))
                for qt4 in range(1):
                    for cc in range(8):
                        PE(lambda e, cc=cc, X0B=X0B: e.matmul(
                            PSC[:, 0:512],
                            WW[:, 2048 + cc * 128:2048 + cc * 128 + 128],
                            X0B[:, cc * 512:(cc + 1) * 512],
                            start=(cc == 0), stop=(cc == 7)),
                           wait=([evVQ] if cc == 0 else []))
                    pv = (sp, sp.n)
                    V(lambda e, tbb=tbb, tsl=tsl: e.tensor_copy(out=QRAW[:, tsl],
                                                                in_=PSC[:, 0:512]),
                      wait=[pv])
                    evVQ = (sv, sv.n)
                for tc4 in range(4):
                    po = (tc4 % 2) * 512
                    for cc in range(8):
                        PE(lambda e, cc=cc, tc4=tc4, po=po, X0B=X0B: e.matmul(
                            PSB[:, po:po + 128],
                            X0B[:, cc * 512 + tc4 * 128:cc * 512 + tc4 * 128 + 128],
                            WW[:, 1024 + cc * 128:1024 + cc * 128 + 128],
                            start=(cc == 0), stop=(cc == 7)),
                           wait=([evVQ] if cc == 0 else []))
                    pv = (sp, sp.n)
                    V(lambda e, tc4=tc4, tbb=tbb, po=po: e.tensor_copy(
                        out=VP[:, tbb * 512 + tc4 * 128:tbb * 512 + tc4 * 128 + 128],
                        in_=PSB[:, po:po + 128]), wait=[pv])
                    evVQ = (sv, sv.n)
                pe_done = (sp, sp.n)
                SCE(lambda e, tbb=tbb, tsl=tsl: e.copy(out=SCR[:, tsl],
                                                       in_=PSA[:, tbb * 512:(tbb + 1) * 512]),
                    wait=[pe_done])
                evK = (sc, sc.n)
            if hp < 7:
                load_pair_weights(hp + 1, (sp, sp.n))
                load_x0_block(0, (sp, sp.n))

            for dc in range(8):
                PE(lambda e, dc=dc: e.matmul(PSD[:, 200:201], WX[:, dc * 128:dc * 128 + 128],
                                             REL0C, start=True, stop=True),
                   wait=[WRD, (sv, sv.n)])
                pv = (sp, sp.n)
                if hp == 0:
                    V(lambda e, dc=dc: e.tensor_copy(out=RELOUT[:, dc:dc + 1],
                                                     in_=PSD[:, 200:201]), wait=[pv])
                else:
                    V(lambda e, dc=dc: e.tensor_add(RELOUT[:, dc:dc + 1],
                                                    RELOUT[:, dc:dc + 1], PSD[:, 200:201]),
                      wait=[pv])

            kevac = (sc, sc.n)
            for nb in range(4):
                PE(lambda e, nb=nb: e.matmul(PSA[:, nb * 512:(nb + 1) * 512],
                                             CX["c_permpair"][:],
                                             SCR[:, nb * 512:(nb + 1) * 512],
                                             start=True, stop=True),
                   wait=[kevac, (sv, sv.n)] if nb == 0 else [])
            pw = (sp, sp.n)
            V(lambda e: e.tensor_mul(KF, SCR, C_L0), wait=[pw])
            V(lambda e: e.tensor_mul(SCR, PSA[:, 0:2048], S_L0))
            V(lambda e: e.tensor_add(KF, KF, SCR))
            V(lambda e: e.tensor_scalar(out=KF, in0=KF, scalar1=REL0C, scalar2=0.125,
                                        op0=ALU.add, op1=ALU.mult))
            KF_RDY = (sv, sv.n)

            ctx_ev = (sv, sv.n)
            for qt in range(16):
                q0 = qt * 128
                kext = q0 + 128
                qsl = slice(q0, q0 + 128)
                PE(lambda e, qsl=qsl: e.matmul(PSD[:, 0:128], CX["c_permpair"][:], QRAW[:, qsl],
                                      start=True, stop=True), wait=[KF_RDY])
                pw = (sp, sp.n)
                QR = MS1[:, 0:128]
                V(lambda e, qsl=qsl: e.tensor_mul(QR, QRAW[:, qsl], C_L0[:, qsl]), wait=[pw])
                V(lambda e, qsl=qsl: e.tensor_mul(MS1[:, 128:256], PSD[:, 0:128], S_L0[:, qsl]))
                V(lambda e: e.tensor_add(QR, QR, MS1[:, 128:256]))
                QR_RDY = (sv, sv.n)
                for hh in range(2):
                    hb = hh * 64
                    hsl = slice(hb, hb + 64)
                    PE(lambda e, hsl=hsl: e.matmul(PSD[:, 128:160], QR[hsl, :], RELDT2[hsl, :],
                                          start=True, stop=True),
                       wait=[QR_RDY, (sv, sv.n)])
                    pw = (sp, sp.n)
                    PP = MS1[:, 256:288]
                    V(lambda e: e.tensor_copy(out=PP, in_=PSD[:, 128:160]), wait=[pw])
                    vw = (sv, sv.n)
                    GD(lambda e: e.dma_start(out=AP(hb_band, 1, [[161, 128], [1, 32]]), in_=PP),
                       wait=[vw, (sg, sg.n)])
                    GD(lambda e: e.dma_start(out=MS1[:, 288:448],
                                             in_=AP(hb_band, 0, [[160, 128], [1, 160]])),
                       wait=[(sg, sg.n)])
                    bw = (sg, sg.n)
                    scw = [(sc, sc.n)]
                    for o in range(0, kext, 512):
                        w = min(512, kext - o)
                        PE(lambda e, o=o, w=w, hsl=hsl: e.matmul(PSA[:, o:o + w], QR[hsl, :],
                                                        KF[hsl, o:o + w], start=True, stop=True),
                           wait=scw)
                        scw = []
                    BND = MS1[:, 288:448]
                    V(lambda e: e.tensor_add(BND, BND, CXM160[:]), wait=[bw, (sp, sp.n)])
                    if qt > 0:
                        V(lambda e, q0=q0: e.tensor_add(PSA[:, q0 - 32:q0], PSA[:, q0 - 32:q0],
                                                 BND[:, 0:32]))
                    V(lambda e, q0=q0: e.tensor_add(PSA[:, q0:q0 + 128], PSA[:, q0:q0 + 128],
                                             BND[:, 32:160]))
                    NM = MM[:, 3:4]; LC = MM[:, 4:5]
                    V(lambda e, kext=kext: e.reduce_max(out=NM, in_=PSA[:, 0:kext],
                                                        axis=AX.X, negate=True))
                    vw = (sv, sv.n)
                    SCE(lambda e, kext=kext: e.activation(out=SCR[:, 0:kext], in_=PSA[:, 0:kext],
                                                          func=AF.Exp, bias=NM, scale=1.0,
                                                          accum_out=LC), wait=[vw])
                    cw_ = (sc, sc.n)
                    V(lambda e: e.reciprocal(out=MM[:, 5:6], in_=LC), wait=[cw_])
                    V(lambda e, kext=kext: e.tensor_scalar(out=SCR[:, 0:kext], in0=SCR[:, 0:kext],
                                                           scalar1=MM[:, 5:6], scalar2=None,
                                                           op0=ALU.mult))
                    EXP_RDY = (sv, sv.n)
                    lo = max(q0 - 32, 0)
                    woff = 32 if qt == 0 else 0
                    GD(lambda e, lo=lo, woff=woff, qt=qt, q0=q0: e.dma_start(
                        out=AP(hb_exp, qt * EXPP + woff * 1, [[160, 128], [1, 160 - woff]]),
                        in_=SCR[:, lo:q0 + 128]), wait=[EXP_RDY, (sg, sg.n)])
                    GD(lambda e, qt=qt: e.dma_start(
                        out=MS1[:, 256:288], in_=AP(hb_exp, qt * EXPP + 1, [[161, 128], [1, 32]])),
                       wait=[(sg, sg.n)])
                    tpw = (sg, sg.n)
                    PE(lambda e: e.transpose(PSD[0:32, 160:288], MS1[:, 256:288], CXID[:]),
                       wait=[tpw])
                    pw = (sp, sp.n)
                    V(lambda e: e.tensor_copy(out=MS1[0:32, 448:576], in_=PSD[0:32, 160:288]),
                      wait=[pw])
                    TT_RDY = (sv, sv.n)
                    tev = {}
                    for kk in range(qt + 1):
                        ksl = slice(kk * 128, (kk + 1) * 128)
                        pcol = (kk % 2) * 512
                        dsl = (kk % 2) * 128
                        PE(lambda e, ksl=ksl, pcol=pcol: e.transpose(
                            PSB[:, pcol:pcol + 128], SCR[:, ksl], CXID[:]),
                           wait=[tev[pcol]] if pcol in tev else [EXP_RDY])
                        pw = (sp, sp.n)
                        V(lambda e, pcol=pcol, dsl=dsl: e.tensor_copy(
                            out=MS1[:, 576 + dsl:704 + dsl], in_=PSB[:, pcol:pcol + 128]),
                          wait=[pw])
                        vw = (sv, sv.n)
                        tev[pcol] = vw
                        avw = [vw] + ([ctx_ev] if kk == 0 else [])
                        PE(lambda e, ksl=ksl, dsl=dsl, kk=kk, hb=hb: e.matmul(
                            PSC[hb:hb + 64, 0:128],
                            VP[:, kk * 128 + hb:kk * 128 + hb + 64],
                            MS1[:, 576 + dsl:704 + dsl],
                            start=(kk == 0), stop=False, tile_position=(0, hb)),
                           wait=avw)
                    PE(lambda e, hb=hb: e.matmul(PSC[hb:hb + 64, 0:128], RELDIF,
                                                 MS1[0:32, 448:576], start=False, stop=True,
                                                 tile_position=(0, hb)),
                       wait=[TT_RDY])
                    pav = (sp, sp.n)
                    V(lambda e, hb=hb, qsl=qsl: e.tensor_copy(out=CTX[hb:hb + 64, qsl],
                                                              in_=PSC[hb:hb + 64, 0:128]),
                      wait=[pav])
                    ctx_ev = (sv, sv.n)
                    wq_slot()

            opw = [ctx_ev]
            for dc in range(8):
                for tbb in range(4):
                    pt = PSC if (dc * 4 + tbb) % 2 == 0 else PSD
                    PE(lambda e, dc=dc, tbb=tbb, pt=pt: e.matmul(
                        pt[:, 0:512], WX[:, dc * 128:dc * 128 + 128],
                        CTX[:, tbb * 512:(tbb + 1) * 512], start=True, stop=True),
                       wait=opw + [(sv, sv.n)])
                    opw = []
                    pv = (sp, sp.n)
                    dst = A[:, dc * 2048 + tbb * 512:dc * 2048 + (tbb + 1) * 512]
                    if hp == 0:
                        V(lambda e, dst=dst, pt=pt: e.tensor_copy(out=dst, in_=pt[:, 0:512]),
                          wait=[pv])
                    else:
                        V(lambda e, dst=dst, pt=pt: e.tensor_add(dst, dst, pt[:, 0:512]),
                          wait=[pv])

        load_pair_weights(0, (sd, 0))
        load_x0_block(0, (sd, 0))
        for hp in range(8):
            s1_pair(hp)
            if hp == 1:
                wq_flush()
                wq_mid()
            if hp == 4:
                wq_flush()
            ckpt(0.6 + hp * 0.01)
        barrier()

        def dump(name, src_ap, rows=128):
            h = dbgt(name, rows, src_ap.shape[1])
            if h is not None:
                barrier()
                SY(lambda e, h=h, s=src_ap: e.dma_start(out=h, in_=s))
                barrier()

        dump("s1A", A[:, 0:16384])
        dump("s1ctx", CTX)
        ckpt(1)

        X0R0 = EW[:, 0:512]; X0R1 = EW[:, 512:1024]
        SQT = EW[:, 1024:1536]
        R0 = EW[0:1, 1536:2048]; R1 = EW[0:1, 2048:2560]; R2 = EW[0:1, 2560:3072]
        EPS = 1e-5

        for tbb in range(4):
            sq_ev = (sc, sc.n)
            sq_pe = (sp, 0)
            for dc in range(8):
                x1t = A[:, dc * 2048 + tbb * 512:dc * 2048 + (tbb + 1) * 512]
                xr = X0R0 if dc % 2 == 0 else X0R1
                ap = AP(din["x0"].tensor, dc * 128 * T + tbb * 512, [[T, 128], [1, 512]])
                SY(lambda e, xr=xr, s=ap: e.dma_start(out=xr, in_=s), wait=[(sv, sv.n)])
                xw = (sd, sd.n)
                V(lambda e, x1t=x1t, dc=dc, xr=xr: e.scalar_tensor_tensor(
                    out=x1t, in0=x1t, scalar=RELOUT[:, dc:dc + 1], in1=xr,
                    op0=ALU.add, op1=ALU.add), wait=[xw])
                vw = (sv, sv.n)
                PE(lambda e, x1t=x1t, dc=dc: e.matmul(PSD[0:1, 0:512], ONES_C, x1t,
                                                      start=(dc == 0), stop=(dc == 7)),
                   wait=[vw] + ([(sv, sv.n)] if dc == 0 else []))
                SCE(lambda e, x1t=x1t: e.activation(out=SQT, in_=x1t, func=AF.Square),
                    wait=[vw, sq_pe] + ([sq_ev] if dc == 0 else []))
                cw_ = (sc, sc.n)
                PE(lambda e, dc=dc: e.matmul(PSC[0:1, 0:512], ONES_C, SQT,
                                             start=(dc == 0), stop=(dc == 7)), wait=[cw_])
                sq_pe = (sp, sp.n)
            pw = (sp, sp.n)
            V(lambda e: e.tensor_scalar(out=R0, in0=PSD[0:1, 0:512], scalar1=1.0 / DM,
                                        scalar2=None, op0=ALU.mult), wait=[pw])
            V(lambda e: e.tensor_scalar(out=R1, in0=PSC[0:1, 0:512], scalar1=1.0 / DM,
                                        scalar2=None, op0=ALU.mult))
            V(lambda e: e.tensor_mul(R2, R0, R0))
            V(lambda e: e.tensor_sub(R1, R1, R2))
            V(lambda e: e.tensor_scalar(out=R1, in0=R1, scalar1=EPS, scalar2=None, op0=ALU.add))
            V(lambda e: e.reciprocal(out=R1, in_=R1))
            vw = (sv, sv.n)
            SCE(lambda e: e.activation(out=R1, in_=R1, func=AF.Sqrt), wait=[vw])
            cw_ = (sc, sc.n)
            V(lambda e: e.tensor_mul(R2, R0, R1), wait=[cw_])
            vw = (sv, sv.n)
            PE(lambda e: e.matmul(PSD[:, 0:512], ONES_R, R1, start=True, stop=True), wait=[vw])
            PE(lambda e: e.matmul(PSC[:, 0:512], ONES_R, R2, start=True, stop=True))
            pw = (sp, sp.n)
            for dc in range(8):
                x1t = A[:, dc * 2048 + tbb * 512:dc * 2048 + (tbb + 1) * 512]
                V(lambda e, x1t=x1t: e.tensor_mul(x1t, x1t, PSD[:, 0:512]),
                  wait=[pw] if dc == 0 else [])
                V(lambda e, x1t=x1t: e.tensor_sub(x1t, x1t, PSC[:, 0:512]))
            sq_ev = (sv, sv.n)

        barrier()
        dump("s2x1", A[:, 0:16384])
        ckpt(2)

        tev = {}
        for tc in range(16):
            for dc in range(8):
                po = ((tc * 8 + dc) % 2) * 512
                src = A[:, dc * 2048 + tc * 128:dc * 2048 + tc * 128 + 128]
                PE(lambda e, src=src, po=po: e.transpose(PSB[:, po:po + 128], src, CXID[:]),
                   wait=[tev[po]] if po in tev else [])
                pw = (sp, sp.n)
                dst = B[:, tc * 1024 + dc * 128:tc * 1024 + dc * 128 + 128]
                V(lambda e, dst=dst, po=po: e.tensor_copy(out=dst, in_=PSB[:, po:po + 128]),
                  wait=[pw])
                tev[po] = (sv, sv.n)
        barrier()
        dump("s2x1tm", B[:, 0:16384])
        ckpt(3)

        CWB = en(nc.sbuf_tensor("cwb", [128, 128], F32))
        SY(lambda e: e.dma_start(out=CWB[:, 0:96],
                                 in_=AP(din["cw"].tensor, 0, [[3, 128], [384, 32], [1, 3]])))
        SY(lambda e: e.dma_start(out=CWB[:, 96:128],
                                 in_=AP(din["cb"].tensor, 0, [[1, 128], [128, 32]]),
                                 allow_slow_non_contiguous=True))
        CW_RDY = (sd, sd.n)

        w1q_done = wq["wn"][31]
        w2q_done = wq["wn"][63]

        MMRC = MM[:, 32:48]
        for tc in range(16):
            xt = B[:, tc * 1024:(tc + 1) * 1024]
            y = A[:, tc * 1024:(tc + 1) * 1024]
            V(lambda e, xt=xt: e.reduce_max(out=MM[:, 20:21], in_=xt, axis=AX.X,
                                            apply_absolute_value=True))
            V(lambda e: e.tensor_scalar(out=MM[:, 21:22], in0=MM[:, 20:21], scalar1=1e-5,
                                        scalar2=None, op0=ALU.max))
            V(lambda e, tc=tc: e.tensor_scalar(out=MMRC[:, tc:tc + 1], in0=MM[:, 21:22],
                                               scalar1=1.0 / 127.0, scalar2=W1MCC,
                                               op0=ALU.mult, op1=ALU.mult))
            V(lambda e: e.reciprocal(out=MM[:, 22:23], in_=MM[:, 21:22]))
            V(lambda e, xt=xt, y=y: e.tensor_scalar(out=y, in0=xt, scalar1=MM[:, 22:23],
                                                    scalar2=127.0, op0=ALU.mult, op1=ALU.mult))
            round_tile(y, EW[:, 0:1024], EW[:, 1024:2048])
            V(lambda e, y=y: e.tensor_scalar(out=y, in0=y, scalar1=-128.0, scalar2=127.0,
                                             op0=ALU.max, op1=ALU.min))
        XQ_RDY = (sv, sv.n)
        GD(lambda e: e.dma_start(out=AP(hb_band, 0, [[1, 128], [128, 16]]), in_=MMRC,
                                 allow_slow_non_contiguous=True),
           wait=[XQ_RDY, (sg, sg.n)])
        GD(lambda e: e.dma_start(out=EW[0:1, 1538:3586], in_=AP(hb_band, 0, [[2048, 1], [1, 2048]])),
           wait=[(sg, sg.n)])
        ROW1_RDY = (sg, sg.n)
        ROW1 = EW[0:1, 1538:3586]
        tev = {}
        for dc in range(8):
            for tcg in range(4):
                po = (tcg % 2) * 512
                for t4 in range(4):
                    tc = tcg * 4 + t4
                    src = A[:, tc * 1024 + dc * 128:tc * 1024 + dc * 128 + 128]
                    PE(lambda e, src=src, po=po, t4=t4: e.transpose(
                        PSB[:, po + t4 * 128:po + t4 * 128 + 128], src, CXID[:]),
                       wait=([XQ_RDY] if dc == 0 and tcg == 0 and t4 == 0 else []) +
                            ([tev[po]] if (t4 == 0 and po in tev) else []))
                pw = (sp, sp.n)
                V(lambda e, po=po: e.tensor_copy(out=Dq[:, 0:512], in_=PSB[:, po:po + 512]),
                  wait=[pw, (sg, sg.n), (swqw, swqw.n)])
                vw = (sv, sv.n)
                tev[po] = vw
                oap = AP(hb_xq, dc * 128 * T + tcg * 512, [[T, 128], [1, 512]])
                GD(lambda e, o=oap: e.dma_start(out=o, in_=Dq[:, 0:512]), wait=[vw])
        barrier()
        ckpt(3.5)

        RB1 = EW[:, 0:258]; SIG = EW[:, 258:514]; SC2B = EW[:, 514:770]
        YQ = EW[:, 770:1026]; RA = EW[:, 1026:1282]; RB_ = EW[:, 1282:1538]
        XF = EW[:, 3586:4610]; XC = EW[:, 4610:5634]
        HMX = MM[:, 28:30]; HMXC = MM[:, 30:32]; RC2 = MM[:, 48:50]

        def xqs_load(tb):
            t0_ = tb * 256
            for cc in range(8):
                dsl = Dq[:, cc * 258:(cc + 1) * 258]
                if tb == 0:
                    V(lambda e, dsl=dsl: e.memset(dsl[:, 0:1], 0.0),
                      wait=[(sp, sp.n)] if cc == 0 else [])
                    ap = AP(hb_xq, cc * 128 * T, [[T, 128], [1, 257]])
                    SY(lambda e, s=ap, dsl=dsl: e.dma_start(out=dsl[:, 1:258], in_=s),
                       wait=[(sv, sv.n), (sp, sp.n), w1q_done] if cc == 0 else [])
                elif tb == 7:
                    V(lambda e, dsl=dsl: e.memset(dsl[:, 257:258], 0.0),
                      wait=[(sp, sp.n)] if cc == 0 else [])
                    ap = AP(hb_xq, cc * 128 * T + t0_ - 1, [[T, 128], [1, 257]])
                    SY(lambda e, s=ap, dsl=dsl: e.dma_start(out=dsl[:, 0:257], in_=s),
                       wait=[(sv, sv.n), (sp, sp.n)] if cc == 0 else [])
                else:
                    ap = AP(hb_xq, cc * 128 * T + t0_ - 1, [[T, 128], [1, 258]])
                    SY(lambda e, s=ap, dsl=dsl: e.dma_start(out=dsl, in_=s),
                       wait=[(sp, sp.n)] if cc == 0 else [])
            return (sd, sd.n)

        xqs = {0: xqs_load(0)}
        for tb in range(8):
            tok0 = tb * 256
            XQS_RDY = xqs[tb]
            lo = max(tok0 - 1, 0)
            w = 257 if tb in (0, 7) else 258
            off = 1 if tb == 0 else 0
            PE(lambda e, lo=lo, w=w, off=off: e.matmul(
                PSD[:, 0:w], ONES_R, ROW1[0:1, lo:lo + w], start=True, stop=True),
               wait=[ROW1_RDY, (sv, sv.n), (sc, sc.n)])
            pw = (sp, sp.n)
            if tb == 0:
                V(lambda e: e.memset(RB1[:, 0:1], 1.0))
            if tb == 7:
                V(lambda e: e.memset(RB1[:, 257:258], 1.0))
            V(lambda e, w=w, off=off: e.tensor_copy(out=RB1[:, off:off + w], in_=PSD[:, 0:w]),
              wait=[pw])
            RB1_RDY = (sv, sv.n)

            def w1q_load(hc, gate):
                wb = hc % 2
                sem = sWQa if wb == 0 else sWQb
                wsl = Dq[:, 2064 + wb * 1024:2064 + wb * 1024 + 1024]
                ap = AP(hb_w1q, hc * 128, [[4096, 128], [4096 * 128, 8], [1, 128]])
                EM("sync", lambda e, s=ap, wsl=wsl: e.dma_start(out=wsl, in_=s),
                   wait=list(gate), inc=sem, inck=16)

            w1q_load(0, [w1q_done] if tb == 0 else [(sp, sp.n)])
            hs_done = None
            psb_ev = {}
            for hc in range(32):
                wb = (hc % 2)
                wsl = Dq[:, 2064 + wb * 1024:2064 + wb * 1024 + 1024]
                wld = (sWQa if wb == 0 else sWQb, (sWQa if wb == 0 else sWQb).n)
                if hc < 31:
                    w1q_load(hc + 1, [(sp, sp.n)])
                po = (hc % 2) * 512
                for cc in range(8):
                    PE(lambda e, cc=cc, po=po, wsl=wsl: e.matmul(
                        PSB[:, po:po + 258], wsl[:, cc * 128:cc * 128 + 128],
                        Dq[:, cc * 258:(cc + 1) * 258], start=(cc == 0), stop=(cc == 7)),
                       wait=([wld, XQS_RDY] + ([psb_ev[po]] if po in psb_ev else []))
                       if cc == 0 else [])
                pw = (sp, sp.n)
                hst = A[:, hc * 258:(hc + 1) * 258]
                V(lambda e, hst=hst, po=po: e.tensor_mul(hst, PSB[:, po:po + 258], RB1),
                  wait=[pw, RB1_RDY])
                psb_ev[po] = (sv, sv.n)
                c0 = CWB[:, hc * 3:hc * 3 + 1]; c1 = CWB[:, hc * 3 + 1:hc * 3 + 2]
                c2 = CWB[:, hc * 3 + 2:hc * 3 + 3]; cbc = CWB[:, 96 + hc:97 + hc]
                V(lambda e, hst=hst, c0=c0: e.tensor_scalar(
                    out=PSD[:, 0:256], in0=hst[:, 0:256], scalar1=c0, scalar2=None,
                    op0=ALU.mult), wait=[CW_RDY, (sc, sc.n)] if hc == 0 else [(sc, sc.n)])
                V(lambda e, hst=hst, c1=c1: e.scalar_tensor_tensor(
                    out=PSD[:, 0:256], in0=hst[:, 1:257], scalar=c1, in1=PSD[:, 0:256],
                    op0=ALU.mult, op1=ALU.add))
                V(lambda e, hst=hst, c2=c2: e.scalar_tensor_tensor(
                    out=PSD[:, 0:256], in0=hst[:, 2:258], scalar=c2, in1=PSD[:, 0:256],
                    op0=ALU.mult, op1=ALU.add))
                vw = (sv, sv.n)
                SCE(lambda e, cbc=cbc: e.activation(out=SIG, in_=PSD[:, 0:256],
                                                    func=AF.Sigmoid, bias=cbc), wait=[vw])
                cw2 = (sc, sc.n)
                V(lambda e, hst=hst, cbc=cbc: e.scalar_tensor_tensor(
                    out=hst[:, 0:256], in0=PSD[:, 0:256], scalar=cbc, in1=SIG,
                    op0=ALU.add, op1=ALU.mult), wait=[cw2])
                hs_done = (sv, sv.n)
            if tb < 7:
                xqs[tb + 1] = xqs_load(tb + 1)
            ckpt(3.6 + tb * 0.001)

            ps_hm = {}
            for half in range(2):
                for grp in range(8):
                    bank = PSC if (half * 8 + grp) % 2 == 0 else PSD
                    for j in range(4):
                        hc = grp * 4 + j
                        hst = A[:, hc * 258:(hc + 1) * 258]
                        w = []
                        if j == 0:
                            b = (half * 8 + grp) % 2
                            w.append(ps_hm[b] if b in ps_hm else hs_done)
                        PE(lambda e, hst=hst, half=half, bank=bank, j=j: e.transpose(
                            bank[:, j * 128:(j + 1) * 128],
                            hst[:, half * 128:half * 128 + 128], CXID[:]), wait=w)
                    pw = (sp, sp.n)
                    if grp == 0:
                        V(lambda e, half=half, bank=bank: e.reduce_max(
                            out=HMX[:, half:half + 1], in_=bank[:, 0:512], axis=AX.X,
                            apply_absolute_value=True), wait=[pw])
                    else:
                        V(lambda e, bank=bank: e.reduce_max(
                            out=MM[:, 20:21], in_=bank[:, 0:512], axis=AX.X,
                            apply_absolute_value=True), wait=[pw])
                        V(lambda e, half=half: e.tensor_max(HMX[:, half:half + 1],
                                                            HMX[:, half:half + 1],
                                                            MM[:, 20:21]))
                    ps_hm[(half * 8 + grp) % 2] = (sv, sv.n)
            hs_done = (sv, sv.n)
            V(lambda e: e.tensor_scalar(out=HMXC, in0=HMX, scalar1=1e-5, scalar2=None,
                                        op0=ALU.max))
            V(lambda e: e.tensor_scalar(out=RC2, in0=HMXC, scalar1=1.0 / 127.0,
                                        scalar2=W2MCC, op0=ALU.mult, op1=ALU.mult))
            V(lambda e: e.reciprocal(out=MM[:, 20:22], in_=HMXC))
            vw = (sv, sv.n)
            PE(lambda e: e.transpose(PSD[0:1, 256:384], MM[:, 20:21], CXID[:]), wait=[vw])
            PE(lambda e: e.transpose(PSD[0:1, 384:512], MM[:, 21:22], CXID[:]))
            pw = (sp, sp.n)
            V(lambda e: e.tensor_scalar(out=EW[0:1, 5634:5890], in0=PSD[0:1, 256:512],
                                        scalar1=127.0, scalar2=None, op0=ALU.mult), wait=[pw])
            vw = (sv, sv.n)
            PE(lambda e: e.matmul(PSD[:, 0:256], ONES_R, EW[0:1, 5634:5890],
                                  start=True, stop=True), wait=[vw])
            pw = (sp, sp.n)
            V(lambda e: e.tensor_copy(out=SC2B, in_=PSD[:, 0:256]), wait=[pw])
            SC2_RDY = (sv, sv.n)

            def w2q_load(hc, gate):
                wb2 = hc % 2
                sem = sWTa if wb2 == 0 else sWTb
                wsl = Dq[:, 4112 + wb2 * 1024:4112 + wb2 * 1024 + 1024]
                ap = AP(hb_w2q, hc * 128 * 1024, [[1024, 128], [1, 1024]])
                EM("sync", lambda e, s=ap, wsl=wsl: e.dma_start(out=wsl, in_=s),
                   wait=list(gate), inc=sem, inck=16)

            YQB = A[:, 8256:9280]; AIN = A[:, 9280:10304]; BDF = A[:, 10304:11328]
            w2q_load(0, [w2q_done] if tb == 0 else [])
            for hcb in range(8):
                slot = Dq[:, 2064 + (hcb % 2) * 1024:2064 + (hcb % 2) * 1024 + 1024]
                for j in range(4):
                    hc = hcb * 4 + j
                    hst = A[:, hc * 258:(hc + 1) * 258]
                    w = []
                    if j == 0:
                        w.append((sg, sg.n))
                        if hcb == 0:
                            w.append(SC2_RDY)
                    V(lambda e, hst=hst, j=j: e.tensor_mul(
                        YQB[:, j * 256:(j + 1) * 256], hst[:, 0:256], SC2B), wait=w)
                for hf in range(2):
                    GD(lambda e, hf=hf: e.dma_start(out=MI[:, 0:512],
                                                    in_=YQB[:, hf * 512:(hf + 1) * 512]),
                       wait=[(sv, sv.n), (sg, sg.n)] if hf == 0 else [(sg, sg.n)])
                    GD(lambda e, hf=hf: e.dma_start(out=AIN[:, hf * 512:(hf + 1) * 512],
                                                    in_=MI[:, 0:512]), wait=[(sg, sg.n)])
                dw = (sg, sg.n)
                V(lambda e: e.tensor_sub(BDF, YQB, AIN), wait=[dw])
                V(lambda e: e.tensor_scalar(out=YQB, in0=BDF, scalar1=0.5, scalar2=None,
                                            op0=ALU.is_ge))
                V(lambda e: e.tensor_add(YQB, YQB, AIN))
                V(lambda e: e.tensor_scalar(out=AIN, in0=BDF, scalar1=-0.5, scalar2=None,
                                            op0=ALU.is_le))
                V(lambda e: e.tensor_sub(YQB, YQB, AIN))
                V(lambda e, slot=slot: e.tensor_scalar(out=slot, in0=YQB, scalar1=-128.0,
                                                       scalar2=127.0, op0=ALU.max, op1=ALU.min),
                  wait=[(sp, sp.n)])
                hq_rdy = (sv, sv.n)
                for j in range(4):
                    hc = hcb * 4 + j
                    hb2 = hc % 2
                    wb2sem = sWTa if hb2 == 0 else sWTb
                    wld = (wb2sem, wb2sem.n)
                    wsl = Dq[:, 4112 + hb2 * 1024:4112 + hb2 * 1024 + 1024]
                    if hc < 31:
                        w2q_load(hc + 1, [(sp, sp.n)])
                    for tc2 in range(2):
                        for dh in range(2):
                            bi = tc2 * 2 + dh
                            HQ = slot[:, j * 256:(j + 1) * 256]
                            PE(lambda e, tc2=tc2, dh=dh, bi=bi, HQ=HQ, wsl=wsl, hc=hc: e.matmul(
                                PSA[:, bi * 512:(bi + 1) * 512],
                                HQ[:, tc2 * 128:tc2 * 128 + 128],
                                wsl[:, dh * 512:(dh + 1) * 512],
                                start=(hc == 0), stop=(hc == 31)),
                               wait=([wld] + ([hq_rdy] if j == 0 else []))
                               if (tc2 == 0 and dh == 0) else [])
            mm2_done = (sp, sp.n)
            ckpt(3.7 + tb * 0.001)

            for tc2 in range(2):
                tcg = tb * 2 + tc2
                x1t = B[:, tcg * 1024:(tcg + 1) * 1024]
                for dh in range(2):
                    bi = tc2 * 2 + dh
                    V(lambda e, bi=bi, dh=dh, tc2=tc2: e.tensor_scalar(
                        out=XF[:, dh * 512:(dh + 1) * 512], in0=PSA[:, bi * 512:(bi + 1) * 512],
                        scalar1=RC2[:, tc2:tc2 + 1], scalar2=None, op0=ALU.mult),
                      wait=[mm2_done] if dh == 0 and tc2 == 0 else [])
                V(lambda e, x1t=x1t: e.tensor_add(XF, XF, x1t))
                V(lambda e: e.reduce_sum(out=MM[:, 20:21], in_=XF, axis=AX.X))
                V(lambda e: e.tensor_scalar(out=MM[:, 20:21], in0=MM[:, 20:21],
                                            scalar1=1.0 / DM, scalar2=None, op0=ALU.mult))
                V(lambda e: e.tensor_scalar(out=XC, in0=XF, scalar1=MM[:, 20:21],
                                            scalar2=None, op0=ALU.subtract))
                vw = (sv, sv.n)
                SCE(lambda e: e.activation(out=XF, in_=XC, func=AF.Square), wait=[vw])
                cw2 = (sc, sc.n)
                V(lambda e: e.reduce_sum(out=MM[:, 21:22], in_=XF, axis=AX.X), wait=[cw2])
                V(lambda e: e.tensor_scalar(out=MM[:, 21:22], in0=MM[:, 21:22],
                                            scalar1=1.0 / DM, scalar2=EPS,
                                            op0=ALU.mult, op1=ALU.add))
                V(lambda e: e.reciprocal(out=MM[:, 21:22], in_=MM[:, 21:22]))
                vw = (sv, sv.n)
                SCE(lambda e: e.activation(out=MM[:, 22:23], in_=MM[:, 21:22], func=AF.Sqrt),
                    wait=[vw])
                cw2 = (sc, sc.n)
                V(lambda e, x1t=x1t: e.tensor_scalar(out=x1t, in0=XC, scalar1=MM[:, 22:23],
                                                     scalar2=None, op0=ALU.mult), wait=[cw2])
        barrier()
        dump("s3x2tm", B[:, 0:16384])
        ckpt(4)

        DECS = MM[0:1, 50:51]
        SY(lambda e: e.dma_start(out=DECS, in_=din["dec"]))
        dw_ = (sd, sd.n)
        SCE(lambda e: e.activation(out=MM[0:1, 51:52], in_=DECS, func=AF.Sigmoid), wait=[dw_])
        SCE(lambda e: e.activation(out=MM[0:1, 52:53], in_=MM[0:1, 51:52], func=AF.Ln))
        LNA = MM[0:1, 52:53]
        cw_ = (sc, sc.n)
        PE(lambda e: e.matmul(PSD[:, 0:1], ONES_R, LNA, start=True, stop=True), wait=[cw_])
        pw = (sp, sp.n)
        V(lambda e: e.tensor_copy(out=MM[:, 53:54], in_=PSD[:, 0:1]), wait=[pw])
        LNAC = MM[:, 53:54]
        GP(lambda e: e.iota(EW[0:1, 0:128], pattern=[[1, 128]], base=0, channel_multiplier=0,
                            allow_small_or_imprecise_dtypes=True), wait=[(sv, sv.n), (sc, sc.n)])
        gw = (sq, sq.n)
        V(lambda e: e.tensor_scalar(out=EW[0:1, 128:256], in0=EW[0:1, 0:128], scalar1=LNA,
                                    scalar2=None, op0=ALU.mult), wait=[gw])
        V(lambda e: e.tensor_scalar(out=EW[0:1, 256:384], in0=EW[0:1, 0:128], scalar1=LNA,
                                    scalar2=LNA, op0=ALU.mult, op1=ALU.add))
        V(lambda e: e.tensor_scalar(out=MM[0:1, 54:55], in0=LNA, scalar1=128.0, scalar2=None,
                                    op0=ALU.mult))
        V(lambda e: e.tensor_scalar(out=MM[:, 55:56], in0=CX["c_rev_c"][:], scalar1=LNAC,
                                    scalar2=None, op0=ALU.mult))
        vw = (sv, sv.n)
        APOW = EW[0:1, 128:256]
        APOW1 = EW[0:1, 256:384]
        SCE(lambda e: e.activation(out=APOW, in_=APOW, func=AF.Exp), wait=[vw])
        SCE(lambda e: e.activation(out=APOW1, in_=APOW1, func=AF.Exp))
        SCE(lambda e: e.activation(out=MM[0:1, 54:55], in_=MM[0:1, 54:55], func=AF.Exp))
        SCE(lambda e: e.activation(out=MM[:, 56:57], in_=MM[:, 55:56], func=AF.Exp))
        A128 = MM[0:1, 54:55]
        APREV = MM[:, 56:57]
        cw_ = (sc, sc.n)
        PE(lambda e: e.matmul(PSD[:, 0:128], ONES_R, APOW, start=True, stop=True), wait=[cw_])
        pw = (sp, sp.n)
        LTS = EW[:, 384:512]
        V(lambda e: e.tensor_mul(LTS, PSD[:, 0:128], CXTRI[:]), wait=[pw])
        vw = (sv, sv.n)
        GD(lambda e: e.dma_start(out=AP(hb_LT, 0, [[129, 128], [1, 128]]), in_=LTS),
           wait=[vw, (sg, sg.n)])
        LTT = EW[:, 512:640]
        GD(lambda e: e.dma_start(out=LTT, in_=AP(hb_LT, 0, [[128, 128], [1, 128]])),
           wait=[(sg, sg.n)])
        LT_RDY = (sg, sg.n)

        EPRV = EW[0:1, 640:1664]
        ECUR = EW[0:1, 1664:2688]
        ps_ev = {}
        for c in range(16):
            x2c = B[:, c * 1024:(c + 1) * 1024]
            grp = c % 2
            for fc in range(8):
                col = grp * 1024 + fc * 128
                PE(lambda e, x2c=x2c, fc=fc, col=col, c=c: e.matmul(
                    PSA[:, col:col + 128], x2c[:, fc * 128:fc * 128 + 128], LTT,
                    start=True, stop=(c == 0)),
                   wait=([LT_RDY] if (c == 0 and fc == 0) else []) +
                        ([ps_ev[grp]] if (fc == 0 and grp in ps_ev) else []))
                if c > 0:
                    PE(lambda e, fc=fc, col=col: e.matmul(
                        PSA[:, col:col + 128], EPRV[0:1, fc * 128:fc * 128 + 128], APOW1,
                        start=False, stop=True),
                       wait=[(sv, sv.n)] if fc == 0 else [])
            pw = (sp, sp.n)
            for fc in range(8):
                col = grp * 1024 + fc * 128
                dst = A[:, fc * 2048 + c * 128:fc * 2048 + c * 128 + 128]
                V(lambda e, dst=dst, col=col: e.tensor_copy(out=dst, in_=PSA[:, col:col + 128]),
                  wait=[pw] if fc == 0 else [])
            ps_ev[grp] = (sv, sv.n)
            if c < 15:
                for hf in range(2):
                    pt = PSD if hf == 0 else PSC
                    PE(lambda e, x2c=x2c, hf=hf, pt=pt: e.matmul(
                        pt[0:1, 0:512], APREV,
                        x2c[:, hf * 512:(hf + 1) * 512], start=True, stop=True),
                       wait=[(sv, sv.n)] if hf == 0 else [])
                pw = (sp, sp.n)
                V(lambda e: e.tensor_copy(out=ECUR[0:1, 0:512], in_=PSD[0:1, 0:512]), wait=[pw])
                V(lambda e: e.tensor_copy(out=ECUR[0:1, 512:1024], in_=PSC[0:1, 0:512]))
                if c == 0:
                    V(lambda e: e.tensor_copy(out=EPRV, in_=ECUR), wait=[(sp, sp.n)])
                else:
                    V(lambda e: e.scalar_tensor_tensor(out=EPRV, in0=EPRV, scalar=A128,
                                                       in1=ECUR, op0=ALU.mult, op1=ALU.add),
                      wait=[(sp, sp.n)])
        barrier()
        dump("s4S", A[:, 0:16384])
        ckpt(5)

        B1B = TBL[:, 0:1024]; GMB = TBL[:, 1024:2048]; BTB = TBL[:, 2048:3072]
        for nm, dst in (("b1r", B1B), ("gamr", GMB), ("betr", BTB)):
            ld = SY(lambda e, nm=nm: e.dma_start(out=EW[0:1, 4096:5120], in_=din[nm]),
                    wait=[(sp, sp.n), (sv, sv.n)])
            for hf in range(2):
                PE(lambda e, hf=hf: e.matmul(PSD[:, 0:512], ONES_R,
                                             EW[0:1, 4096 + hf * 512:4096 + hf * 512 + 512],
                                             start=True, stop=True),
                   wait=[(sd, ld), (sv, sv.n)])
                pw = (sp, sp.n)
                V(lambda e, dst=dst, hf=hf: e.tensor_copy(out=dst[:, hf * 512:(hf + 1) * 512],
                                                          in_=PSD[:, 0:512]), wait=[pw])

        Y4 = EW[:, 0:4096]
        YB = EW[:, 4096:5120]; YC = EW[:, 5120:6144]

        def wt_tile(gg, j):
            if gg % 2 == 0:
                return WW[:, j * 512:(j + 1) * 512]
            return (WW[:, 2048 + j * 512:2048 + (j + 1) * 512] if j < 2
                    else WX[:, (j - 2) * 512:(j - 1) * 512])

        wt_rdy = {}

        def wt_load_grp(gg, gate):
            sem = sWTa if gg % 2 == 0 else sWTb
            tcb_, r = divmod(gg, 4)
            eh_, dh_ = divmod(r, 2)
            semg = sWGa if gg % 2 == 0 else sWGb
            for j in range(4):
                dc = dh_ * 4 + j
                ap = AP(din["Wt"].tensor, dc * 128 * DM + eh_ * 512, [[DM, 128], [1, 512]])
                eng = "sync" if j % 2 == 0 else "gpsimd"
                EM(eng, lambda e, dst=wt_tile(gg, j), s=ap: e.dma_start(out=dst, in_=s),
                   wait=list(gate) if j < 2 else [], inc=(sem if j % 2 == 0 else semg),
                   inck=16)
            wt_rdy[gg] = [(sem, sem.n), (semg, semg.n)]

        wt_load_grp(0, [(sp, sp.n), (sv, sv.n)])
        wt_load_grp(1, [])
        for tcb in range(4):
            for eh in range(2):
                for dh in range(2):
                    gg = (tcb * 2 + eh) * 2 + dh
                    for j in range(4):
                        dc = dh * 4 + j
                        for t4 in range(4):
                            src = A[:, dc * 2048 + (tcb * 4 + t4) * 128:
                                    dc * 2048 + (tcb * 4 + t4) * 128 + 128]
                            w = []
                            if j == 0 and t4 == 0:
                                w += wt_rdy[gg]
                                if dh == 0:
                                    w.append((sv, sv.n))
                            PE(lambda e, src=src, dst=wt_tile(gg, j), dc=dc, t4=t4: e.matmul(
                                PSA[:, t4 * 512:(t4 + 1) * 512], src, dst,
                                start=(dc == 0), stop=(dc == 7)), wait=w)
                    if gg + 2 <= 15:
                        wt_load_grp(gg + 2, [(sp, sp.n)])
                pw = (sp, sp.n)
                for t4 in range(4):
                    V(lambda e, t4=t4, eh=eh: e.tensor_add(
                        Y4[:, t4 * 1024 + eh * 512:t4 * 1024 + eh * 512 + 512],
                        PSA[:, t4 * 512:(t4 + 1) * 512],
                        B1B[:, eh * 512:(eh + 1) * 512]), wait=[pw] if t4 == 0 else [])
            for t4 in range(4):
                tc = tcb * 4 + t4
                x2t = B[:, tc * 1024:(tc + 1) * 1024]
                Yt = Y4[:, t4 * 1024:(t4 + 1) * 1024]
                V(lambda e, Yt=Yt: e.reduce_sum(out=MM[:, 20:21], in_=Yt, axis=AX.X))
                V(lambda e: e.tensor_scalar(out=MM[:, 20:21], in0=MM[:, 20:21],
                                            scalar1=1.0 / DM, scalar2=None, op0=ALU.mult))
                V(lambda e, Yt=Yt: e.tensor_scalar(out=YC, in0=Yt, scalar1=MM[:, 20:21],
                                                   scalar2=None, op0=ALU.subtract))
                vw = (sv, sv.n)
                SCE(lambda e: e.activation(out=YB, in_=YC, func=AF.Square), wait=[vw])
                cw_ = (sc, sc.n)
                V(lambda e: e.reduce_sum(out=MM[:, 21:22], in_=YB, axis=AX.X), wait=[cw_])
                V(lambda e: e.tensor_scalar(out=MM[:, 21:22], in0=MM[:, 21:22], scalar1=1.0 / DM,
                                            scalar2=EPS, op0=ALU.mult, op1=ALU.add))
                V(lambda e: e.reciprocal(out=MM[:, 21:22], in_=MM[:, 21:22]))
                vw = (sv, sv.n)
                SCE(lambda e: e.activation(out=MM[:, 22:23], in_=MM[:, 21:22], func=AF.Sqrt),
                    wait=[vw])
                cw_ = (sc, sc.n)
                V(lambda e: e.tensor_scalar(out=YB, in0=YC, scalar1=MM[:, 22:23], scalar2=None,
                                            op0=ALU.mult), wait=[cw_])
                V(lambda e: e.tensor_mul(YB, YB, GMB))
                V(lambda e: e.tensor_add(YB, YB, BTB))
                V(lambda e, x2t=x2t: e.tensor_add(YC, YB, x2t))
                V(lambda e: e.reduce_sum(out=MM[:, 20:21], in_=YC, axis=AX.X))
                V(lambda e: e.tensor_scalar(out=MM[:, 20:21], in0=MM[:, 20:21], scalar1=1.0 / DM,
                                            scalar2=None, op0=ALU.mult))
                V(lambda e: e.tensor_scalar(out=YB, in0=YC, scalar1=MM[:, 20:21], scalar2=None,
                                            op0=ALU.subtract))
                vw = (sv, sv.n)
                SCE(lambda e: e.activation(out=YC, in_=YB, func=AF.Square), wait=[vw])
                cw_ = (sc, sc.n)
                V(lambda e: e.reduce_sum(out=MM[:, 21:22], in_=YC, axis=AX.X), wait=[cw_])
                V(lambda e: e.tensor_scalar(out=MM[:, 21:22], in0=MM[:, 21:22], scalar1=1.0 / DM,
                                            scalar2=EPS, op0=ALU.mult, op1=ALU.add))
                V(lambda e: e.reciprocal(out=MM[:, 21:22], in_=MM[:, 21:22]))
                vw = (sv, sv.n)
                SCE(lambda e: e.activation(out=MM[:, 22:23], in_=MM[:, 21:22], func=AF.Sqrt),
                    wait=[vw])
                cw_ = (sc, sc.n)
                V(lambda e, x2t=x2t: e.tensor_scalar(out=x2t, in0=YB, scalar1=MM[:, 22:23],
                                                     scalar2=None, op0=ALU.mult), wait=[cw_])
        barrier()
        dump("s4x3tm", B[:, 0:16384])
        ckpt(6)

        tev = {}
        for tc in range(16):
            for dc in range(8):
                po = ((tc * 8 + dc) % 2) * 512
                src = B[:, tc * 1024 + dc * 128:tc * 1024 + dc * 128 + 128]
                PE(lambda e, src=src, po=po: e.transpose(PSB[:, po:po + 128], src, CXID[:]),
                   wait=[tev[po]] if po in tev else [])
                pw = (sp, sp.n)
                dst = A[:, dc * 2048 + tc * 128:dc * 2048 + tc * 128 + 128]
                V(lambda e, dst=dst, po=po: e.tensor_copy(out=dst, in_=PSB[:, po:po + 128]),
                  wait=[pw])
                tev[po] = (sv, sv.n)
        barrier()
        dump("s5x3fm", A[:, 0:16384])
        ckpt(7)

        C_L1 = TBL[:, 0:2048]; S_L1 = TBL[:, 2048:4096]
        SY(lambda e: e.dma_start(out=C_L1, in_=din["c_cosl1"]), wait=[(sp, sp.n), (sv, sv.n)])
        SY(lambda e: e.dma_start(out=S_L1, in_=din["c_ssinl1"]))
        V(lambda e: e.memset(MM[:, 57:58], 128.0))
        V(lambda e: e.reciprocal(out=MM[:, 57:58], in_=MM[:, 57:58]))
        vw = (sv, sv.n)
        SCE(lambda e: e.activation(out=MM[:, 57:58], in_=MM[:, 57:58], func=AF.Sqrt), wait=[vw])
        R128C = MM[:, 57:58]
        barrier()
        ckpt(7.2)

        GTM = en(nc.sbuf_tensor("gtm", [128, 256], F32))
        SY(lambda e: e.dma_start(out=WX[:, 0:128],
                                 in_=AP(din["winT"].tensor, 4096,
                                        [[4112, 128], [128 * 4112, 8], [1, 16]])))
        rw = (sd, sd.n)
        for tc in range(16):
            for cc in range(8):
                PE(lambda e, tc=tc, cc=cc: e.matmul(
                    PSD[:, 0:16], A[:, cc * 2048 + tc * 128:cc * 2048 + tc * 128 + 128],
                    WX[:, cc * 16:(cc + 1) * 16], start=(cc == 0), stop=(cc == 7)),
                   wait=[rw, (sv, sv.n)] if cc == 0 else [])
            pw = (sp, sp.n)
            g = GTM[:, tc * 16:(tc + 1) * 16]
            V(lambda e, g=g: e.reduce_max(out=MM[:, 20:21], in_=PSD[:, 0:16], axis=AX.X),
              wait=[pw])
            V(lambda e, g=g: e.tensor_scalar(out=g, in0=PSD[:, 0:16], scalar1=MM[:, 20:21],
                                             scalar2=None, op0=ALU.is_equal))
            V(lambda e, g=g: e.scalar_tensor_tensor(out=g, in0=g, scalar=-1e30,
                                                    in1=PSD[:, 0:16], op0=ALU.mult, op1=ALU.add))
            V(lambda e, g=g: e.reduce_max(out=MM[:, 21:22], in_=g, axis=AX.X))
            V(lambda e, g=g: e.tensor_scalar(out=g, in0=PSD[:, 0:16], scalar1=MM[:, 21:22],
                                             scalar2=None, op0=ALU.is_ge))
            vw = (sv, sv.n)
            SCE(lambda e: e.activation(out=MM[:, 24:40], in_=PSD[:, 0:16], func=AF.Sigmoid),
                wait=[vw])
            cw_ = (sc, sc.n)
            V(lambda e, g=g: e.tensor_mul(g, g, MM[:, 24:40]), wait=[cw_])
        ckpt(7.3)
        GT = EW[0:16, 0:2048]
        tev = {}
        for tc in range(16):
            pt = PSD if tc % 2 == 0 else PSC
            PE(lambda e, tc=tc, pt=pt: e.transpose(
                pt[0:16, 128:256], GTM[:, tc * 16:(tc + 1) * 16], CXID[:]),
               wait=([(sv, sv.n)] if tc < 2 else []) + ([tev[tc % 2]] if tc % 2 in tev else []))
            pw = (sp, sp.n)
            V(lambda e, tc=tc, pt=pt: e.tensor_copy(out=GT[:, tc * 128:(tc + 1) * 128],
                                                    in_=pt[0:16, 128:256]),
              wait=[pw])
            tev[tc % 2] = (sv, sv.n)
        barrier()

        KH = EW[:, 0:2048]
        GT2 = WW[0:16, 0:2048]
        V(lambda e: e.tensor_copy(out=GT2, in_=GT))
        barrier()
        ckpt(7.5)
        VH = EW[:, 2048:4096]
        EXPB = EW[:, 4096:6144]

        attn_prev = (sv, sv.n)
        for h in range(8):
            pe_prev = (sp, sp.n)
            SY(lambda e, h=h: e.dma_start(out=WX[:, 0:1024],
                                          in_=AP(din["winT"].tensor, 2048 + h * 128,
                                                 [[4112, 128], [128 * 4112, 8], [1, 128]])),
               wait=[pe_prev])
            kwl = (sd, sd.n)
            SY(lambda e, h=h: e.dma_start(out=WW[:, 2048:3072],
                                          in_=AP(din["winT"].tensor, 3072 + h * 128,
                                                 [[4112, 128], [128 * 4112, 8], [1, 128]])),
               wait=[pe_prev])
            vwl = (sd, sd.n)

            for nb in range(4):
                for cc in range(8):
                    PE(lambda e, nb=nb, cc=cc: e.matmul(
                        PSA[:, nb * 512:(nb + 1) * 512], WX[:, cc * 128:(cc + 1) * 128],
                        A[:, cc * 2048 + nb * 512:cc * 2048 + (nb + 1) * 512],
                        start=(cc == 0), stop=(cc == 7)),
                       wait=[kwl, attn_prev] if (nb == 0 and cc == 0) else [])
            pw = (sp, sp.n)
            SCE(lambda e: e.copy(out=KH, in_=PSA[:, 0:2048]), wait=[pw])
            kev = (sc, sc.n)
            for nb in range(4):
                PE(lambda e, nb=nb: e.matmul(PSA[:, nb * 512:(nb + 1) * 512],
                                             CX["c_permhalf"][:], KH[:, nb * 512:(nb + 1) * 512],
                                             start=True, stop=True),
                   wait=[kev] if nb == 0 else [])
            pw = (sp, sp.n)
            V(lambda e: e.tensor_mul(EXPB, KH, C_L1), wait=[pw])
            V(lambda e: e.tensor_mul(KH, PSA[:, 0:2048], S_L1))
            V(lambda e: e.tensor_add(KH, KH, EXPB))
            V(lambda e: e.tensor_scalar(out=KH, in0=KH, scalar1=R128C, scalar2=None,
                                        op0=ALU.mult))
            KH_RDY = (sv, sv.n)
            tevv = {}
            for tc in range(16):
                po = (tc % 2)
                pt = PSC if po == 0 else PSD
                for cc in range(8):
                    w = []
                    if cc == 0:
                        w = [vwl] + ([KH_RDY] if tc == 0 else []) + \
                            ([tevv[po]] if po in tevv else [])
                    PE(lambda e, tc=tc, cc=cc, pt=pt: e.matmul(
                        pt[:, 0:128], A[:, cc * 2048 + tc * 128:cc * 2048 + tc * 128 + 128],
                        WW[:, 2048 + cc * 128:2048 + (cc + 1) * 128],
                        start=(cc == 0), stop=(cc == 7)), wait=w)
                pw = (sp, sp.n)
                V(lambda e, tc=tc, pt=pt: e.tensor_copy(out=VH[:, tc * 128:(tc + 1) * 128],
                                                        in_=pt[:, 0:128]), wait=[pw])
                tevv[po] = (sv, sv.n)
            VH_RDY = (sv, sv.n)
            SY(lambda e, h=h: e.dma_start(out=WX[:, 0:1024],
                                          in_=AP(din["winT"].tensor, 1024 + h * 128,
                                                 [[4112, 128], [128 * 4112, 8], [1, 128]])),
               wait=[(sp, sp.n)])
            qwl = (sd, sd.n)

            exp_ev = (sc, sc.n)
            ctx_ev = (sv, sv.n)
            for qt in range(16):
                q0 = qt * 128
                kext = q0 + 128
                for cc in range(8):
                    PE(lambda e, cc=cc, q0=q0: e.matmul(
                        PSD[:, 0:128], WX[:, cc * 128:(cc + 1) * 128],
                        A[:, cc * 2048 + q0:cc * 2048 + q0 + 128],
                        start=(cc == 0), stop=(cc == 7)),
                       wait=[qwl, (sv, sv.n)] if cc == 0 else [])
                pw = (sp, sp.n)
                QRB = WX[:, 1024:1152]; QTMP = WX[:, 1152:1280]
                V(lambda e: e.tensor_copy(out=QTMP, in_=PSD[:, 0:128]), wait=[pw])
                vw = (sv, sv.n)
                PE(lambda e: e.matmul(PSD[:, 128:256], CX["c_permhalf"][:], QTMP,
                                      start=True, stop=True), wait=[vw])
                pw = (sp, sp.n)
                V(lambda e, q0=q0: e.tensor_mul(QRB, QTMP, C_L1[:, q0:q0 + 128]), wait=[pw])
                V(lambda e, q0=q0: e.tensor_mul(QTMP, PSD[:, 128:256], S_L1[:, q0:q0 + 128]))
                V(lambda e: e.tensor_add(QRB, QRB, QTMP))
                QR_RDY = (sv, sv.n)
                scw = [QR_RDY, exp_ev]
                for o in range(0, kext, 512):
                    w = min(512, kext - o)
                    PE(lambda e, o=o, w=w: e.matmul(PSA[:, o:o + w], QRB, KH[:, o:o + w],
                                                    start=True, stop=True), wait=scw)
                    scw = []
                pw = (sp, sp.n)
                V(lambda e, q0=q0: e.tensor_add(PSA[:, q0:q0 + 128], PSA[:, q0:q0 + 128],
                                                CXM160[:, 32:160]), wait=[pw])
                NM = MM[:, 3:4]; LC = MM[:, 4:5]
                V(lambda e, kext=kext: e.reduce_max(out=NM, in_=PSA[:, 0:kext], axis=AX.X,
                                                    negate=True))
                vw = (sv, sv.n)
                SCE(lambda e, kext=kext: e.activation(out=EXPB[:, 0:kext], in_=PSA[:, 0:kext],
                                                      func=AF.Exp, bias=NM, scale=1.0,
                                                      accum_out=LC), wait=[vw])
                exp_ev = (sc, sc.n)
                V(lambda e: e.reciprocal(out=MM[:, 5:6], in_=LC), wait=[exp_ev])
                V(lambda e, kext=kext: e.tensor_scalar(out=EXPB[:, 0:kext], in0=EXPB[:, 0:kext],
                                                       scalar1=MM[:, 5:6], scalar2=None,
                                                       op0=ALU.mult))
                EXP_RDY = (sv, sv.n)
                tev2 = {}
                for kk in range(qt + 1):
                    pcol = (kk % 2) * 512
                    dsl = (kk % 2) * 128
                    PE(lambda e, kk=kk, pcol=pcol: e.transpose(
                        PSB[:, pcol:pcol + 128], EXPB[:, kk * 128:(kk + 1) * 128], CXID[:]),
                       wait=[tev2[pcol]] if pcol in tev2 else [EXP_RDY])
                    pw = (sp, sp.n)
                    V(lambda e, pcol=pcol, dsl=dsl: e.tensor_copy(
                        out=WX[:, 1280 + dsl:1408 + dsl], in_=PSB[:, pcol:pcol + 128]),
                      wait=[pw])
                    vw = (sv, sv.n)
                    tev2[pcol] = vw
                    PE(lambda e, kk=kk, dsl=dsl, qt=qt: e.matmul(
                        PSC[:, 0:128], VH[:, kk * 128:(kk + 1) * 128],
                        WX[:, 1280 + dsl:1408 + dsl],
                        start=(kk == 0), stop=(kk == qt)),
                       wait=[vw] + ([ctx_ev] if kk == 0 else []))
                pw = (sp, sp.n)
                V(lambda e, h=h, q0=q0: e.tensor_copy(
                    out=B[:, h * 2048 + q0:h * 2048 + q0 + 128], in_=PSC[:, 0:128]), wait=[pw])
                ctx_ev = (sv, sv.n)
            attn_prev = ctx_ev
            ckpt(7.6 + h * 0.01)
        barrier()
        dump("s5attn", B[:, 0:16384])
        ckpt(8)

        GB_ROW = EW[0:1, 0:512]
        QF = EW[:, 512:1024]
        AEW0 = EW[:, 1024:1536]
        AEW1 = EW[:, 1536:2048]
        X4B = TBL[:, 1024:2048]
        X4C = TBL[:, 2048:3072]
        X4S = TBL[:, 3072:4096]

        n_out_dma = 0
        ffw_prev = (sp, sp.n)
        ONES32 = GTM[32:33, 0:128]
        V(lambda e: e.memset(ONES32, 1.0), wait=[(sp, sp.n)])

        def qf_load(h, gate):
            SY(lambda e, h=h: e.dma_start(out=WW[:, 2048:3072],
                                          in_=AP(din["winT"].tensor, h * 128,
                                                 [[4112, 128], [128 * 4112, 8], [1, 128]])),
               wait=list(gate))
            return (sd, sd.n)

        qfw_next = qf_load(0, [(sp, sp.n), (sv, sv.n)])
        for tb5 in range(4):
            tok0 = tb5 * 512
            for h in range(8):
                qfw = qfw_next
                for cc in range(8):
                    PE(lambda e, cc=cc, tok0=tok0: e.matmul(
                        PSB[:, 0:512], WW[:, 2048 + cc * 128:2048 + (cc + 1) * 128],
                        A[:, cc * 2048 + tok0:cc * 2048 + tok0 + 512],
                        start=(cc == 0), stop=(cc == 7)),
                       wait=[qfw, (sv, sv.n)] if cc == 0 else [])
                if not (tb5 == 3 and h == 7):
                    qfw_next = qf_load((h + 1) % 8, [(sp, sp.n)])
                pw = (sp, sp.n)
                V(lambda e: e.tensor_copy(out=QF, in_=PSB[:, 0:512]), wait=[pw])
                QF_RDY = (sv, sv.n)

                def kv_load(ee, gate):
                    kb = (ee % 2) * 512
                    sem = sKVa if ee % 2 == 0 else sKVb
                    kap = AP(din["kfT"].tensor, (ee * 8 + h) * 128 * 256,
                             [[256, 128], [1, 256]])
                    vap = AP(din["vf"].tensor, (ee * 8 + h) * 256 * 128,
                             [[128, 128], [128 * 128, 2], [1, 128]])
                    EM("sync", lambda e, s=kap, d=WX[:, kb:kb + 256]: e.dma_start(out=d, in_=s),
                       wait=list(gate), inc=sem, inck=16)
                    EM("sync", lambda e, s=vap, d=WX[:, kb + 256:kb + 512]: e.dma_start(out=d, in_=s),
                       inc=sem, inck=16)

                def gb_slot(ee):
                    return EW[0:1, 0:512] if ee % 2 == 0 else EW[32:33, 0:512]

                def grow(ee, gate):
                    GD(lambda e, ee=ee, tok0=tok0: e.dma_start(
                        out=gb_slot(ee), in_=GT2[ee:ee + 1, tok0:tok0 + 512]),
                       wait=list(gate))
                    return (sg, sg.n)

                def gbank(ee):
                    return PSD[:, 0:512] if ee % 2 == 0 else PSB[:, 0:512]

                kv_load(0, [(sp, sp.n)])
                g0 = [(sp, sp.n)] + ([(sv, sv.n)] if (tb5 == 0 and h == 0) else [])
                grw = {0: grow(0, g0)}
                gbp = {}
                gelu_after = {}
                mul_after = {}
                lhs1s = (ONES_R, ONES32)
                PE(lambda e: e.matmul(gbank(0), lhs1s[0], gb_slot(0),
                                      start=True, stop=True),
                   wait=[grw[0], (sv, sv.n), QF_RDY])
                gbp[0] = (sp, sp.n)
                for ee in range(16):
                    kb = (ee % 2) * 512
                    ksl = WX[:, kb:kb + 256]
                    vsl = WX[:, kb + 256:kb + 512]
                    kvsem = sKVa if ee % 2 == 0 else sKVb
                    kvw = (kvsem, kvsem.n)
                    eep = (ee % 2) * 1024
                    vws = []
                    pws = []
                    for sch in range(2):
                        w = []
                        if sch == 0:
                            w += [kvw, QF_RDY]
                            if ee - 2 in gelu_after:
                                w.append(gelu_after[ee - 2])
                            elif ee < 2:
                                w.append((sc, sc.n))
                        PE(lambda e, sch=sch, ksl=ksl, eep=eep: e.matmul(
                            PSA[:, eep + sch * 512:eep + sch * 512 + 512],
                            ksl[:, sch * 128:sch * 128 + 128],
                            QF, start=True, stop=True), wait=w)
                        pws.append((sp, sp.n))
                    for sch in range(2):
                        aew = AEW0 if sch == 0 else AEW1
                        SCE(lambda e, sch=sch, aew=aew, eep=eep: e.activation(
                            out=aew, in_=PSA[:, eep + sch * 512:eep + sch * 512 + 512],
                            func=AF.Gelu), wait=[pws[sch]])
                    gelu_after[ee] = (sc, sc.n)
                    for sch in range(2):
                        aew = AEW0 if sch == 0 else AEW1
                        V(lambda e, aew=aew, ee=ee: e.tensor_mul(aew, aew, gbank(ee)),
                          wait=([gelu_after[ee], gbp[ee]] if sch == 0 else []))
                        vws.append((sv, sv.n))
                    mul_after[ee] = (sv, sv.n)
                    if ee < 15:
                        kv_load(ee + 1, [(sp, sp.n)])
                        grw[ee + 1] = grow(ee + 1, [(sp, sp.n)])
                        wb_ = [grw[ee + 1]]
                        wb_.append(mul_after[ee - 1] if ee - 1 in mul_after else (sv, sv.n))
                        PE(lambda e, ee=ee: e.matmul(gbank(ee + 1), lhs1s[(ee + 1) % 2],
                                                     gb_slot(ee + 1), start=True, stop=True),
                           wait=wb_)
                        gbp[ee + 1] = (sp, sp.n)
                    for sch in range(2):
                        aew = AEW0 if sch == 0 else AEW1
                        PE(lambda e, sch=sch, vsl=vsl, ee=ee, aew=aew: e.matmul(
                            PSC[:, 0:512], vsl[:, sch * 128:sch * 128 + 128], aew,
                            start=(ee == 0 and sch == 0), stop=(ee == 15 and sch == 1)),
                           wait=[vws[sch]] + ([ffw_prev] if (ee == 0 and sch == 0) else []))
                pw = (sp, sp.n)
                V(lambda e, h=h: e.tensor_copy(out=EW[:, 2048 + h * 512:2048 + (h + 1) * 512],
                                               in_=PSC[:, 0:512]), wait=[pw])
            FFW_RDY = (sv, sv.n)

            def wo_psum(tcX, eh):
                k = tcX * 2 + eh
                if k < 4:
                    return PSA[:, k * 512:(k + 1) * 512]
                if k < 6:
                    return PSB[:, (k - 4) * 512:(k - 3) * 512]
                return (PSC if k == 6 else PSD)[:, 0:512]

            def wo_load(cc, gate):
                wb = (cc % 2) * 1024
                sem = sWQa if cc % 2 == 0 else sWQb
                wap = AP(din["woutT"].tensor, cc * 128 * DM, [[DM, 128], [1, 1024]])
                EM("sync", lambda e, s=wap, wb=wb: e.dma_start(out=TBL[:, wb:wb + 1024], in_=s),
                   wait=list(gate), inc=sem, inck=16)
                return (sem, sem.n)

            wo_rdy = {0: wo_load(0, [FFW_RDY, (sp, sp.n), (sv, sv.n)]),
                      1: wo_load(1, [])}
            pe_after = {}
            for cc in range(16):
                wb = (cc % 2) * 1024
                for tcX in range(4):
                    for eh in range(2):
                        if cc < 8:
                            lhs = B[:, cc * 2048 + tok0 + tcX * 128:
                                    cc * 2048 + tok0 + tcX * 128 + 128]
                        else:
                            lhs = EW[:, 2048 + (cc - 8) * 512 + tcX * 128:
                                     2048 + (cc - 8) * 512 + tcX * 128 + 128]
                        w = []
                        if tcX == 0 and eh == 0:
                            w.append(wo_rdy[cc])
                            if cc == 0:
                                w += [(sv, sv.n), (sc, sc.n)]
                        PE(lambda e, lhs=lhs, tcX=tcX, eh=eh, wb=wb, cc=cc: e.matmul(
                            wo_psum(tcX, eh), lhs, TBL[:, wb + eh * 512:wb + eh * 512 + 512],
                            start=(cc == 0), stop=(cc == 15)), wait=w)
                pe_after[cc] = (sp, sp.n)
                if cc + 2 <= 15:
                    wo_rdy[cc + 2] = wo_load(cc + 2, [pe_after[cc]])
            woc = (sp, sp.n)
            X4ACC = EW[:, 2048:6144]
            for tcX in range(4):
                for eh in range(2):
                    V(lambda e, tcX=tcX, eh=eh: e.tensor_copy(
                        out=X4ACC[:, tcX * 1024 + eh * 512:tcX * 1024 + eh * 512 + 512],
                        in_=wo_psum(tcX, eh)),
                      wait=[woc] if (tcX == 0 and eh == 0) else [])
            ffw_prev = (sv, sv.n)

            for tcX in range(4):
                for dc in range(8):
                    src = A[:, dc * 2048 + tok0 + tcX * 128:dc * 2048 + tok0 + tcX * 128 + 128]
                    PE(lambda e, src=src, dc=dc: e.transpose(
                        PSB[:, dc * 128:(dc + 1) * 128], src, CXID[:]),
                       wait=[(sv, sv.n)] if dc == 0 else [])
                pw = (sp, sp.n)
                X4B2 = EW[:, 1024:2048]
                X4C2 = WX[:, 0:1024]
                V(lambda e: e.tensor_copy(out=X4C2, in_=PSB[:, 0:1024]), wait=[pw])
                for eh in range(2):
                    V(lambda e, eh=eh, tcX=tcX: e.tensor_add(
                        X4B2[:, eh * 512:(eh + 1) * 512],
                        X4ACC[:, tcX * 1024 + eh * 512:tcX * 1024 + eh * 512 + 512],
                        X4C2[:, eh * 512:(eh + 1) * 512]))
                V(lambda e: e.reduce_sum(out=MM[:, 20:21], in_=X4B2, axis=AX.X))
                V(lambda e: e.tensor_scalar(out=MM[:, 20:21], in0=MM[:, 20:21],
                                            scalar1=1.0 / DM, scalar2=None, op0=ALU.mult))
                V(lambda e: e.tensor_scalar(out=X4C2, in0=X4B2, scalar1=MM[:, 20:21],
                                            scalar2=None, op0=ALU.subtract))
                vw = (sv, sv.n)
                SCE(lambda e: e.activation(out=X4B2, in_=X4C2, func=AF.Square), wait=[vw])
                cw_ = (sc, sc.n)
                V(lambda e: e.reduce_sum(out=MM[:, 21:22], in_=X4B2, axis=AX.X), wait=[cw_])
                V(lambda e: e.tensor_scalar(out=MM[:, 21:22], in0=MM[:, 21:22],
                                            scalar1=1.0 / DM, scalar2=EPS,
                                            op0=ALU.mult, op1=ALU.add))
                V(lambda e: e.reciprocal(out=MM[:, 21:22], in_=MM[:, 21:22]))
                vw = (sv, sv.n)
                SCE(lambda e: e.activation(out=MM[:, 22:23], in_=MM[:, 21:22], func=AF.Sqrt),
                    wait=[vw])
                cw_ = (sc, sc.n)
                X4S2 = TBL[:, 2048:3072]
                V(lambda e: e.tensor_scalar(out=X4S2, in0=X4C2, scalar1=MM[:, 22:23],
                                            scalar2=None, op0=ALU.mult),
                  wait=[cw_, (st_, n_out_dma * 16)])
                vw = (sv, sv.n)
                oap = AP(out_h.tensor, (tok0 + tcX * 128) * DM, [[DM, 128], [1, 1024]])
                EM("sync", lambda e, s=oap: e.dma_start(out=s, in_=X4S2), wait=[vw],
                   inc=st_, inck=16)
                n_out_dma += 1

        EM("gpsimd", lambda e: None, wait=[(st_, n_out_dma * 16)])
        barrier()


    try:
        _emit_all()
    except _StopEmission:
        pass

    @blk.sync
    def _(sync):
        for fn in prog.streams["sync"]:
            fn(sync)

    @blk.tensor
    def _(tensor):
        for fn in prog.streams["tensor"]:
            fn(tensor)

    @blk.vector
    def _(vector):
        for fn in prog.streams["vector"]:
            fn(vector)

    @blk.scalar
    def _(scalar):
        for fn in prog.streams["scalar"]:
            fn(scalar)

    @blk.gpsimd
    def _(gpsimd):
        for fn in prog.streams["gpsimd"]:
            fn(gpsimd)

    stack.close()
    build_nc.last_prog = prog
    build_nc.last_sems = dict(sd=sd, sg=sg, st=st_, sv=sv, sc=sc, sp=sp, sq=sq, ssy=ssy)
    return nc, sorted(dbg_h.keys())


_CACHE = {}


def kernel(**inputs):
    g = _prep_inputs(inputs)
    consts = _host_consts()
    key = ("nc", tuple(_DBG), _MAX_STAGE)
    if key not in _CACHE:
        _CACHE[key] = build_nc(dbg=tuple(_DBG))
    nc, dbg_names = _CACHE[key]
    in_maps = []
    for c in range(8):
        m = {"x0": np.ascontiguousarray(g["x0"][c])}
        for k in ("wqT", "wkT", "wvT", "woT", "rel", "w1T", "cw", "cb", "w2T",
                  "dec", "Wt", "b1r", "gamr", "betr", "winT", "woutT", "kfT", "vf"):
            m[k] = g[k]
        for k, v in consts.items():
            m[k] = v
        in_maps.append(m)
    res = run_bass_kernel_spmd(nc, in_maps, core_ids=list(range(8)))
    out = np.stack([np.asarray(res.results[c]["out"]) for c in range(8)], axis=0)
    if _DBG:
        kernel._dbg_res = res
    return out


# version 14
# speedup vs baseline: 1.0052x; 1.0052x over previous
import numpy as np
import concourse.bass as bass
import concourse.mybir as mybir
from concourse.bass_utils import run_bass_kernel_spmd
from concourse.ap import AP
from contextlib import ExitStack

F32 = mybir.dt.float32
BF16 = mybir.dt.bfloat16
I32 = mybir.dt.int32
AF = mybir.ActivationFunctionType
ALU = mybir.AluOpType
AX = mybir.AxisListType

T = 2048
DM = 1024
NEG = -1.0e18
TS3 = 256
_DBG = []
_MAX_STAGE = 99

LN1E4 = np.float64(np.log(np.float64(10000.0)))
LNHI = float(np.float32(LN1E4))
LNLO = float(np.float32(LN1E4 - np.float64(np.float32(LN1E4))))
PIHI = float(np.float32(np.pi))
PILO = float(np.float32(np.float64(np.pi) - np.float64(np.float32(np.pi))))
INVPI = float(np.float32(1.0 / np.pi))


_TRIG = {}


def _trig_tables():
    if _TRIG:
        return _TRIG
    try:
        import jax
        jax.config.update("jax_enable_x64", True)
        import jax.numpy as jnp
        with jax.default_device(jax.devices("cpu")[0]):
            inv0 = 1.0 / 10000.0 ** (jnp.arange(0, 64, 2, dtype=jnp.float32) / 64)
            t = jnp.arange(T, dtype=jnp.float32)
            ang0 = jnp.outer(t, inv0)
            c0, s0 = np.asarray(jnp.cos(ang0)), np.asarray(jnp.sin(ang0))
            inv1 = 1.0 / 10000.0 ** (jnp.arange(64, dtype=jnp.float32) / 64)
            ang1 = jnp.arange(T, dtype=jnp.float32)[:, None] * inv1
            c1, s1 = np.asarray(jnp.cos(ang1)), np.asarray(jnp.sin(ang1))
    except Exception:
        inv0 = (1.0 / 10000.0 ** (np.arange(0, 64, 2, dtype=np.float32) / np.float32(64))).astype(np.float32)
        ang0 = np.outer(np.arange(T, dtype=np.float32), inv0).astype(np.float32)
        c0, s0 = np.cos(ang0), np.sin(ang0)
        inv1 = (1.0 / 10000.0 ** (np.arange(64, dtype=np.float32) / np.float32(64))).astype(np.float32)
        ang1 = (np.arange(T, dtype=np.float32)[:, None] * inv1).astype(np.float32)
        c1, s1 = np.cos(ang1), np.sin(ang1)
    _TRIG.update(c0=c0.astype(np.float32), s0=s0.astype(np.float32),
                 c1=c1.astype(np.float32), s1=s1.astype(np.float32))
    return _TRIG


def _host_consts():
    c = {}
    p = np.arange(128)
    pm = np.zeros((128, 128), dtype=np.float32); pm[p, p ^ 1] = 1.0
    c["c_permpair"] = pm
    ph = np.zeros((128, 128), dtype=np.float32); ph[p, (p + 64) % 128] = 1.0
    c["c_permhalf"] = ph
    c["c_ones_r"] = np.ones((1, 128), dtype=np.float32)
    c["c_ones_c"] = np.ones((128, 1), dtype=np.float32)
    c["c_iota_c"] = p.astype(np.float32)[:, None]
    c["c_rev_c"] = (127 - p).astype(np.float32)[:, None]
    c["c_i_l0"] = ((p % 64) >> 1).astype(np.float32)[:, None]
    c["c_sign_l0"] = np.where(p % 2 == 0, -1.0, 1.0).astype(np.float32)[:, None]
    c["c_j_l1"] = (p % 64).astype(np.float32)[:, None]
    c["c_sign_l1"] = np.where(p < 64, -1.0, 1.0).astype(np.float32)[:, None]
    tg = _trig_tables()
    i0 = ((p % 64) >> 1)
    sg0 = np.where(p % 2 == 0, -1.0, 1.0).astype(np.float32)[:, None]
    c["c_cosl0"] = np.ascontiguousarray(tg["c0"][:, i0].T)
    c["c_ssinl0"] = np.ascontiguousarray(tg["s0"][:, i0].T * sg0)
    j1 = (p % 64)
    sg1 = np.where(p < 64, -1.0, 1.0).astype(np.float32)[:, None]
    c["c_cosl1"] = np.ascontiguousarray(tg["c1"][:, j1].T)
    c["c_ssinl1"] = np.ascontiguousarray(tg["s1"][:, j1].T * sg1)
    return c


def _prep_inputs(inp):
    g = {}
    g["x0"] = np.ascontiguousarray(np.transpose(np.asarray(inp["L0_attn_x"]), (0, 2, 1)))
    for nm, key in (("wqT", "L0_attn_w_q"), ("wkT", "L0_attn_w_k"),
                    ("wvT", "L0_attn_w_v"), ("woT", "L0_attn_w_o")):
        g[nm] = np.ascontiguousarray(np.asarray(inp[key]).T)
    g["rel"] = np.ascontiguousarray(np.asarray(inp["L0_attn_rel_emb"]))
    g["w1T"] = np.ascontiguousarray(np.asarray(inp["L0_ffn_w1"]).T)
    g["cw"] = np.ascontiguousarray(np.asarray(inp["L0_ffn_conv_w"])[:, 0, :])
    g["cb"] = np.ascontiguousarray(np.asarray(inp["L0_ffn_conv_b"])[:, None])
    g["w2T"] = np.ascontiguousarray(np.asarray(inp["L0_ffn_w2"]).T)
    g["dec"] = np.asarray(inp["L1_attn_decay"], dtype=np.float32).reshape(1, 1)
    g["Wt"] = np.ascontiguousarray(np.asarray(inp["L1_attn_W"]).T)
    g["b1r"] = np.ascontiguousarray(np.asarray(inp["L1_attn_b"])[None, :].astype(np.float32))
    g["gamr"] = np.ascontiguousarray(np.asarray(inp["L1_attn_gamma"])[None, :].astype(np.float32))
    g["betr"] = np.ascontiguousarray(np.asarray(inp["L1_attn_beta"])[None, :].astype(np.float32))
    g["winT"] = np.ascontiguousarray(np.asarray(inp["L1_ffn_w_in"]).T)
    g["woutT"] = np.ascontiguousarray(np.asarray(inp["L1_ffn_w_out"]).T)
    g["kfT"] = np.ascontiguousarray(np.transpose(np.asarray(inp["L1_ffn_k_ffwd"]), (0, 1, 3, 2)))
    g["vf"] = np.ascontiguousarray(np.asarray(inp["L1_ffn_v_ffwd"]))
    return g


class _StopEmission(Exception):
    pass


class Sem:
    def __init__(self, h, owner=None):
        self.h = h
        self.n = 0
        self.owner = owner
        self.orders = []


class Prog:
    ENGS = ("sync", "tensor", "vector", "scalar", "gpsimd")

    def __init__(self):
        self.streams = {k: [] for k in self.ENGS}

    def __init2__(self):
        pass

    def set_progress(self, mapping):
        self.progress = mapping

    def emit(self, eng, fn, wait=(), inc=None, inck=1):
        wait = [(s, s.n if s.owner is not None else v) for (s, v) in wait]
        wait = [(s, v) for (s, v) in wait if v > 0]
        if inc is not None and inc.owner is not None:
            wait = wait + inc.orders
            inc.orders = []
        wait = list({id(s): (s, v) for s, v in sorted(wait, key=lambda t: t[1])}.values())
        for s, v in wait:
            assert v <= s.n, f"wait on future sem value {v} > {s.n} on {eng}"
        inc_pre = None
        if inc is not None:
            inc_pre = inc.n
            inc.n += inck
        foreign = [s for s, v in wait if s.owner is not None and s.owner != eng]
        extra_inc = None
        if foreign:
            ps = self.progress[eng]
            if inc is ps:
                proofv = ps.n
            else:
                ps.n += 1
                proofv = ps.n
                extra_inc = ps
            for s in foreign:
                s.orders.append((ps, proofv))

        extra_pre = None if extra_inc is None else proofv - 1

        def closure(e, fn=fn, wait=tuple(wait), inc=inc, inck=inck,
                    extra_inc=extra_inc, inc_pre=inc_pre, extra_pre=extra_pre):
            for s, v in wait:
                e.wait_ge(s.h, v)
            r = fn(e)
            if inc is not None:
                if r is None:
                    e.wait_ge(inc.h, inc_pre)
                    e.sem_inc(inc.h, inck)
                else:
                    r.then_inc(inc.h, inck)
            if extra_inc is not None:
                e.wait_ge(extra_inc.h, extra_pre)
                e.sem_inc(extra_inc.h, 1)

        self.streams[eng].append(closure)
        if not hasattr(self, "log"):
            self.log = []
        import traceback
        tb = traceback.extract_stack(limit=5)
        loc = ";".join(f"{f.lineno}" for f in tb[:-1])
        self.log.append((eng, getattr(inc, "h", None) and id(inc), inck,
                         [(id(s), v) for s, v in wait],
                         extra_inc is not None and id(extra_inc), loc))
        return None if inc is None else inc.n


def build_nc(dbg=()):
    nc = bass.Bass("TRN2", target_bir_lowering=False)
    consts = _host_consts()

    din = {}
    def dt_in(name, shape, dtype=F32):
        din[name] = nc.dram_tensor(name, list(shape), dtype, kind="ExternalInput").ap()

    dt_in("x0", (DM, T))
    for nm in ("wqT", "wkT", "wvT", "woT"):
        dt_in(nm, (DM, DM))
    dt_in("rel", (65, 64))
    dt_in("w1T", (DM, 4096)); dt_in("cw", (4096, 3)); dt_in("cb", (4096, 1)); dt_in("w2T", (4096, DM))
    dt_in("dec", (1, 1)); dt_in("Wt", (DM, DM)); dt_in("b1r", (1, DM)); dt_in("gamr", (1, DM)); dt_in("betr", (1, DM))
    dt_in("winT", (DM, 4112)); dt_in("woutT", (2 * DM, DM))
    dt_in("kfT", (16, 8, 128, 256)); dt_in("vf", (16, 8, 256, 128))
    for k, v in consts.items():
        dt_in(k, v.shape)

    out_h = nc.dram_tensor("out", [T, DM], F32, kind="ExternalOutput").ap()
    dbg_h = {}
    def dbgt(name, rows, cols):
        if name in dbg and name not in dbg_h:
            dbg_h[name] = nc.dram_tensor("dbg_" + name, [rows, cols], F32,
                                         kind="ExternalOutput").ap()
        return dbg_h.get(name)

    hb_band = nc.dram_tensor("hb_band", [128 * 160 + 64], F32)
    EXPP = 128 * 160
    hb_exp = nc.dram_tensor("hb_exp", [16 * EXPP], F32)
    hb_LT = nc.dram_tensor("hb_LT", [128 * 130 + 64], F32)
    hb_w1q = nc.dram_tensor("hb_w1q", [DM * 4096], BF16)
    hb_w2q = nc.dram_tensor("hb_w2q", [4096 * DM], BF16)
    hb_xq = nc.dram_tensor("hb_xq", [DM * T], BF16)

    prog = Prog()
    EM = prog.emit
    stack = ExitStack()
    en = stack.enter_context

    A = en(nc.sbuf_tensor("arA", [128, 16448], F32))
    B = en(nc.sbuf_tensor("arB", [128, 16384], F32))
    Dq = en(nc.sbuf_tensor("arD", [128, 6160], BF16))
    EW = en(nc.sbuf_tensor("arE", [128, 6144], F32))
    TBL = en(nc.sbuf_tensor("arT", [128, 4096], F32))
    WW = en(nc.sbuf_tensor("arW", [128, 3072], F32))
    WX = en(nc.sbuf_tensor("arX", [128, 1536], F32))
    MM = en(nc.sbuf_tensor("arM", [128, 64], F32))
    MI = en(nc.sbuf_tensor("arMi", [128, 512], I32))
    MB = en(nc.sbuf_tensor("arMb", [128, 128], BF16))
    CXID = en(nc.sbuf_tensor("c_idm", [128, 128], F32))
    CXM160 = en(nc.sbuf_tensor("c_m160", [128, 160], F32))
    CXTRI = en(nc.sbuf_tensor("c_trim", [128, 128], F32))
    RELS = en(nc.sbuf_tensor("rels", [128, 164], F32))
    BIGC = ("c_cosl0", "c_ssinl0", "c_cosl1", "c_ssinl1")
    CX = {k: en(nc.sbuf_tensor("s" + k, list(v.shape), F32))
          for k, v in consts.items() if k not in BIGC}

    PSA = en(nc.psum_tensor("psA", [128, 2048], F32))
    PSB = en(nc.psum_tensor("psB", [128, 1024], F32))
    PSC = en(nc.psum_tensor("psC", [128, 512], F32))
    PSD = en(nc.psum_tensor("psD", [128, 512], F32))

    sd = Sem(en(nc.semaphore("sd")), owner="sync")
    sg = Sem(en(nc.semaphore("sg")), owner="gpsimd")
    st_ = Sem(en(nc.semaphore("st")), owner="sync")
    sv = Sem(en(nc.semaphore("sv")))
    sc = Sem(en(nc.semaphore("sc")))
    sp = Sem(en(nc.semaphore("sp")))
    sq = Sem(en(nc.semaphore("sq")))
    ssy = Sem(en(nc.semaphore("ssy")))
    sW1 = Sem(en(nc.semaphore("sW1")), owner="sync")
    sX0a = Sem(en(nc.semaphore("sX0a")), owner="sync")
    sX0b = Sem(en(nc.semaphore("sX0b")), owner="sync")
    sKVa = Sem(en(nc.semaphore("sKVa")), owner="sync")
    sKVb = Sem(en(nc.semaphore("sKVb")), owner="sync")
    sWQa = Sem(en(nc.semaphore("sWQa")), owner="sync")
    sWQb = Sem(en(nc.semaphore("sWQb")), owner="sync")
    sWTa = Sem(en(nc.semaphore("sWTa")), owner="sync")
    sWTb = Sem(en(nc.semaphore("sWTb")), owner="sync")
    sWGa = Sem(en(nc.semaphore("sWGa")), owner="gpsimd")
    sWGb = Sem(en(nc.semaphore("sWGb")), owner="gpsimd")
    swq = Sem(en(nc.semaphore("swq")), owner="sync")
    swqw = Sem(en(nc.semaphore("swqw")), owner="sync")
    prog.set_progress({"sync": ssy, "tensor": sp, "vector": sv,
                       "scalar": sc, "gpsimd": sq})

    blk = en(nc.Block())

    def ckpt(n):
        if _MAX_STAGE <= n:
            raise _StopEmission()

    def barrier():
        snap = []
        for eng, s in (("tensor", sp), ("vector", sv), ("scalar", sc), ("gpsimd", sq)):
            EM(eng, lambda e: None, inc=s)
            snap.append((s, s.n))
        dsnap = [(sd, sd.n), (sg, sg.n)]
        for eng in Prog.ENGS:
            EM(eng, lambda e: None, wait=list(snap) + dsnap)

    def V(fn, wait=()):
        return EM("vector", fn, wait=list(wait) + [(sv, sv.n)], inc=sv)

    def SCE(fn, wait=()):
        return EM("scalar", fn, wait=list(wait) + [(sc, sc.n)], inc=sc)

    def PE(fn, wait=()):
        return EM("tensor", fn, wait=wait, inc=sp)

    def SY(fn, wait=()):
        return EM("sync", fn, wait=wait, inc=sd, inck=16)

    def GD(fn, wait=()):
        return EM("gpsimd", fn, wait=list(wait) + [(sq, sq.n)], inc=sg, inck=16)

    def GP(fn, wait=()):
        return EM("gpsimd", fn, wait=list(wait) + [(sq, sq.n)], inc=sq)

    def _emit_all():
        for k in consts:
            if k in BIGC:
                continue
            SY(lambda e, d=CX[k], s=din[k]: e.dma_start(out=d[:], in_=s))
        W_CONST = (sd, sd.n)
        ONES_R = CX["c_ones_r"][:]
        ONES_C = CX["c_ones_c"][:]

        GP(lambda e: e.memset(CXID[:], 1.0))
        GP(lambda e: e.affine_select(out=CXID[:], in_=CXID[:], pattern=[[-1, 128]],
                                     compare_op=ALU.is_ge, fill=0.0, base=0, channel_multiplier=1))
        GP(lambda e: e.affine_select(out=CXID[:], in_=CXID[:], pattern=[[1, 128]],
                                     compare_op=ALU.is_ge, fill=0.0, base=0, channel_multiplier=-1))
        GP(lambda e: e.memset(CXM160[:], 0.0))
        GP(lambda e: e.affine_select(out=CXM160[:], in_=CXM160[:], pattern=[[-1, 160]],
                                     compare_op=ALU.is_ge, fill=NEG, base=32, channel_multiplier=1))
        GP(lambda e: e.memset(CXTRI[:], 1.0))
        GP(lambda e: e.affine_select(out=CXTRI[:], in_=CXTRI[:], pattern=[[-1, 128]],
                                     compare_op=ALU.is_ge, fill=0.0, base=127, channel_multiplier=-1))
        TROW = B[:, 14336:16384]
        GP(lambda e: e.iota(TROW, pattern=[[1, 2048]], base=0, channel_multiplier=0,
                            allow_small_or_imprecise_dtypes=True))
        DEV_CONST = (sq, sq.n)
        IDBF = MB[:, 0:128]
        V(lambda e: e.tensor_copy(out=IDBF, in_=CXID[:]), wait=[DEV_CONST])

        ZT = B[:, 0:160]
        V(lambda e: e.memset(ZT, 0.0))
        ZT_RDY = (sv, sv.n)
        SY(lambda e: e.dma_start(out=AP(hb_band, 0, [[160, 128], [1, 160]]), in_=ZT), wait=[ZT_RDY])
        SY(lambda e: e.dma_start(out=AP(hb_LT, 0, [[130, 128], [1, 130]]), in_=B[:, 0:130]), wait=[ZT_RDY])
        SY(lambda e: e.dma_start(out=AP(hb_exp, 0, [[160, 128], [1, 32]]), in_=B[:, 0:32]), wait=[ZT_RDY])
        HBZERO = (sd, sd.n)

        def round_cols(y, a, b, pre=()):
            n = y.shape[1]
            ii = MI[:, 0:n]
            GD(lambda e, y=y, ii=ii: e.dma_start(out=ii, in_=y),
               wait=[(sv, sv.n), (sg, sg.n)] + list(pre))
            GD(lambda e, a=a, ii=ii: e.dma_start(out=a, in_=ii), wait=[(sg, sg.n)])
            dw = (sg, sg.n)
            V(lambda e, y=y, a=a, b=b: e.tensor_sub(b, y, a), wait=[dw])
            V(lambda e, y=y, b=b: e.tensor_scalar(out=y, in0=b, scalar1=0.5, scalar2=None, op0=ALU.is_ge))
            V(lambda e, y=y, a=a: e.tensor_add(y, y, a))
            V(lambda e, a=a, b=b: e.tensor_scalar(out=a, in0=b, scalar1=-0.5, scalar2=None, op0=ALU.is_le))
            V(lambda e, y=y, a=a: e.tensor_sub(y, y, a))
            return (sv, sv.n)

        def round_tile(y, a, b):
            n = y.shape[1]
            done = None
            for o in range(0, n, 512):
                w = min(512, n - o)
                done = round_cols(y[:, o:o + w], a[:, o:o + w], b[:, o:o + w])
            return done

        def round_to(dst, src, a, b):
            n = src.shape[1]
            for o in range(0, n, 512):
                w = min(512, n - o)
                sl = slice(o, o + w)
                ii = MI[:, 0:w]
                GD(lambda e, y=src[:, sl], ii=ii: e.dma_start(out=ii, in_=y),
                   wait=[(sv, sv.n), (sg, sg.n)])
                GD(lambda e, a2=a[:, sl], ii=ii: e.dma_start(out=a2, in_=ii), wait=[(sg, sg.n)])
                dw = (sg, sg.n)
                V(lambda e, sl=sl: e.tensor_sub(b[:, sl], src[:, sl], a[:, sl]), wait=[dw])
                V(lambda e, sl=sl: e.tensor_scalar(out=dst[:, sl], in0=b[:, sl], scalar1=0.5,
                                                   scalar2=None, op0=ALU.is_ge))
                V(lambda e, sl=sl: e.tensor_add(dst[:, sl], dst[:, sl], a[:, sl]))
                V(lambda e, sl=sl: e.tensor_scalar(out=a[:, sl], in0=b[:, sl], scalar1=-0.5,
                                                   scalar2=None, op0=ALU.is_le))
                V(lambda e, sl=sl: e.tensor_sub(dst[:, sl], dst[:, sl], a[:, sl]))
            return (sv, sv.n)

        W0 = B[:, 0:2048]; W1 = B[:, 2048:4096]; W2 = B[:, 4096:6144]
        W3 = B[:, 6144:8192]; W4 = B[:, 8192:10240]
        C_L0 = TBL[:, 0:2048]; S_L0 = TBL[:, 2048:4096]
        FCOL = MM[:, 0:1]

        def emit_freq_col(idx_col, denom):
            t1 = MM[:, 1:2]; t2 = MM[:, 2:3]
            V(lambda e: e.tensor_scalar(out=t1, in0=idx_col, scalar1=1.0 / denom,
                                        scalar2=-LNHI, op0=ALU.mult, op1=ALU.mult), wait=[W_CONST])
            V(lambda e: e.tensor_scalar(out=t2, in0=idx_col, scalar1=1.0 / denom,
                                        scalar2=-LNLO, op0=ALU.mult, op1=ALU.mult))
            V(lambda e: e.tensor_add(t1, t1, t2))
            vw = (sv, sv.n)
            SCE(lambda e: e.activation(out=FCOL, in_=t1, func=AF.Exp), wait=[vw])
            return (sc, sc.n)

        def emit_sin(ang, dst, kb, tmp, w4):
            V(lambda e: e.tensor_scalar(out=kb, in0=ang, scalar1=INVPI, scalar2=None, op0=ALU.mult))
            round_tile(kb, dst, tmp)
            V(lambda e: e.scalar_tensor_tensor(out=tmp, in0=kb, scalar=-PIHI, in1=ang,
                                               op0=ALU.mult, op1=ALU.add))
            V(lambda e: e.scalar_tensor_tensor(out=tmp, in0=kb, scalar=-PILO, in1=tmp,
                                               op0=ALU.mult, op1=ALU.add))
            V(lambda e: e.tensor_scalar(out=ang, in0=kb, scalar1=0.5, scalar2=None, op0=ALU.mult))
            round_to(dst, ang, kb, w4)
            V(lambda e: e.tensor_sub(ang, ang, dst))
            V(lambda e: e.tensor_mul(ang, ang, ang))
            V(lambda e: e.tensor_scalar(out=ang, in0=ang, scalar1=-8.0, scalar2=1.0,
                                        op0=ALU.mult, op1=ALU.add))
            vw = (sv, sv.n)
            SCE(lambda e: e.activation(out=dst, in_=tmp, func=AF.Sin), wait=[vw])
            cw_ = (sc, sc.n)
            V(lambda e: e.tensor_mul(dst, dst, ang), wait=[cw_])
            return (sv, sv.n)

        def emit_tables(idx_col, denom, c_out, s_out, sign_col):
            emit_freq_col(idx_col, denom)
            fr = (sc, sc.n)
            V(lambda e: e.tensor_scalar(out=W0, in0=TROW, scalar1=FCOL, scalar2=None,
                                        op0=ALU.mult), wait=[fr, (sq, sq.n), HBZERO])
            V(lambda e: e.tensor_scalar(out=W1, in0=W0, scalar1=PIHI / 2, scalar2=PILO / 2,
                                        op0=ALU.add, op1=ALU.add))
            emit_sin(W1, c_out, W2, W3, W4)
            emit_sin(W0, s_out, W1, W2, W3)
            V(lambda e: e.tensor_scalar(out=s_out, in0=s_out, scalar1=sign_col,
                                        scalar2=None, op0=ALU.mult))
            return (sv, sv.n)

        SY(lambda e: e.dma_start(out=C_L0, in_=din["c_cosl0"]))
        SY(lambda e: e.dma_start(out=S_L0, in_=din["c_ssinl0"]))

        RELDT2 = RELS[:, 0:32]
        REL0C = RELS[:, 32:33]
        RELDIF = RELS[0:32, 33:97]
        SY(lambda e: e.dma_start(out=RELS[0:32, 33:97], in_=din["rel"][1:33, :]))
        SY(lambda e: e.dma_start(out=RELS[0:64, 32:33],
                                 in_=AP(din["rel"].tensor, 0, [[1, 64], [1, 1]])))
        SY(lambda e: e.dma_start(out=RELS[64:128, 32:33],
                                 in_=AP(din["rel"].tensor, 0, [[1, 64], [1, 1]])))
        SY(lambda e: e.dma_start(out=RELS[0:1, 97:161],
                                 in_=AP(din["rel"].tensor, 0, [[64, 1], [1, 64]])))
        RELL = (sd, sd.n)
        PE(lambda e: e.matmul(PSD[0:32, 0:64], ONES_R[0:1, 0:32], RELS[0:1, 97:161],
                              start=True, stop=True), wait=[RELL, W_CONST])
        pw = (sp, sp.n)
        V(lambda e: e.tensor_sub(RELDIF, RELS[0:32, 33:97], PSD[0:32, 0:64]), wait=[pw])
        V(lambda e: e.tensor_scalar(out=RELS[0:32, 97:161], in0=RELDIF, scalar1=0.125,
                                    scalar2=None, op0=ALU.mult))
        vw2 = (sv, sv.n)
        PE(lambda e: e.transpose(PSD[0:64, 64:96], RELS[0:32, 97:161], CXID[0:32, 0:32]),
           wait=[vw2, DEV_CONST])
        pw2 = (sp, sp.n)
        V(lambda e: e.tensor_copy(out=RELDT2[0:64, :], in_=PSD[0:64, 64:96]), wait=[pw2])
        vw3 = (sv, sv.n)
        GD(lambda e: e.dma_start(out=RELDT2[64:128, :], in_=RELDT2[0:64, :]), wait=[vw3])

        barrier()
        ckpt(0)

        W2MCC = MM[:, 16:17]; W1MCC = MM[:, 17:18]; SW1C = MM[:, 18:19]; SW2C = MM[:, 19:20]
        WQSS = (EW[:, 3072:4096], EW[:, 4096:5120])
        WQT = EW[:, 5120:6144]
        ACC1 = MM[:, 20:21]; ACC2 = MM[:, 24:25]; WQTMP = MM[:, 25:26]
        wq_specs = []
        for cc in range(8):
            for h in range(0, 4096, 1024):
                wq_specs.append(("w1T", hb_w1q, cc * 128 * 4096 + h, ACC1, SW1C))
        for cc in range(32):
            wq_specs.append(("w2T", hb_w2q, cc * 128 * 1024, ACC2, SW2C))
        wq = {"mi": 0, "mr": 0, "qi": 0, "qp": 0, "mid": False,
              "mrdy": None, "dn": {}, "wn": {}}

        def wq_mean_dma(i):
            nm, _, off, _, _ = wq_specs[i]
            ap = AP(din[nm].tensor, off, [[4096 if nm == "w1T" else 1024, 128], [1, 1024]])
            EM("sync", lambda e, s=ap, d=WQSS[i % 2]: e.dma_start(out=d, in_=s),
               wait=[(sv, sv.n)], inc=swq, inck=16)
            wq["dn"][i] = (swq, swq.n)

        def wq_mean_red(i):
            acc = wq_specs[i][3]
            if i in (0, 32):
                V(lambda e, acc=acc, i=i: e.reduce_sum(out=acc, in_=WQSS[i % 2], axis=AX.X,
                                                       apply_absolute_value=True),
                  wait=[wq["dn"][i]])
            else:
                V(lambda e, i=i: e.reduce_sum(out=WQTMP, in_=WQSS[i % 2], axis=AX.X,
                                              apply_absolute_value=True), wait=[wq["dn"][i]])
                V(lambda e, acc=acc: e.tensor_add(acc, acc, WQTMP))

        def wq_mid():
            for k, (acc, mc, sw, n) in enumerate((
                    (ACC1, W1MCC, SW1C, 8 * 128 * 4096), (ACC2, W2MCC, SW2C, 32 * 128 * 1024))):
                PE(lambda e, acc=acc: e.matmul(PSD[0:1, 0:1], ONES_C, acc,
                                               start=True, stop=True),
                   wait=[(sv, sv.n)] if k == 0 else [(sv, sv.n)])
                pw = (sp, sp.n)
                V(lambda e, n=n: e.tensor_scalar(out=MM[0:1, 26:27], in0=PSD[0:1, 0:1],
                                                 scalar1=1.0 / n, scalar2=1e-5,
                                                 op0=ALU.mult, op1=ALU.max), wait=[pw])
                V(lambda e: e.reciprocal(out=MM[0:1, 27:28], in_=MM[0:1, 26:27]))
                vw = (sv, sv.n)
                PE(lambda e: e.matmul(PSD[:, 0:1], ONES_R, MM[0:1, 26:27],
                                      start=True, stop=True), wait=[vw])
                PE(lambda e: e.matmul(PSD[:, 1:2], ONES_R, MM[0:1, 27:28],
                                      start=True, stop=True))
                pw = (sp, sp.n)
                V(lambda e, mc=mc: e.tensor_copy(out=mc, in_=PSD[:, 0:1]), wait=[pw])
                V(lambda e, sw=sw: e.tensor_copy(out=sw, in_=PSD[:, 1:2]))
            wq["mrdy"] = (sv, sv.n)
            wq["mid"] = True

        def wq_quant_dma(i):
            nm, _, off, _, _ = wq_specs[i]
            ap = AP(din[nm].tensor, off, [[4096 if nm == "w1T" else 1024, 128], [1, 1024]])
            EM("sync", lambda e, s=ap, d=WQSS[i % 2]: e.dma_start(out=d, in_=s),
               wait=[(sv, sv.n)], inc=swq, inck=16)
            wq["dn"][64 + i] = (swq, swq.n)

        def wq_quant_proc(i):
            nm, hbq, off, _, sw = wq_specs[i]
            bu = WQSS[i % 2]
            w0 = [wq["dn"][64 + i]] + ([wq["mrdy"]] if i == 0 else [])
            V(lambda e, bu=bu, sw=sw: e.tensor_scalar(out=WQT, in0=bu, scalar1=sw,
                                                      scalar2=0.5, op0=ALU.mult,
                                                      op1=ALU.is_ge), wait=w0)
            V(lambda e, bu=bu, sw=sw: e.tensor_scalar(out=bu, in0=bu, scalar1=sw,
                                                      scalar2=-0.5, op0=ALU.mult,
                                                      op1=ALU.is_le))
            V(lambda e, bu=bu: e.tensor_sub(Dq[:, 0:1024], WQT, bu),
              wait=[wq["wn"][i - 1]] if i - 1 in wq["wn"] else [])
            vw = (sv, sv.n)
            oap = AP(hbq, off, [[4096 if nm == "w1T" else 1024, 128], [1, 1024]])
            EM("sync", lambda e, o=oap: e.dma_start(out=o, in_=Dq[:, 0:1024]),
               wait=[vw], inc=swqw, inck=16)
            wq["wn"][i] = (swqw, swqw.n)

        def wq_slot():
            if wq["mi"] < 64:
                if wq["mi"] >= 2:
                    wq_mean_red(wq["mr"]); wq["mr"] += 1
                wq_mean_dma(wq["mi"]); wq["mi"] += 1
            elif wq["mid"] and wq["qi"] < 64:
                if wq["qi"] >= 2:
                    wq_quant_proc(wq["qp"]); wq["qp"] += 1
                wq_quant_dma(wq["qi"]); wq["qi"] += 1

        def wq_flush():
            while wq["mr"] < wq["mi"]:
                wq_mean_red(wq["mr"]); wq["mr"] += 1
            while wq["qp"] < wq["qi"]:
                wq_quant_proc(wq["qp"]); wq["qp"] += 1

        KF = B[:, 0:2048]
        SCR = B[:, 2048:4096]
        VP = B[:, 4096:6144]
        CTX = B[:, 6144:8192]
        QRAW = EW[:, 0:2048]
        MS1 = EW[:, 2048:3072]
        X0BS = (B[:, 8192:12288], B[:, 12288:16384])
        X0SEM = (sX0a, sX0b)
        RELOUT = MM[:, 8:16]

        def load_pair_weights(hp, pe_gate):
            for wi, nm in enumerate(("wkT", "wvT", "wqT")):
                ap = AP(din[nm].tensor, hp * 128,
                        [[DM, 128], [128 * DM, 8], [1, 128]])
                EM("sync", lambda e, d=WW[:, wi * 1024:(wi + 1) * 1024], s=ap:
                   e.dma_start(out=d, in_=s), wait=[pe_gate], inc=sW1, inck=16)

        def load_wo(hp, pe_gate):
            ap = AP(din["woT"].tensor, hp * 128 * DM, [[DM, 128], [1, 1024]])
            EM("sync", lambda e, s=ap: e.dma_start(out=WX[:, 0:1024], in_=s),
               wait=[pe_gate], inc=sW1, inck=16)

        def load_x0_block(tbb, pe_gate):
            buf = X0BS[tbb % 2]
            sem = X0SEM[tbb % 2]
            ap = AP(din["x0"].tensor, tbb * 512, [[T, 128], [128 * T, 8], [1, 512]])
            EM("sync", lambda e, s=ap, buf=buf: e.dma_start(out=buf, in_=s),
               wait=[pe_gate], inc=sem, inck=16)

        def s1_pair(hp):
            load_wo(hp, (sp, sp.n))
            WRD = (sW1, sW1.n)
            evK = (sc, sc.n)
            evVQ = (sv, sv.n)
            for tbb in range(4):
                tsl = slice(tbb * 512, (tbb + 1) * 512)
                X0B = X0BS[tbb % 2]
                xw = (X0SEM[tbb % 2], X0SEM[tbb % 2].n)
                if tbb < 3:
                    load_x0_block(tbb + 1, (sp, sp.n))
                for cc in range(8):
                    PE(lambda e, cc=cc, tbb=tbb, X0B=X0B: e.matmul(
                        PSA[:, tbb * 512:(tbb + 1) * 512],
                        WW[:, cc * 128:cc * 128 + 128],
                        X0B[:, cc * 512:(cc + 1) * 512],
                        start=(cc == 0), stop=(cc == 7)),
                       wait=([xw, WRD, evK] if cc == 0 else []))
                for qt4 in range(1):
                    for cc in range(8):
                        PE(lambda e, cc=cc, X0B=X0B: e.matmul(
                            PSC[:, 0:512],
                            WW[:, 2048 + cc * 128:2048 + cc * 128 + 128],
                            X0B[:, cc * 512:(cc + 1) * 512],
                            start=(cc == 0), stop=(cc == 7)),
                           wait=([evVQ] if cc == 0 else []))
                    pv = (sp, sp.n)
                    V(lambda e, tbb=tbb, tsl=tsl: e.tensor_copy(out=QRAW[:, tsl],
                                                                in_=PSC[:, 0:512]),
                      wait=[pv])
                    evVQ = (sv, sv.n)
                for tc4 in range(4):
                    po = (tc4 % 2) * 512
                    for cc in range(8):
                        PE(lambda e, cc=cc, tc4=tc4, po=po, X0B=X0B: e.matmul(
                            PSB[:, po:po + 128],
                            X0B[:, cc * 512 + tc4 * 128:cc * 512 + tc4 * 128 + 128],
                            WW[:, 1024 + cc * 128:1024 + cc * 128 + 128],
                            start=(cc == 0), stop=(cc == 7)),
                           wait=([evVQ] if cc == 0 else []))
                    pv = (sp, sp.n)
                    V(lambda e, tc4=tc4, tbb=tbb, po=po: e.tensor_copy(
                        out=VP[:, tbb * 512 + tc4 * 128:tbb * 512 + tc4 * 128 + 128],
                        in_=PSB[:, po:po + 128]), wait=[pv])
                    evVQ = (sv, sv.n)
                pe_done = (sp, sp.n)
                SCE(lambda e, tbb=tbb, tsl=tsl: e.copy(out=SCR[:, tsl],
                                                       in_=PSA[:, tbb * 512:(tbb + 1) * 512]),
                    wait=[pe_done])
                evK = (sc, sc.n)
            if hp < 7:
                load_pair_weights(hp + 1, (sp, sp.n))
                load_x0_block(0, (sp, sp.n))

            for dc in range(8):
                PE(lambda e, dc=dc: e.matmul(PSD[:, 200:201], WX[:, dc * 128:dc * 128 + 128],
                                             REL0C, start=True, stop=True),
                   wait=[WRD, (sv, sv.n)])
                pv = (sp, sp.n)
                if hp == 0:
                    V(lambda e, dc=dc: e.tensor_copy(out=RELOUT[:, dc:dc + 1],
                                                     in_=PSD[:, 200:201]), wait=[pv])
                else:
                    V(lambda e, dc=dc: e.tensor_add(RELOUT[:, dc:dc + 1],
                                                    RELOUT[:, dc:dc + 1], PSD[:, 200:201]),
                      wait=[pv])

            kevac = (sc, sc.n)
            for nb in range(4):
                PE(lambda e, nb=nb: e.matmul(PSA[:, nb * 512:(nb + 1) * 512],
                                             CX["c_permpair"][:],
                                             SCR[:, nb * 512:(nb + 1) * 512],
                                             start=True, stop=True),
                   wait=[kevac, (sv, sv.n)] if nb == 0 else [])
            pw = (sp, sp.n)
            V(lambda e: e.tensor_mul(KF, SCR, C_L0), wait=[pw])
            V(lambda e: e.tensor_mul(SCR, PSA[:, 0:2048], S_L0))
            V(lambda e: e.tensor_add(KF, KF, SCR))
            V(lambda e: e.tensor_scalar(out=KF, in0=KF, scalar1=REL0C, scalar2=0.125,
                                        op0=ALU.add, op1=ALU.mult))
            KF_RDY = (sv, sv.n)

            ctx_ev = (sv, sv.n)
            for qt in range(16):
                q0 = qt * 128
                kext = q0 + 128
                qsl = slice(q0, q0 + 128)
                PE(lambda e, qsl=qsl: e.matmul(PSD[:, 0:128], CX["c_permpair"][:], QRAW[:, qsl],
                                      start=True, stop=True), wait=[KF_RDY])
                pw = (sp, sp.n)
                QR = MS1[:, 0:128]
                V(lambda e, qsl=qsl: e.tensor_mul(QR, QRAW[:, qsl], C_L0[:, qsl]), wait=[pw])
                V(lambda e, qsl=qsl: e.tensor_mul(MS1[:, 128:256], PSD[:, 0:128], S_L0[:, qsl]))
                V(lambda e: e.tensor_add(QR, QR, MS1[:, 128:256]))
                QR_RDY = (sv, sv.n)
                for hh in range(2):
                    hb = hh * 64
                    hsl = slice(hb, hb + 64)
                    PE(lambda e, hsl=hsl: e.matmul(PSD[:, 128:160], QR[hsl, :], RELDT2[hsl, :],
                                          start=True, stop=True),
                       wait=[QR_RDY, (sv, sv.n)])
                    pw = (sp, sp.n)
                    PP = MS1[:, 256:288]
                    V(lambda e: e.tensor_copy(out=PP, in_=PSD[:, 128:160]), wait=[pw])
                    vw = (sv, sv.n)
                    GD(lambda e: e.dma_start(out=AP(hb_band, 1, [[161, 128], [1, 32]]), in_=PP),
                       wait=[vw, (sg, sg.n)])
                    GD(lambda e: e.dma_start(out=MS1[:, 288:448],
                                             in_=AP(hb_band, 0, [[160, 128], [1, 160]])),
                       wait=[(sg, sg.n)])
                    bw = (sg, sg.n)
                    scw = [(sc, sc.n)]
                    for o in range(0, kext, 512):
                        w = min(512, kext - o)
                        PE(lambda e, o=o, w=w, hsl=hsl: e.matmul(PSA[:, o:o + w], QR[hsl, :],
                                                        KF[hsl, o:o + w], start=True, stop=True),
                           wait=scw)
                        scw = []
                    BND = MS1[:, 288:448]
                    V(lambda e: e.tensor_add(BND, BND, CXM160[:]), wait=[bw, (sp, sp.n)])
                    if qt > 0:
                        V(lambda e, q0=q0: e.tensor_add(PSA[:, q0 - 32:q0], PSA[:, q0 - 32:q0],
                                                 BND[:, 0:32]))
                    V(lambda e, q0=q0: e.tensor_add(PSA[:, q0:q0 + 128], PSA[:, q0:q0 + 128],
                                             BND[:, 32:160]))
                    NM = MM[:, 3:4]; LC = MM[:, 4:5]
                    V(lambda e, kext=kext: e.reduce_max(out=NM, in_=PSA[:, 0:kext],
                                                        axis=AX.X, negate=True))
                    vw = (sv, sv.n)
                    SCE(lambda e, kext=kext: e.activation(out=SCR[:, 0:kext], in_=PSA[:, 0:kext],
                                                          func=AF.Exp, bias=NM, scale=1.0,
                                                          accum_out=LC), wait=[vw])
                    cw_ = (sc, sc.n)
                    V(lambda e: e.reciprocal(out=MM[:, 5:6], in_=LC), wait=[cw_])
                    V(lambda e, kext=kext: e.tensor_scalar(out=SCR[:, 0:kext], in0=SCR[:, 0:kext],
                                                           scalar1=MM[:, 5:6], scalar2=None,
                                                           op0=ALU.mult))
                    EXP_RDY = (sv, sv.n)
                    lo = max(q0 - 32, 0)
                    woff = 32 if qt == 0 else 0
                    GD(lambda e, lo=lo, woff=woff, qt=qt, q0=q0: e.dma_start(
                        out=AP(hb_exp, qt * EXPP + woff * 1, [[160, 128], [1, 160 - woff]]),
                        in_=SCR[:, lo:q0 + 128]), wait=[EXP_RDY, (sg, sg.n)])
                    GD(lambda e, qt=qt: e.dma_start(
                        out=MS1[:, 256:288], in_=AP(hb_exp, qt * EXPP + 1, [[161, 128], [1, 32]])),
                       wait=[(sg, sg.n)])
                    tpw = (sg, sg.n)
                    PE(lambda e: e.transpose(PSD[0:32, 160:288], MS1[:, 256:288], CXID[:]),
                       wait=[tpw])
                    pw = (sp, sp.n)
                    V(lambda e: e.tensor_copy(out=MS1[0:32, 448:576], in_=PSD[0:32, 160:288]),
                      wait=[pw])
                    TT_RDY = (sv, sv.n)
                    tev = {}
                    for kk in range(qt + 1):
                        ksl = slice(kk * 128, (kk + 1) * 128)
                        pcol = (kk % 2) * 512
                        dsl = (kk % 2) * 128
                        PE(lambda e, ksl=ksl, pcol=pcol: e.transpose(
                            PSB[:, pcol:pcol + 128], SCR[:, ksl], CXID[:]),
                           wait=[tev[pcol]] if pcol in tev else [EXP_RDY])
                        pw = (sp, sp.n)
                        V(lambda e, pcol=pcol, dsl=dsl: e.tensor_copy(
                            out=MS1[:, 576 + dsl:704 + dsl], in_=PSB[:, pcol:pcol + 128]),
                          wait=[pw])
                        vw = (sv, sv.n)
                        tev[pcol] = vw
                        avw = [vw] + ([ctx_ev] if kk == 0 else [])
                        PE(lambda e, ksl=ksl, dsl=dsl, kk=kk, hb=hb: e.matmul(
                            PSC[hb:hb + 64, 0:128],
                            VP[:, kk * 128 + hb:kk * 128 + hb + 64],
                            MS1[:, 576 + dsl:704 + dsl],
                            start=(kk == 0), stop=False, tile_position=(0, hb)),
                           wait=avw)
                    PE(lambda e, hb=hb: e.matmul(PSC[hb:hb + 64, 0:128], RELDIF,
                                                 MS1[0:32, 448:576], start=False, stop=True,
                                                 tile_position=(0, hb)),
                       wait=[TT_RDY])
                    pav = (sp, sp.n)
                    V(lambda e, hb=hb, qsl=qsl: e.tensor_copy(out=CTX[hb:hb + 64, qsl],
                                                              in_=PSC[hb:hb + 64, 0:128]),
                      wait=[pav])
                    ctx_ev = (sv, sv.n)
                    wq_slot()

            opw = [ctx_ev]
            for dc in range(8):
                for tbb in range(4):
                    pt = PSC if (dc * 4 + tbb) % 2 == 0 else PSD
                    PE(lambda e, dc=dc, tbb=tbb, pt=pt: e.matmul(
                        pt[:, 0:512], WX[:, dc * 128:dc * 128 + 128],
                        CTX[:, tbb * 512:(tbb + 1) * 512], start=True, stop=True),
                       wait=opw + [(sv, sv.n)])
                    opw = []
                    pv = (sp, sp.n)
                    dst = A[:, dc * 2048 + tbb * 512:dc * 2048 + (tbb + 1) * 512]
                    if hp == 0:
                        V(lambda e, dst=dst, pt=pt: e.tensor_copy(out=dst, in_=pt[:, 0:512]),
                          wait=[pv])
                    else:
                        V(lambda e, dst=dst, pt=pt: e.tensor_add(dst, dst, pt[:, 0:512]),
                          wait=[pv])

        load_pair_weights(0, (sd, 0))
        load_x0_block(0, (sd, 0))
        for hp in range(8):
            s1_pair(hp)
            if hp == 1:
                wq_flush()
                wq_mid()
            if hp == 4:
                wq_flush()
            ckpt(0.6 + hp * 0.01)
        barrier()

        def dump(name, src_ap, rows=128):
            h = dbgt(name, rows, src_ap.shape[1])
            if h is not None:
                barrier()
                SY(lambda e, h=h, s=src_ap: e.dma_start(out=h, in_=s))
                barrier()

        dump("s1A", A[:, 0:16384])
        dump("s1ctx", CTX)
        ckpt(1)

        X0R0 = EW[:, 0:512]; X0R1 = EW[:, 512:1024]
        SQT = EW[:, 1024:1536]
        R0 = EW[0:1, 1536:2048]; R1 = EW[0:1, 2048:2560]; R2 = EW[0:1, 2560:3072]
        EPS = 1e-5

        for tbb in range(4):
            sq_ev = (sc, sc.n)
            sq_pe = (sp, 0)
            for dc in range(8):
                x1t = A[:, dc * 2048 + tbb * 512:dc * 2048 + (tbb + 1) * 512]
                xr = X0R0 if dc % 2 == 0 else X0R1
                ap = AP(din["x0"].tensor, dc * 128 * T + tbb * 512, [[T, 128], [1, 512]])
                SY(lambda e, xr=xr, s=ap: e.dma_start(out=xr, in_=s), wait=[(sv, sv.n)])
                xw = (sd, sd.n)
                V(lambda e, x1t=x1t, dc=dc, xr=xr: e.scalar_tensor_tensor(
                    out=x1t, in0=x1t, scalar=RELOUT[:, dc:dc + 1], in1=xr,
                    op0=ALU.add, op1=ALU.add), wait=[xw])
                vw = (sv, sv.n)
                PE(lambda e, x1t=x1t, dc=dc: e.matmul(PSD[0:1, 0:512], ONES_C, x1t,
                                                      start=(dc == 0), stop=(dc == 7)),
                   wait=[vw] + ([(sv, sv.n)] if dc == 0 else []))
                SCE(lambda e, x1t=x1t: e.activation(out=SQT, in_=x1t, func=AF.Square),
                    wait=[vw, sq_pe] + ([sq_ev] if dc == 0 else []))
                cw_ = (sc, sc.n)
                PE(lambda e, dc=dc: e.matmul(PSC[0:1, 0:512], ONES_C, SQT,
                                             start=(dc == 0), stop=(dc == 7)), wait=[cw_])
                sq_pe = (sp, sp.n)
            pw = (sp, sp.n)
            V(lambda e: e.tensor_scalar(out=R0, in0=PSD[0:1, 0:512], scalar1=1.0 / DM,
                                        scalar2=None, op0=ALU.mult), wait=[pw])
            V(lambda e: e.tensor_scalar(out=R1, in0=PSC[0:1, 0:512], scalar1=1.0 / DM,
                                        scalar2=None, op0=ALU.mult))
            V(lambda e: e.tensor_mul(R2, R0, R0))
            V(lambda e: e.tensor_sub(R1, R1, R2))
            V(lambda e: e.tensor_scalar(out=R1, in0=R1, scalar1=EPS, scalar2=None, op0=ALU.add))
            V(lambda e: e.reciprocal(out=R1, in_=R1))
            vw = (sv, sv.n)
            SCE(lambda e: e.activation(out=R1, in_=R1, func=AF.Sqrt), wait=[vw])
            cw_ = (sc, sc.n)
            V(lambda e: e.tensor_mul(R2, R0, R1), wait=[cw_])
            vw = (sv, sv.n)
            PE(lambda e: e.matmul(PSD[:, 0:512], ONES_R, R1, start=True, stop=True), wait=[vw])
            PE(lambda e: e.matmul(PSC[:, 0:512], ONES_R, R2, start=True, stop=True))
            pw = (sp, sp.n)
            for dc in range(8):
                x1t = A[:, dc * 2048 + tbb * 512:dc * 2048 + (tbb + 1) * 512]
                V(lambda e, x1t=x1t: e.tensor_mul(x1t, x1t, PSD[:, 0:512]),
                  wait=[pw] if dc == 0 else [])
                V(lambda e, x1t=x1t: e.tensor_sub(x1t, x1t, PSC[:, 0:512]))
            sq_ev = (sv, sv.n)

        barrier()
        dump("s2x1", A[:, 0:16384])
        ckpt(2)

        tev = {}
        for tc in range(16):
            for dc in range(8):
                po = ((tc * 8 + dc) % 2) * 512
                src = A[:, dc * 2048 + tc * 128:dc * 2048 + tc * 128 + 128]
                PE(lambda e, src=src, po=po: e.transpose(PSB[:, po:po + 128], src, CXID[:]),
                   wait=[tev[po]] if po in tev else [])
                pw = (sp, sp.n)
                dst = B[:, tc * 1024 + dc * 128:tc * 1024 + dc * 128 + 128]
                V(lambda e, dst=dst, po=po: e.tensor_copy(out=dst, in_=PSB[:, po:po + 128]),
                  wait=[pw])
                tev[po] = (sv, sv.n)
        barrier()
        dump("s2x1tm", B[:, 0:16384])
        ckpt(3)

        CWB = en(nc.sbuf_tensor("cwb", [128, 128], F32))
        SY(lambda e: e.dma_start(out=CWB[:, 0:96],
                                 in_=AP(din["cw"].tensor, 0, [[3, 128], [384, 32], [1, 3]])))
        SY(lambda e: e.dma_start(out=CWB[:, 96:128],
                                 in_=AP(din["cb"].tensor, 0, [[1, 128], [128, 32]]),
                                 allow_slow_non_contiguous=True))
        CW_RDY = (sd, sd.n)

        w1q_done = wq["wn"][31]
        w2q_done = wq["wn"][63]

        MMRC = MM[:, 32:48]
        for tc in range(16):
            xt = B[:, tc * 1024:(tc + 1) * 1024]
            y = A[:, tc * 1024:(tc + 1) * 1024]
            V(lambda e, xt=xt: e.reduce_max(out=MM[:, 20:21], in_=xt, axis=AX.X,
                                            apply_absolute_value=True))
            V(lambda e: e.tensor_scalar(out=MM[:, 21:22], in0=MM[:, 20:21], scalar1=1e-5,
                                        scalar2=None, op0=ALU.max))
            V(lambda e, tc=tc: e.tensor_scalar(out=MMRC[:, tc:tc + 1], in0=MM[:, 21:22],
                                               scalar1=1.0 / 127.0, scalar2=W1MCC,
                                               op0=ALU.mult, op1=ALU.mult))
            V(lambda e: e.reciprocal(out=MM[:, 22:23], in_=MM[:, 21:22]))
            V(lambda e, xt=xt, y=y: e.tensor_scalar(out=y, in0=xt, scalar1=MM[:, 22:23],
                                                    scalar2=127.0, op0=ALU.mult, op1=ALU.mult))
            round_tile(y, EW[:, 0:1024], EW[:, 1024:2048])
            V(lambda e, y=y: e.tensor_scalar(out=y, in0=y, scalar1=-128.0, scalar2=127.0,
                                             op0=ALU.max, op1=ALU.min))
        XQ_RDY = (sv, sv.n)
        GD(lambda e: e.dma_start(out=AP(hb_band, 0, [[1, 128], [128, 16]]), in_=MMRC,
                                 allow_slow_non_contiguous=True),
           wait=[XQ_RDY, (sg, sg.n)])
        GD(lambda e: e.dma_start(out=EW[0:1, 1538:3586], in_=AP(hb_band, 0, [[2048, 1], [1, 2048]])),
           wait=[(sg, sg.n)])
        ROW1_RDY = (sg, sg.n)
        ROW1 = EW[0:1, 1538:3586]
        tev = {}
        for dc in range(8):
            for tcg in range(4):
                po = (tcg % 2) * 512
                for t4 in range(4):
                    tc = tcg * 4 + t4
                    src = A[:, tc * 1024 + dc * 128:tc * 1024 + dc * 128 + 128]
                    PE(lambda e, src=src, po=po, t4=t4: e.transpose(
                        PSB[:, po + t4 * 128:po + t4 * 128 + 128], src, CXID[:]),
                       wait=([XQ_RDY] if dc == 0 and tcg == 0 and t4 == 0 else []) +
                            ([tev[po]] if (t4 == 0 and po in tev) else []))
                pw = (sp, sp.n)
                V(lambda e, po=po: e.tensor_copy(out=Dq[:, 0:512], in_=PSB[:, po:po + 512]),
                  wait=[pw, (sg, sg.n), (swqw, swqw.n)])
                vw = (sv, sv.n)
                tev[po] = vw
                oap = AP(hb_xq, dc * 128 * T + tcg * 512, [[T, 128], [1, 512]])
                GD(lambda e, o=oap: e.dma_start(out=o, in_=Dq[:, 0:512]), wait=[vw])
        barrier()
        ckpt(3.5)

        RB1 = EW[:, 0:258]; SIG = EW[:, 258:514]; SC2B = EW[:, 514:770]
        YQ = EW[:, 770:1026]; RA = EW[:, 1026:1282]; RB_ = EW[:, 1282:1538]
        XF = EW[:, 3586:4610]; XC = EW[:, 4610:5634]
        HMX = MM[:, 28:30]; HMXC = MM[:, 30:32]; RC2 = MM[:, 48:50]

        def xqs_load(tb):
            t0_ = tb * 256
            for cc in range(8):
                dsl = Dq[:, cc * 258:(cc + 1) * 258]
                if tb == 0:
                    V(lambda e, dsl=dsl: e.memset(dsl[:, 0:1], 0.0),
                      wait=[(sp, sp.n)] if cc == 0 else [])
                    ap = AP(hb_xq, cc * 128 * T, [[T, 128], [1, 257]])
                    SY(lambda e, s=ap, dsl=dsl: e.dma_start(out=dsl[:, 1:258], in_=s),
                       wait=[(sv, sv.n), (sp, sp.n), w1q_done] if cc == 0 else [])
                elif tb == 7:
                    V(lambda e, dsl=dsl: e.memset(dsl[:, 257:258], 0.0),
                      wait=[(sp, sp.n)] if cc == 0 else [])
                    ap = AP(hb_xq, cc * 128 * T + t0_ - 1, [[T, 128], [1, 257]])
                    SY(lambda e, s=ap, dsl=dsl: e.dma_start(out=dsl[:, 0:257], in_=s),
                       wait=[(sv, sv.n), (sp, sp.n)] if cc == 0 else [])
                else:
                    ap = AP(hb_xq, cc * 128 * T + t0_ - 1, [[T, 128], [1, 258]])
                    SY(lambda e, s=ap, dsl=dsl: e.dma_start(out=dsl, in_=s),
                       wait=[(sp, sp.n)] if cc == 0 else [])
            return (sd, sd.n)

        xqs = {0: xqs_load(0)}
        for tb in range(8):
            tok0 = tb * 256
            XQS_RDY = xqs[tb]
            lo = max(tok0 - 1, 0)
            w = 257 if tb in (0, 7) else 258
            off = 1 if tb == 0 else 0
            PE(lambda e, lo=lo, w=w, off=off: e.matmul(
                PSD[:, 0:w], ONES_R, ROW1[0:1, lo:lo + w], start=True, stop=True),
               wait=[ROW1_RDY, (sv, sv.n), (sc, sc.n)])
            pw = (sp, sp.n)
            if tb == 0:
                V(lambda e: e.memset(RB1[:, 0:1], 1.0))
            if tb == 7:
                V(lambda e: e.memset(RB1[:, 257:258], 1.0))
            V(lambda e, w=w, off=off: e.tensor_copy(out=RB1[:, off:off + w], in_=PSD[:, 0:w]),
              wait=[pw])
            RB1_RDY = (sv, sv.n)

            def w1q_load(hc, gate):
                wb = hc % 2
                sem = sWQa if wb == 0 else sWQb
                wsl = Dq[:, 2064 + wb * 1024:2064 + wb * 1024 + 1024]
                ap = AP(hb_w1q, hc * 128, [[4096, 128], [4096 * 128, 8], [1, 128]])
                EM("sync", lambda e, s=ap, wsl=wsl: e.dma_start(out=wsl, in_=s),
                   wait=list(gate), inc=sem, inck=16)

            w1q_load(0, [w1q_done] if tb == 0 else [(sp, sp.n)])
            hs_done = None
            psb_ev = {}
            for hc in range(32):
                wb = (hc % 2)
                wsl = Dq[:, 2064 + wb * 1024:2064 + wb * 1024 + 1024]
                wld = (sWQa if wb == 0 else sWQb, (sWQa if wb == 0 else sWQb).n)
                if hc < 31:
                    w1q_load(hc + 1, [(sp, sp.n)])
                po = (hc % 2) * 512
                for cc in range(8):
                    PE(lambda e, cc=cc, po=po, wsl=wsl: e.matmul(
                        PSB[:, po:po + 258], wsl[:, cc * 128:cc * 128 + 128],
                        Dq[:, cc * 258:(cc + 1) * 258], start=(cc == 0), stop=(cc == 7)),
                       wait=([wld, XQS_RDY] + ([psb_ev[po]] if po in psb_ev else []))
                       if cc == 0 else [])
                pw = (sp, sp.n)
                hst = A[:, hc * 258:(hc + 1) * 258]
                V(lambda e, hst=hst, po=po: e.tensor_mul(hst, PSB[:, po:po + 258], RB1),
                  wait=[pw, RB1_RDY])
                psb_ev[po] = (sv, sv.n)
                c0 = CWB[:, hc * 3:hc * 3 + 1]; c1 = CWB[:, hc * 3 + 1:hc * 3 + 2]
                c2 = CWB[:, hc * 3 + 2:hc * 3 + 3]; cbc = CWB[:, 96 + hc:97 + hc]
                V(lambda e, hst=hst, c0=c0: e.tensor_scalar(
                    out=PSD[:, 0:256], in0=hst[:, 0:256], scalar1=c0, scalar2=None,
                    op0=ALU.mult), wait=[CW_RDY, (sc, sc.n)] if hc == 0 else [(sc, sc.n)])
                V(lambda e, hst=hst, c1=c1: e.scalar_tensor_tensor(
                    out=PSD[:, 0:256], in0=hst[:, 1:257], scalar=c1, in1=PSD[:, 0:256],
                    op0=ALU.mult, op1=ALU.add))
                V(lambda e, hst=hst, c2=c2: e.scalar_tensor_tensor(
                    out=PSD[:, 0:256], in0=hst[:, 2:258], scalar=c2, in1=PSD[:, 0:256],
                    op0=ALU.mult, op1=ALU.add))
                vw = (sv, sv.n)
                SCE(lambda e, cbc=cbc: e.activation(out=SIG, in_=PSD[:, 0:256],
                                                    func=AF.Sigmoid, bias=cbc), wait=[vw])
                cw2 = (sc, sc.n)
                V(lambda e, hst=hst, cbc=cbc: e.scalar_tensor_tensor(
                    out=hst[:, 0:256], in0=PSD[:, 0:256], scalar=cbc, in1=SIG,
                    op0=ALU.add, op1=ALU.mult), wait=[cw2])
                hs_done = (sv, sv.n)
            if tb < 7:
                xqs[tb + 1] = xqs_load(tb + 1)
            ckpt(3.6 + tb * 0.001)

            ps_hm = {}
            for half in range(2):
                for grp in range(8):
                    bank = PSC if (half * 8 + grp) % 2 == 0 else PSD
                    for j in range(4):
                        hc = grp * 4 + j
                        hst = A[:, hc * 258:(hc + 1) * 258]
                        w = []
                        if j == 0:
                            b = (half * 8 + grp) % 2
                            w.append(ps_hm[b] if b in ps_hm else hs_done)
                        PE(lambda e, hst=hst, half=half, bank=bank, j=j: e.transpose(
                            bank[:, j * 128:(j + 1) * 128],
                            hst[:, half * 128:half * 128 + 128], CXID[:]), wait=w)
                    pw = (sp, sp.n)
                    if grp == 0:
                        V(lambda e, half=half, bank=bank: e.reduce_max(
                            out=HMX[:, half:half + 1], in_=bank[:, 0:512], axis=AX.X,
                            apply_absolute_value=True), wait=[pw])
                    else:
                        V(lambda e, bank=bank: e.reduce_max(
                            out=MM[:, 20:21], in_=bank[:, 0:512], axis=AX.X,
                            apply_absolute_value=True), wait=[pw])
                        V(lambda e, half=half: e.tensor_max(HMX[:, half:half + 1],
                                                            HMX[:, half:half + 1],
                                                            MM[:, 20:21]))
                    ps_hm[(half * 8 + grp) % 2] = (sv, sv.n)
            hs_done = (sv, sv.n)
            V(lambda e: e.tensor_scalar(out=HMXC, in0=HMX, scalar1=1e-5, scalar2=None,
                                        op0=ALU.max))
            V(lambda e: e.tensor_scalar(out=RC2, in0=HMXC, scalar1=1.0 / 127.0,
                                        scalar2=W2MCC, op0=ALU.mult, op1=ALU.mult))
            V(lambda e: e.reciprocal(out=MM[:, 20:22], in_=HMXC))
            vw = (sv, sv.n)
            PE(lambda e: e.transpose(PSD[0:1, 256:384], MM[:, 20:21], CXID[:]), wait=[vw])
            PE(lambda e: e.transpose(PSD[0:1, 384:512], MM[:, 21:22], CXID[:]))
            pw = (sp, sp.n)
            V(lambda e: e.tensor_scalar(out=EW[0:1, 5634:5890], in0=PSD[0:1, 256:512],
                                        scalar1=127.0, scalar2=None, op0=ALU.mult), wait=[pw])
            vw = (sv, sv.n)
            PE(lambda e: e.matmul(PSD[:, 0:256], ONES_R, EW[0:1, 5634:5890],
                                  start=True, stop=True), wait=[vw])
            pw = (sp, sp.n)
            V(lambda e: e.tensor_copy(out=SC2B, in_=PSD[:, 0:256]), wait=[pw])
            SC2_RDY = (sv, sv.n)

            def w2q_load(hc, gate):
                wb2 = hc % 2
                sem = sWTa if wb2 == 0 else sWTb
                wsl = Dq[:, 4112 + wb2 * 1024:4112 + wb2 * 1024 + 1024]
                ap = AP(hb_w2q, hc * 128 * 1024, [[1024, 128], [1, 1024]])
                EM("sync", lambda e, s=ap, wsl=wsl: e.dma_start(out=wsl, in_=s),
                   wait=list(gate), inc=sem, inck=16)

            YQB = A[:, 8256:9280]; AIN = A[:, 9280:10304]; BDF = A[:, 10304:11328]
            w2q_load(0, [w2q_done] if tb == 0 else [])
            for hcb in range(8):
                slot = Dq[:, 2064 + (hcb % 2) * 1024:2064 + (hcb % 2) * 1024 + 1024]
                for j in range(4):
                    hc = hcb * 4 + j
                    hst = A[:, hc * 258:(hc + 1) * 258]
                    w = []
                    if j == 0:
                        w.append((sg, sg.n))
                        if hcb == 0:
                            w.append(SC2_RDY)
                    V(lambda e, hst=hst, j=j: e.tensor_mul(
                        YQB[:, j * 256:(j + 1) * 256], hst[:, 0:256], SC2B), wait=w)
                for hf in range(2):
                    GD(lambda e, hf=hf: e.dma_start(out=MI[:, 0:512],
                                                    in_=YQB[:, hf * 512:(hf + 1) * 512]),
                       wait=[(sv, sv.n), (sg, sg.n)] if hf == 0 else [(sg, sg.n)])
                    GD(lambda e, hf=hf: e.dma_start(out=AIN[:, hf * 512:(hf + 1) * 512],
                                                    in_=MI[:, 0:512]), wait=[(sg, sg.n)])
                dw = (sg, sg.n)
                V(lambda e: e.tensor_sub(BDF, YQB, AIN), wait=[dw])
                V(lambda e: e.tensor_scalar(out=YQB, in0=BDF, scalar1=0.5, scalar2=None,
                                            op0=ALU.is_ge))
                V(lambda e: e.tensor_add(YQB, YQB, AIN))
                V(lambda e: e.tensor_scalar(out=AIN, in0=BDF, scalar1=-0.5, scalar2=None,
                                            op0=ALU.is_le))
                V(lambda e: e.tensor_sub(YQB, YQB, AIN))
                V(lambda e, slot=slot: e.tensor_scalar(out=slot, in0=YQB, scalar1=-128.0,
                                                       scalar2=127.0, op0=ALU.max, op1=ALU.min),
                  wait=[(sp, sp.n)])
                hq_rdy = (sv, sv.n)
                for j in range(4):
                    hc = hcb * 4 + j
                    hb2 = hc % 2
                    wb2sem = sWTa if hb2 == 0 else sWTb
                    wld = (wb2sem, wb2sem.n)
                    wsl = Dq[:, 4112 + hb2 * 1024:4112 + hb2 * 1024 + 1024]
                    if hc < 31:
                        w2q_load(hc + 1, [(sp, sp.n)])
                    for tc2 in range(2):
                        for dh in range(2):
                            bi = tc2 * 2 + dh
                            HQ = slot[:, j * 256:(j + 1) * 256]
                            PE(lambda e, tc2=tc2, dh=dh, bi=bi, HQ=HQ, wsl=wsl, hc=hc: e.matmul(
                                PSA[:, bi * 512:(bi + 1) * 512],
                                HQ[:, tc2 * 128:tc2 * 128 + 128],
                                wsl[:, dh * 512:(dh + 1) * 512],
                                start=(hc == 0), stop=(hc == 31)),
                               wait=([wld] + ([hq_rdy] if j == 0 else []))
                               if (tc2 == 0 and dh == 0) else [])
            mm2_done = (sp, sp.n)
            ckpt(3.7 + tb * 0.001)

            for tc2 in range(2):
                tcg = tb * 2 + tc2
                x1t = B[:, tcg * 1024:(tcg + 1) * 1024]
                for dh in range(2):
                    bi = tc2 * 2 + dh
                    V(lambda e, bi=bi, dh=dh, tc2=tc2: e.tensor_scalar(
                        out=XF[:, dh * 512:(dh + 1) * 512], in0=PSA[:, bi * 512:(bi + 1) * 512],
                        scalar1=RC2[:, tc2:tc2 + 1], scalar2=None, op0=ALU.mult),
                      wait=[mm2_done] if dh == 0 and tc2 == 0 else [])
                V(lambda e, x1t=x1t: e.tensor_add(XF, XF, x1t))
                V(lambda e: e.reduce_sum(out=MM[:, 20:21], in_=XF, axis=AX.X))
                V(lambda e: e.tensor_scalar(out=MM[:, 20:21], in0=MM[:, 20:21],
                                            scalar1=1.0 / DM, scalar2=None, op0=ALU.mult))
                V(lambda e: e.tensor_scalar(out=XC, in0=XF, scalar1=MM[:, 20:21],
                                            scalar2=None, op0=ALU.subtract))
                vw = (sv, sv.n)
                SCE(lambda e: e.activation(out=XF, in_=XC, func=AF.Square), wait=[vw])
                cw2 = (sc, sc.n)
                V(lambda e: e.reduce_sum(out=MM[:, 21:22], in_=XF, axis=AX.X), wait=[cw2])
                V(lambda e: e.tensor_scalar(out=MM[:, 21:22], in0=MM[:, 21:22],
                                            scalar1=1.0 / DM, scalar2=EPS,
                                            op0=ALU.mult, op1=ALU.add))
                V(lambda e: e.reciprocal(out=MM[:, 21:22], in_=MM[:, 21:22]))
                vw = (sv, sv.n)
                SCE(lambda e: e.activation(out=MM[:, 22:23], in_=MM[:, 21:22], func=AF.Sqrt),
                    wait=[vw])
                cw2 = (sc, sc.n)
                V(lambda e, x1t=x1t: e.tensor_scalar(out=x1t, in0=XC, scalar1=MM[:, 22:23],
                                                     scalar2=None, op0=ALU.mult), wait=[cw2])
        barrier()
        dump("s3x2tm", B[:, 0:16384])
        ckpt(4)

        DECS = MM[0:1, 50:51]
        SY(lambda e: e.dma_start(out=DECS, in_=din["dec"]))
        dw_ = (sd, sd.n)
        SCE(lambda e: e.activation(out=MM[0:1, 51:52], in_=DECS, func=AF.Sigmoid), wait=[dw_])
        SCE(lambda e: e.activation(out=MM[0:1, 52:53], in_=MM[0:1, 51:52], func=AF.Ln))
        LNA = MM[0:1, 52:53]
        cw_ = (sc, sc.n)
        PE(lambda e: e.matmul(PSD[:, 0:1], ONES_R, LNA, start=True, stop=True), wait=[cw_])
        pw = (sp, sp.n)
        V(lambda e: e.tensor_copy(out=MM[:, 53:54], in_=PSD[:, 0:1]), wait=[pw])
        LNAC = MM[:, 53:54]
        GP(lambda e: e.iota(EW[0:1, 0:128], pattern=[[1, 128]], base=0, channel_multiplier=0,
                            allow_small_or_imprecise_dtypes=True), wait=[(sv, sv.n), (sc, sc.n)])
        gw = (sq, sq.n)
        V(lambda e: e.tensor_scalar(out=EW[0:1, 128:256], in0=EW[0:1, 0:128], scalar1=LNA,
                                    scalar2=None, op0=ALU.mult), wait=[gw])
        V(lambda e: e.tensor_scalar(out=EW[0:1, 256:384], in0=EW[0:1, 0:128], scalar1=LNA,
                                    scalar2=LNA, op0=ALU.mult, op1=ALU.add))
        V(lambda e: e.tensor_scalar(out=MM[0:1, 54:55], in0=LNA, scalar1=128.0, scalar2=None,
                                    op0=ALU.mult))
        V(lambda e: e.tensor_scalar(out=MM[:, 55:56], in0=CX["c_rev_c"][:], scalar1=LNAC,
                                    scalar2=None, op0=ALU.mult))
        vw = (sv, sv.n)
        APOW = EW[0:1, 128:256]
        APOW1 = EW[0:1, 256:384]
        SCE(lambda e: e.activation(out=APOW, in_=APOW, func=AF.Exp), wait=[vw])
        SCE(lambda e: e.activation(out=APOW1, in_=APOW1, func=AF.Exp))
        SCE(lambda e: e.activation(out=MM[0:1, 54:55], in_=MM[0:1, 54:55], func=AF.Exp))
        SCE(lambda e: e.activation(out=MM[:, 56:57], in_=MM[:, 55:56], func=AF.Exp))
        A128 = MM[0:1, 54:55]
        APREV = MM[:, 56:57]
        cw_ = (sc, sc.n)
        PE(lambda e: e.matmul(PSD[:, 0:128], ONES_R, APOW, start=True, stop=True), wait=[cw_])
        pw = (sp, sp.n)
        LTS = EW[:, 384:512]
        V(lambda e: e.tensor_mul(LTS, PSD[:, 0:128], CXTRI[:]), wait=[pw])
        vw = (sv, sv.n)
        GD(lambda e: e.dma_start(out=AP(hb_LT, 0, [[129, 128], [1, 128]]), in_=LTS),
           wait=[vw, (sg, sg.n)])
        LTT = EW[:, 512:640]
        GD(lambda e: e.dma_start(out=LTT, in_=AP(hb_LT, 0, [[128, 128], [1, 128]])),
           wait=[(sg, sg.n)])
        LT_RDY = (sg, sg.n)

        EPRV = EW[0:1, 640:1664]
        ECUR = EW[0:1, 1664:2688]
        ps_ev = {}
        for c in range(16):
            x2c = B[:, c * 1024:(c + 1) * 1024]
            grp = c % 2
            for fc in range(8):
                col = grp * 1024 + fc * 128
                PE(lambda e, x2c=x2c, fc=fc, col=col, c=c: e.matmul(
                    PSA[:, col:col + 128], x2c[:, fc * 128:fc * 128 + 128], LTT,
                    start=True, stop=(c == 0)),
                   wait=([LT_RDY] if (c == 0 and fc == 0) else []) +
                        ([ps_ev[grp]] if (fc == 0 and grp in ps_ev) else []))
                if c > 0:
                    PE(lambda e, fc=fc, col=col: e.matmul(
                        PSA[:, col:col + 128], EPRV[0:1, fc * 128:fc * 128 + 128], APOW1,
                        start=False, stop=True),
                       wait=[(sv, sv.n)] if fc == 0 else [])
            pw = (sp, sp.n)
            for fc in range(8):
                col = grp * 1024 + fc * 128
                dst = A[:, fc * 2048 + c * 128:fc * 2048 + c * 128 + 128]
                V(lambda e, dst=dst, col=col: e.tensor_copy(out=dst, in_=PSA[:, col:col + 128]),
                  wait=[pw] if fc == 0 else [])
            ps_ev[grp] = (sv, sv.n)
            if c < 15:
                for hf in range(2):
                    pt = PSD if hf == 0 else PSC
                    PE(lambda e, x2c=x2c, hf=hf, pt=pt: e.matmul(
                        pt[0:1, 0:512], APREV,
                        x2c[:, hf * 512:(hf + 1) * 512], start=True, stop=True),
                       wait=[(sv, sv.n)] if hf == 0 else [])
                pw = (sp, sp.n)
                V(lambda e: e.tensor_copy(out=ECUR[0:1, 0:512], in_=PSD[0:1, 0:512]), wait=[pw])
                V(lambda e: e.tensor_copy(out=ECUR[0:1, 512:1024], in_=PSC[0:1, 0:512]))
                if c == 0:
                    V(lambda e: e.tensor_copy(out=EPRV, in_=ECUR), wait=[(sp, sp.n)])
                else:
                    V(lambda e: e.scalar_tensor_tensor(out=EPRV, in0=EPRV, scalar=A128,
                                                       in1=ECUR, op0=ALU.mult, op1=ALU.add),
                      wait=[(sp, sp.n)])
        barrier()
        dump("s4S", A[:, 0:16384])
        ckpt(5)

        B1B = TBL[:, 0:1024]; GMB = TBL[:, 1024:2048]; BTB = TBL[:, 2048:3072]
        for nm, dst in (("b1r", B1B), ("gamr", GMB), ("betr", BTB)):
            ld = SY(lambda e, nm=nm: e.dma_start(out=EW[0:1, 4096:5120], in_=din[nm]),
                    wait=[(sp, sp.n), (sv, sv.n)])
            for hf in range(2):
                PE(lambda e, hf=hf: e.matmul(PSD[:, 0:512], ONES_R,
                                             EW[0:1, 4096 + hf * 512:4096 + hf * 512 + 512],
                                             start=True, stop=True),
                   wait=[(sd, ld), (sv, sv.n)])
                pw = (sp, sp.n)
                V(lambda e, dst=dst, hf=hf: e.tensor_copy(out=dst[:, hf * 512:(hf + 1) * 512],
                                                          in_=PSD[:, 0:512]), wait=[pw])

        Y4 = EW[:, 0:4096]
        YB = EW[:, 4096:5120]; YC = EW[:, 5120:6144]

        def wt_tile(gg, j):
            if gg % 2 == 0:
                return WW[:, j * 512:(j + 1) * 512]
            return (WW[:, 2048 + j * 512:2048 + (j + 1) * 512] if j < 2
                    else WX[:, (j - 2) * 512:(j - 1) * 512])

        wt_rdy = {}

        def wt_load_grp(gg, gate):
            sem = sWTa if gg % 2 == 0 else sWTb
            tcb_, r = divmod(gg, 4)
            eh_, dh_ = divmod(r, 2)
            semg = sWGa if gg % 2 == 0 else sWGb
            for j in range(4):
                dc = dh_ * 4 + j
                ap = AP(din["Wt"].tensor, dc * 128 * DM + eh_ * 512, [[DM, 128], [1, 512]])
                eng = "sync" if j % 2 == 0 else "gpsimd"
                EM(eng, lambda e, dst=wt_tile(gg, j), s=ap: e.dma_start(out=dst, in_=s),
                   wait=list(gate) if j < 2 else [], inc=(sem if j % 2 == 0 else semg),
                   inck=16)
            wt_rdy[gg] = [(sem, sem.n), (semg, semg.n)]

        wt_load_grp(0, [(sp, sp.n), (sv, sv.n)])
        wt_load_grp(1, [])
        for tcb in range(4):
            for eh in range(2):
                for dh in range(2):
                    gg = (tcb * 2 + eh) * 2 + dh
                    for j in range(4):
                        dc = dh * 4 + j
                        for t4 in range(4):
                            src = A[:, dc * 2048 + (tcb * 4 + t4) * 128:
                                    dc * 2048 + (tcb * 4 + t4) * 128 + 128]
                            w = []
                            if j == 0 and t4 == 0:
                                w += wt_rdy[gg]
                                if dh == 0:
                                    w.append((sv, sv.n))
                            PE(lambda e, src=src, dst=wt_tile(gg, j), dc=dc, t4=t4: e.matmul(
                                PSA[:, t4 * 512:(t4 + 1) * 512], src, dst,
                                start=(dc == 0), stop=(dc == 7)), wait=w)
                    if gg + 2 <= 15:
                        wt_load_grp(gg + 2, [(sp, sp.n)])
                pw = (sp, sp.n)
                for t4 in range(4):
                    V(lambda e, t4=t4, eh=eh: e.tensor_add(
                        Y4[:, t4 * 1024 + eh * 512:t4 * 1024 + eh * 512 + 512],
                        PSA[:, t4 * 512:(t4 + 1) * 512],
                        B1B[:, eh * 512:(eh + 1) * 512]), wait=[pw] if t4 == 0 else [])
            for t4 in range(4):
                tc = tcb * 4 + t4
                x2t = B[:, tc * 1024:(tc + 1) * 1024]
                Yt = Y4[:, t4 * 1024:(t4 + 1) * 1024]
                V(lambda e, Yt=Yt: e.reduce_sum(out=MM[:, 20:21], in_=Yt, axis=AX.X))
                V(lambda e: e.tensor_scalar(out=MM[:, 20:21], in0=MM[:, 20:21],
                                            scalar1=1.0 / DM, scalar2=None, op0=ALU.mult))
                V(lambda e, Yt=Yt: e.tensor_scalar(out=YC, in0=Yt, scalar1=MM[:, 20:21],
                                                   scalar2=None, op0=ALU.subtract))
                vw = (sv, sv.n)
                SCE(lambda e: e.activation(out=YB, in_=YC, func=AF.Square), wait=[vw])
                cw_ = (sc, sc.n)
                V(lambda e: e.reduce_sum(out=MM[:, 21:22], in_=YB, axis=AX.X), wait=[cw_])
                V(lambda e: e.tensor_scalar(out=MM[:, 21:22], in0=MM[:, 21:22], scalar1=1.0 / DM,
                                            scalar2=EPS, op0=ALU.mult, op1=ALU.add))
                V(lambda e: e.reciprocal(out=MM[:, 21:22], in_=MM[:, 21:22]))
                vw = (sv, sv.n)
                SCE(lambda e: e.activation(out=MM[:, 22:23], in_=MM[:, 21:22], func=AF.Sqrt),
                    wait=[vw])
                cw_ = (sc, sc.n)
                V(lambda e: e.tensor_scalar(out=YB, in0=YC, scalar1=MM[:, 22:23], scalar2=None,
                                            op0=ALU.mult), wait=[cw_])
                V(lambda e: e.tensor_mul(YB, YB, GMB))
                V(lambda e: e.tensor_add(YB, YB, BTB))
                V(lambda e, x2t=x2t: e.tensor_add(YC, YB, x2t))
                V(lambda e: e.reduce_sum(out=MM[:, 20:21], in_=YC, axis=AX.X))
                V(lambda e: e.tensor_scalar(out=MM[:, 20:21], in0=MM[:, 20:21], scalar1=1.0 / DM,
                                            scalar2=None, op0=ALU.mult))
                V(lambda e: e.tensor_scalar(out=YB, in0=YC, scalar1=MM[:, 20:21], scalar2=None,
                                            op0=ALU.subtract))
                vw = (sv, sv.n)
                SCE(lambda e: e.activation(out=YC, in_=YB, func=AF.Square), wait=[vw])
                cw_ = (sc, sc.n)
                V(lambda e: e.reduce_sum(out=MM[:, 21:22], in_=YC, axis=AX.X), wait=[cw_])
                V(lambda e: e.tensor_scalar(out=MM[:, 21:22], in0=MM[:, 21:22], scalar1=1.0 / DM,
                                            scalar2=EPS, op0=ALU.mult, op1=ALU.add))
                V(lambda e: e.reciprocal(out=MM[:, 21:22], in_=MM[:, 21:22]))
                vw = (sv, sv.n)
                SCE(lambda e: e.activation(out=MM[:, 22:23], in_=MM[:, 21:22], func=AF.Sqrt),
                    wait=[vw])
                cw_ = (sc, sc.n)
                V(lambda e, x2t=x2t: e.tensor_scalar(out=x2t, in0=YB, scalar1=MM[:, 22:23],
                                                     scalar2=None, op0=ALU.mult), wait=[cw_])
        barrier()
        dump("s4x3tm", B[:, 0:16384])
        ckpt(6)

        tev = {}
        for tc in range(16):
            for dc in range(8):
                po = ((tc * 8 + dc) % 2) * 512
                src = B[:, tc * 1024 + dc * 128:tc * 1024 + dc * 128 + 128]
                PE(lambda e, src=src, po=po: e.transpose(PSB[:, po:po + 128], src, CXID[:]),
                   wait=[tev[po]] if po in tev else [])
                pw = (sp, sp.n)
                dst = A[:, dc * 2048 + tc * 128:dc * 2048 + tc * 128 + 128]
                V(lambda e, dst=dst, po=po: e.tensor_copy(out=dst, in_=PSB[:, po:po + 128]),
                  wait=[pw])
                tev[po] = (sv, sv.n)
        barrier()
        dump("s5x3fm", A[:, 0:16384])
        ckpt(7)

        C_L1 = TBL[:, 0:2048]; S_L1 = TBL[:, 2048:4096]
        SY(lambda e: e.dma_start(out=C_L1, in_=din["c_cosl1"]), wait=[(sp, sp.n), (sv, sv.n)])
        SY(lambda e: e.dma_start(out=S_L1, in_=din["c_ssinl1"]))
        V(lambda e: e.memset(MM[:, 57:58], 128.0))
        V(lambda e: e.reciprocal(out=MM[:, 57:58], in_=MM[:, 57:58]))
        vw = (sv, sv.n)
        SCE(lambda e: e.activation(out=MM[:, 57:58], in_=MM[:, 57:58], func=AF.Sqrt), wait=[vw])
        R128C = MM[:, 57:58]
        barrier()
        ckpt(7.2)

        GTM = en(nc.sbuf_tensor("gtm", [128, 256], F32))
        SY(lambda e: e.dma_start(out=WX[:, 0:128],
                                 in_=AP(din["winT"].tensor, 4096,
                                        [[4112, 128], [128 * 4112, 8], [1, 16]])))
        rw = (sd, sd.n)
        for tc in range(16):
            for cc in range(8):
                PE(lambda e, tc=tc, cc=cc: e.matmul(
                    PSD[:, 0:16], A[:, cc * 2048 + tc * 128:cc * 2048 + tc * 128 + 128],
                    WX[:, cc * 16:(cc + 1) * 16], start=(cc == 0), stop=(cc == 7)),
                   wait=[rw, (sv, sv.n)] if cc == 0 else [])
            pw = (sp, sp.n)
            g = GTM[:, tc * 16:(tc + 1) * 16]
            V(lambda e, g=g: e.reduce_max(out=MM[:, 20:21], in_=PSD[:, 0:16], axis=AX.X),
              wait=[pw])
            V(lambda e, g=g: e.tensor_scalar(out=g, in0=PSD[:, 0:16], scalar1=MM[:, 20:21],
                                             scalar2=None, op0=ALU.is_equal))
            V(lambda e, g=g: e.scalar_tensor_tensor(out=g, in0=g, scalar=-1e30,
                                                    in1=PSD[:, 0:16], op0=ALU.mult, op1=ALU.add))
            V(lambda e, g=g: e.reduce_max(out=MM[:, 21:22], in_=g, axis=AX.X))
            V(lambda e, g=g: e.tensor_scalar(out=g, in0=PSD[:, 0:16], scalar1=MM[:, 21:22],
                                             scalar2=None, op0=ALU.is_ge))
            vw = (sv, sv.n)
            SCE(lambda e: e.activation(out=MM[:, 24:40], in_=PSD[:, 0:16], func=AF.Sigmoid),
                wait=[vw])
            cw_ = (sc, sc.n)
            V(lambda e, g=g: e.tensor_mul(g, g, MM[:, 24:40]), wait=[cw_])
        ckpt(7.3)
        GT = EW[0:16, 0:2048]
        tev = {}
        for tc in range(16):
            pt = PSD if tc % 2 == 0 else PSC
            PE(lambda e, tc=tc, pt=pt: e.transpose(
                pt[0:16, 128:256], GTM[:, tc * 16:(tc + 1) * 16], CXID[:]),
               wait=([(sv, sv.n)] if tc < 2 else []) + ([tev[tc % 2]] if tc % 2 in tev else []))
            pw = (sp, sp.n)
            V(lambda e, tc=tc, pt=pt: e.tensor_copy(out=GT[:, tc * 128:(tc + 1) * 128],
                                                    in_=pt[0:16, 128:256]),
              wait=[pw])
            tev[tc % 2] = (sv, sv.n)
        barrier()

        KH = EW[:, 0:2048]
        GT2 = WW[0:16, 0:2048]
        V(lambda e: e.tensor_copy(out=GT2, in_=GT))
        barrier()
        ckpt(7.5)
        VH = EW[:, 2048:4096]
        EXPB = EW[:, 4096:6144]

        attn_prev = (sv, sv.n)
        for h in range(8):
            pe_prev = (sp, sp.n)
            SY(lambda e, h=h: e.dma_start(out=WX[:, 0:1024],
                                          in_=AP(din["winT"].tensor, 2048 + h * 128,
                                                 [[4112, 128], [128 * 4112, 8], [1, 128]])),
               wait=[pe_prev])
            kwl = (sd, sd.n)
            SY(lambda e, h=h: e.dma_start(out=WW[:, 2048:3072],
                                          in_=AP(din["winT"].tensor, 3072 + h * 128,
                                                 [[4112, 128], [128 * 4112, 8], [1, 128]])),
               wait=[pe_prev])
            vwl = (sd, sd.n)

            for nb in range(4):
                for cc in range(8):
                    PE(lambda e, nb=nb, cc=cc: e.matmul(
                        PSA[:, nb * 512:(nb + 1) * 512], WX[:, cc * 128:(cc + 1) * 128],
                        A[:, cc * 2048 + nb * 512:cc * 2048 + (nb + 1) * 512],
                        start=(cc == 0), stop=(cc == 7)),
                       wait=[kwl, attn_prev] if (nb == 0 and cc == 0) else [])
            pw = (sp, sp.n)
            SCE(lambda e: e.copy(out=KH, in_=PSA[:, 0:2048]), wait=[pw])
            kev = (sc, sc.n)
            for nb in range(4):
                PE(lambda e, nb=nb: e.matmul(PSA[:, nb * 512:(nb + 1) * 512],
                                             CX["c_permhalf"][:], KH[:, nb * 512:(nb + 1) * 512],
                                             start=True, stop=True),
                   wait=[kev] if nb == 0 else [])
            pw = (sp, sp.n)
            V(lambda e: e.tensor_mul(EXPB, KH, C_L1), wait=[pw])
            V(lambda e: e.tensor_mul(KH, PSA[:, 0:2048], S_L1))
            V(lambda e: e.tensor_add(KH, KH, EXPB))
            V(lambda e: e.tensor_scalar(out=KH, in0=KH, scalar1=R128C, scalar2=None,
                                        op0=ALU.mult))
            KH_RDY = (sv, sv.n)
            tevv = {}
            for tc in range(16):
                po = (tc % 2)
                pt = PSC if po == 0 else PSD
                for cc in range(8):
                    w = []
                    if cc == 0:
                        w = [vwl] + ([KH_RDY] if tc == 0 else []) + \
                            ([tevv[po]] if po in tevv else [])
                    PE(lambda e, tc=tc, cc=cc, pt=pt: e.matmul(
                        pt[:, 0:128], A[:, cc * 2048 + tc * 128:cc * 2048 + tc * 128 + 128],
                        WW[:, 2048 + cc * 128:2048 + (cc + 1) * 128],
                        start=(cc == 0), stop=(cc == 7)), wait=w)
                pw = (sp, sp.n)
                V(lambda e, tc=tc, pt=pt: e.tensor_copy(out=VH[:, tc * 128:(tc + 1) * 128],
                                                        in_=pt[:, 0:128]), wait=[pw])
                tevv[po] = (sv, sv.n)
            VH_RDY = (sv, sv.n)
            SY(lambda e, h=h: e.dma_start(out=WX[:, 0:1024],
                                          in_=AP(din["winT"].tensor, 1024 + h * 128,
                                                 [[4112, 128], [128 * 4112, 8], [1, 128]])),
               wait=[(sp, sp.n)])
            qwl = (sd, sd.n)

            exp_ev = (sc, sc.n)
            ctx_ev = (sv, sv.n)
            for qt in range(16):
                q0 = qt * 128
                kext = q0 + 128
                for cc in range(8):
                    PE(lambda e, cc=cc, q0=q0: e.matmul(
                        PSD[:, 0:128], WX[:, cc * 128:(cc + 1) * 128],
                        A[:, cc * 2048 + q0:cc * 2048 + q0 + 128],
                        start=(cc == 0), stop=(cc == 7)),
                       wait=[qwl, (sv, sv.n)] if cc == 0 else [])
                pw = (sp, sp.n)
                QRB = WX[:, 1024:1152]; QTMP = WX[:, 1152:1280]
                V(lambda e: e.tensor_copy(out=QTMP, in_=PSD[:, 0:128]), wait=[pw])
                vw = (sv, sv.n)
                PE(lambda e: e.matmul(PSD[:, 128:256], CX["c_permhalf"][:], QTMP,
                                      start=True, stop=True), wait=[vw])
                pw = (sp, sp.n)
                V(lambda e, q0=q0: e.tensor_mul(QRB, QTMP, C_L1[:, q0:q0 + 128]), wait=[pw])
                V(lambda e, q0=q0: e.tensor_mul(QTMP, PSD[:, 128:256], S_L1[:, q0:q0 + 128]))
                V(lambda e: e.tensor_add(QRB, QRB, QTMP))
                QR_RDY = (sv, sv.n)
                scw = [QR_RDY, exp_ev]
                for o in range(0, kext, 512):
                    w = min(512, kext - o)
                    PE(lambda e, o=o, w=w: e.matmul(PSA[:, o:o + w], QRB, KH[:, o:o + w],
                                                    start=True, stop=True), wait=scw)
                    scw = []
                pw = (sp, sp.n)
                V(lambda e, q0=q0: e.tensor_add(PSA[:, q0:q0 + 128], PSA[:, q0:q0 + 128],
                                                CXM160[:, 32:160]), wait=[pw])
                NM = MM[:, 3:4]; LC = MM[:, 4:5]
                V(lambda e, kext=kext: e.reduce_max(out=NM, in_=PSA[:, 0:kext], axis=AX.X,
                                                    negate=True))
                vw = (sv, sv.n)
                SCE(lambda e, kext=kext: e.activation(out=EXPB[:, 0:kext], in_=PSA[:, 0:kext],
                                                      func=AF.Exp, bias=NM, scale=1.0,
                                                      accum_out=LC), wait=[vw])
                exp_ev = (sc, sc.n)
                V(lambda e: e.reciprocal(out=MM[:, 5:6], in_=LC), wait=[exp_ev])
                V(lambda e, kext=kext: e.tensor_scalar(out=EXPB[:, 0:kext], in0=EXPB[:, 0:kext],
                                                       scalar1=MM[:, 5:6], scalar2=None,
                                                       op0=ALU.mult))
                EXP_RDY = (sv, sv.n)
                tev2 = {}
                for kk in range(qt + 1):
                    pcol = (kk % 2) * 512
                    dsl = (kk % 2) * 128
                    PE(lambda e, kk=kk, pcol=pcol: e.transpose(
                        PSB[:, pcol:pcol + 128], EXPB[:, kk * 128:(kk + 1) * 128], CXID[:]),
                       wait=[tev2[pcol]] if pcol in tev2 else [EXP_RDY])
                    pw = (sp, sp.n)
                    V(lambda e, pcol=pcol, dsl=dsl: e.tensor_copy(
                        out=WX[:, 1280 + dsl:1408 + dsl], in_=PSB[:, pcol:pcol + 128]),
                      wait=[pw])
                    vw = (sv, sv.n)
                    tev2[pcol] = vw
                    PE(lambda e, kk=kk, dsl=dsl, qt=qt: e.matmul(
                        PSC[:, 0:128], VH[:, kk * 128:(kk + 1) * 128],
                        WX[:, 1280 + dsl:1408 + dsl],
                        start=(kk == 0), stop=(kk == qt)),
                       wait=[vw] + ([ctx_ev] if kk == 0 else []))
                pw = (sp, sp.n)
                V(lambda e, h=h, q0=q0: e.tensor_copy(
                    out=B[:, h * 2048 + q0:h * 2048 + q0 + 128], in_=PSC[:, 0:128]), wait=[pw])
                ctx_ev = (sv, sv.n)
            attn_prev = ctx_ev
            ckpt(7.6 + h * 0.01)
        barrier()
        dump("s5attn", B[:, 0:16384])
        ckpt(8)

        GB_ROW = EW[0:1, 0:512]
        QF = EW[:, 512:1024]
        AEW0 = EW[:, 1024:1536]
        AEW1 = EW[:, 1536:2048]
        X4B = TBL[:, 1024:2048]
        X4C = TBL[:, 2048:3072]
        X4S = TBL[:, 3072:4096]

        n_out_dma = 0
        ffw_prev = (sp, sp.n)
        ONES32 = GTM[32:33, 0:128]
        V(lambda e: e.memset(ONES32, 1.0), wait=[(sp, sp.n)])

        def qf_load(h, gate):
            SY(lambda e, h=h: e.dma_start(out=WW[:, 2048:3072],
                                          in_=AP(din["winT"].tensor, h * 128,
                                                 [[4112, 128], [128 * 4112, 8], [1, 128]])),
               wait=list(gate))
            return (sd, sd.n)

        qfw_next = qf_load(0, [(sp, sp.n), (sv, sv.n)])
        for tb5 in range(4):
            tok0 = tb5 * 512
            for h in range(8):
                qfw = qfw_next
                for cc in range(8):
                    PE(lambda e, cc=cc, tok0=tok0: e.matmul(
                        PSB[:, 0:512], WW[:, 2048 + cc * 128:2048 + (cc + 1) * 128],
                        A[:, cc * 2048 + tok0:cc * 2048 + tok0 + 512],
                        start=(cc == 0), stop=(cc == 7)),
                       wait=[qfw, (sv, sv.n)] if cc == 0 else [])
                if not (tb5 == 3 and h == 7):
                    qfw_next = qf_load((h + 1) % 8, [(sp, sp.n)])
                pw = (sp, sp.n)
                V(lambda e: e.tensor_copy(out=QF, in_=PSB[:, 0:512]), wait=[pw])
                QF_RDY = (sv, sv.n)

                def kv_load(ee, gate):
                    kb = (ee % 2) * 512
                    sem = sKVa if ee % 2 == 0 else sKVb
                    kap = AP(din["kfT"].tensor, (ee * 8 + h) * 128 * 256,
                             [[256, 128], [1, 256]])
                    vap = AP(din["vf"].tensor, (ee * 8 + h) * 256 * 128,
                             [[128, 128], [128 * 128, 2], [1, 128]])
                    EM("sync", lambda e, s=kap, d=WX[:, kb:kb + 256]: e.dma_start(out=d, in_=s),
                       wait=list(gate), inc=sem, inck=16)
                    EM("sync", lambda e, s=vap, d=WX[:, kb + 256:kb + 512]: e.dma_start(out=d, in_=s),
                       inc=sem, inck=16)

                def gb_slot(ee):
                    return EW[0:1, 0:512] if ee % 2 == 0 else EW[32:33, 0:512]

                def grow(ee, gate):
                    GD(lambda e, ee=ee, tok0=tok0: e.dma_start(
                        out=gb_slot(ee), in_=GT2[ee:ee + 1, tok0:tok0 + 512]),
                       wait=list(gate))
                    return (sg, sg.n)

                def gbank(ee):
                    return PSD[:, 0:512] if ee % 2 == 0 else PSB[:, 0:512]

                kv_load(0, [(sp, sp.n)])
                g0 = [(sp, sp.n)] + ([(sv, sv.n)] if (tb5 == 0 and h == 0) else [])
                grw = {0: grow(0, g0)}
                gbp = {}
                gelu_after = {}
                mul_after = {}
                lhs1s = (ONES_R, ONES32)
                PE(lambda e: e.matmul(gbank(0), lhs1s[0], gb_slot(0),
                                      start=True, stop=True),
                   wait=[grw[0], (sv, sv.n), QF_RDY])
                gbp[0] = (sp, sp.n)
                for ee in range(16):
                    kb = (ee % 2) * 512
                    ksl = WX[:, kb:kb + 256]
                    vsl = WX[:, kb + 256:kb + 512]
                    kvsem = sKVa if ee % 2 == 0 else sKVb
                    kvw = (kvsem, kvsem.n)
                    eep = (ee % 2) * 1024
                    vws = []
                    pws = []
                    for sch in range(2):
                        w = []
                        if sch == 0:
                            w += [kvw, QF_RDY]
                            if ee - 2 in gelu_after:
                                w.append(gelu_after[ee - 2])
                            elif ee < 2:
                                w.append((sc, sc.n))
                        PE(lambda e, sch=sch, ksl=ksl, eep=eep: e.matmul(
                            PSA[:, eep + sch * 512:eep + sch * 512 + 512],
                            ksl[:, sch * 128:sch * 128 + 128],
                            QF, start=True, stop=True), wait=w)
                        pws.append((sp, sp.n))
                    for sch in range(2):
                        aew = AEW0 if sch == 0 else AEW1
                        SCE(lambda e, sch=sch, aew=aew, eep=eep: e.activation(
                            out=aew, in_=PSA[:, eep + sch * 512:eep + sch * 512 + 512],
                            func=AF.Gelu), wait=[pws[sch]])
                    gelu_after[ee] = (sc, sc.n)
                    for sch in range(2):
                        aew = AEW0 if sch == 0 else AEW1
                        V(lambda e, aew=aew, ee=ee: e.tensor_mul(aew, aew, gbank(ee)),
                          wait=([gelu_after[ee], gbp[ee]] if sch == 0 else []))
                        vws.append((sv, sv.n))
                    mul_after[ee] = (sv, sv.n)
                    if ee < 15:
                        kv_load(ee + 1, [(sp, sp.n)])
                        grw[ee + 1] = grow(ee + 1, [(sp, sp.n)])
                        wb_ = [grw[ee + 1]]
                        wb_.append(mul_after[ee - 1] if ee - 1 in mul_after else (sv, sv.n))
                        PE(lambda e, ee=ee: e.matmul(gbank(ee + 1), lhs1s[(ee + 1) % 2],
                                                     gb_slot(ee + 1), start=True, stop=True),
                           wait=wb_)
                        gbp[ee + 1] = (sp, sp.n)
                    for sch in range(2):
                        aew = AEW0 if sch == 0 else AEW1
                        PE(lambda e, sch=sch, vsl=vsl, ee=ee, aew=aew: e.matmul(
                            PSC[:, 0:512], vsl[:, sch * 128:sch * 128 + 128], aew,
                            start=(ee == 0 and sch == 0), stop=(ee == 15 and sch == 1)),
                           wait=[vws[sch]] + ([ffw_prev] if (ee == 0 and sch == 0) else []))
                pw = (sp, sp.n)
                V(lambda e, h=h: e.tensor_copy(out=EW[:, 2048 + h * 512:2048 + (h + 1) * 512],
                                               in_=PSC[:, 0:512]), wait=[pw])
            FFW_RDY = (sv, sv.n)

            def wo_psum(tcX, eh):
                k = tcX * 2 + eh
                if k < 4:
                    return PSA[:, k * 512:(k + 1) * 512]
                if k < 6:
                    return PSB[:, (k - 4) * 512:(k - 3) * 512]
                return (PSC if k == 6 else PSD)[:, 0:512]

            def wo_load(cc, gate):
                wb = (cc % 2) * 1024
                sem = sWQa if cc % 2 == 0 else sWQb
                semg = sWGa if cc % 2 == 0 else sWGb
                wap0 = AP(din["woutT"].tensor, cc * 128 * DM, [[DM, 128], [1, 512]])
                wap1 = AP(din["woutT"].tensor, cc * 128 * DM + 512, [[DM, 128], [1, 512]])
                EM("sync", lambda e, s=wap0, wb=wb: e.dma_start(out=TBL[:, wb:wb + 512], in_=s),
                   wait=list(gate), inc=sem, inck=16)
                EM("gpsimd", lambda e, s=wap1, wb=wb: e.dma_start(
                    out=TBL[:, wb + 512:wb + 1024], in_=s),
                   wait=list(gate), inc=semg, inck=16)
                return [(sem, sem.n), (semg, semg.n)]

            wo_rdy = {0: wo_load(0, [FFW_RDY, (sp, sp.n), (sv, sv.n)]),
                      1: wo_load(1, [(sp, sp.n)])}
            pe_after = {}
            for cc in range(16):
                wb = (cc % 2) * 1024
                for tcX in range(4):
                    for eh in range(2):
                        if cc < 8:
                            lhs = B[:, cc * 2048 + tok0 + tcX * 128:
                                    cc * 2048 + tok0 + tcX * 128 + 128]
                        else:
                            lhs = EW[:, 2048 + (cc - 8) * 512 + tcX * 128:
                                     2048 + (cc - 8) * 512 + tcX * 128 + 128]
                        w = []
                        if tcX == 0 and eh == 0:
                            w += wo_rdy[cc]
                            if cc == 0:
                                w += [(sv, sv.n), (sc, sc.n)]
                        PE(lambda e, lhs=lhs, tcX=tcX, eh=eh, wb=wb, cc=cc: e.matmul(
                            wo_psum(tcX, eh), lhs, TBL[:, wb + eh * 512:wb + eh * 512 + 512],
                            start=(cc == 0), stop=(cc == 15)), wait=w)
                pe_after[cc] = (sp, sp.n)
                if cc + 2 <= 15:
                    wo_rdy[cc + 2] = wo_load(cc + 2, [pe_after[cc]])
            woc = (sp, sp.n)
            X4ACC = EW[:, 2048:6144]
            for tcX in range(4):
                for eh in range(2):
                    V(lambda e, tcX=tcX, eh=eh: e.tensor_copy(
                        out=X4ACC[:, tcX * 1024 + eh * 512:tcX * 1024 + eh * 512 + 512],
                        in_=wo_psum(tcX, eh)),
                      wait=[woc] if (tcX == 0 and eh == 0) else [])
            ffw_prev = (sv, sv.n)

            for tcX in range(4):
                for dc in range(8):
                    src = A[:, dc * 2048 + tok0 + tcX * 128:dc * 2048 + tok0 + tcX * 128 + 128]
                    PE(lambda e, src=src, dc=dc: e.transpose(
                        PSB[:, dc * 128:(dc + 1) * 128], src, CXID[:]),
                       wait=[(sv, sv.n)] if dc == 0 else [])
                pw = (sp, sp.n)
                X4B2 = EW[:, 1024:2048]
                X4C2 = WX[:, 0:1024]
                V(lambda e: e.tensor_copy(out=X4C2, in_=PSB[:, 0:1024]), wait=[pw])
                for eh in range(2):
                    V(lambda e, eh=eh, tcX=tcX: e.tensor_add(
                        X4B2[:, eh * 512:(eh + 1) * 512],
                        X4ACC[:, tcX * 1024 + eh * 512:tcX * 1024 + eh * 512 + 512],
                        X4C2[:, eh * 512:(eh + 1) * 512]))
                V(lambda e: e.reduce_sum(out=MM[:, 20:21], in_=X4B2, axis=AX.X))
                V(lambda e: e.tensor_scalar(out=MM[:, 20:21], in0=MM[:, 20:21],
                                            scalar1=1.0 / DM, scalar2=None, op0=ALU.mult))
                V(lambda e: e.tensor_scalar(out=X4C2, in0=X4B2, scalar1=MM[:, 20:21],
                                            scalar2=None, op0=ALU.subtract))
                vw = (sv, sv.n)
                SCE(lambda e: e.activation(out=X4B2, in_=X4C2, func=AF.Square), wait=[vw])
                cw_ = (sc, sc.n)
                V(lambda e: e.reduce_sum(out=MM[:, 21:22], in_=X4B2, axis=AX.X), wait=[cw_])
                V(lambda e: e.tensor_scalar(out=MM[:, 21:22], in0=MM[:, 21:22],
                                            scalar1=1.0 / DM, scalar2=EPS,
                                            op0=ALU.mult, op1=ALU.add))
                V(lambda e: e.reciprocal(out=MM[:, 21:22], in_=MM[:, 21:22]))
                vw = (sv, sv.n)
                SCE(lambda e: e.activation(out=MM[:, 22:23], in_=MM[:, 21:22], func=AF.Sqrt),
                    wait=[vw])
                cw_ = (sc, sc.n)
                X4S2 = TBL[:, 2048:3072]
                V(lambda e: e.tensor_scalar(out=X4S2, in0=X4C2, scalar1=MM[:, 22:23],
                                            scalar2=None, op0=ALU.mult),
                  wait=[cw_, (st_, n_out_dma * 16)])
                vw = (sv, sv.n)
                oap = AP(out_h.tensor, (tok0 + tcX * 128) * DM, [[DM, 128], [1, 1024]])
                EM("sync", lambda e, s=oap: e.dma_start(out=s, in_=X4S2), wait=[vw],
                   inc=st_, inck=16)
                n_out_dma += 1

        EM("gpsimd", lambda e: None, wait=[(st_, n_out_dma * 16)])
        barrier()


    try:
        _emit_all()
    except _StopEmission:
        pass

    @blk.sync
    def _(sync):
        for fn in prog.streams["sync"]:
            fn(sync)

    @blk.tensor
    def _(tensor):
        for fn in prog.streams["tensor"]:
            fn(tensor)

    @blk.vector
    def _(vector):
        for fn in prog.streams["vector"]:
            fn(vector)

    @blk.scalar
    def _(scalar):
        for fn in prog.streams["scalar"]:
            fn(scalar)

    @blk.gpsimd
    def _(gpsimd):
        for fn in prog.streams["gpsimd"]:
            fn(gpsimd)

    stack.close()
    build_nc.last_prog = prog
    build_nc.last_sems = dict(sd=sd, sg=sg, st=st_, sv=sv, sc=sc, sp=sp, sq=sq, ssy=ssy)
    return nc, sorted(dbg_h.keys())


_CACHE = {}


def kernel(**inputs):
    g = _prep_inputs(inputs)
    consts = _host_consts()
    key = ("nc", tuple(_DBG), _MAX_STAGE)
    if key not in _CACHE:
        _CACHE[key] = build_nc(dbg=tuple(_DBG))
    nc, dbg_names = _CACHE[key]
    in_maps = []
    for c in range(8):
        m = {"x0": np.ascontiguousarray(g["x0"][c])}
        for k in ("wqT", "wkT", "wvT", "woT", "rel", "w1T", "cw", "cb", "w2T",
                  "dec", "Wt", "b1r", "gamr", "betr", "winT", "woutT", "kfT", "vf"):
            m[k] = g[k]
        for k, v in consts.items():
            m[k] = v
        in_maps.append(m)
    res = run_bass_kernel_spmd(nc, in_maps, core_ids=list(range(8)))
    out = np.stack([np.asarray(res.results[c]["out"]) for c in range(8)], axis=0)
    if _DBG:
        kernel._dbg_res = res
    return out
